# version 17
# baseline (speedup 1.0000x reference)
import numpy as np
import ml_dtypes
from contextlib import ExitStack
import concourse.bass as bass
import concourse.mybir as mybir
from concourse.bass_utils import run_bass_kernel_spmd

F32 = mybir.dt.float32
BF16 = mybir.dt.bfloat16
FP8 = mybir.dt.float8e4
AF = mybir.ActivationFunctionType
ALU = mybir.AluOpType
AX = mybir.AxisListType

S, DM, H, HD = 4096, 1024, 16, 64
TILE, NSUB, C, B = 512, 4, 128, 32
NCH = TILE // C
NT = S // TILE
P3 = TILE + 3


def make_consts():
    c = {}
    ii = np.arange(128)
    bf = ml_dtypes.bfloat16
    c["IDENT128"] = np.eye(128, dtype=bf)
    id64 = np.zeros((128, 64), dtype=bf)
    id64[:64, :] = np.eye(64)
    id64[64:, :] = np.eye(64)
    c["ID64"] = id64
    c["MASK1"] = (ii[:, None] < ii[None, :]).astype(np.float32).astype(bf)
    c["IDT1"] = np.eye(128, dtype=np.float32).astype(bf)
    c["NEGBLK1"] = (-(ii[:, None] < (ii[None, :] // B) * B).astype(np.float32)).astype(bf)
    idd = np.zeros((128, B), dtype=np.float32)
    for i in range(4):
        idd[32 * i:32 * i + 32, :] = np.eye(32)
    c["IDD1"] = idd.astype(bf)
    ind2 = np.zeros((128, 8 * H), dtype=np.float32)
    for fb in range(8):
        ind2[0:64, fb * H + 2 * fb] = 1.0
        ind2[64:128, fb * H + 2 * fb + 1] = 1.0
    c["IND2"] = ind2.astype(bf)
    indt = np.zeros((H, 8 * 128), dtype=np.float32)
    for fb in range(8):
        indt[2 * fb, fb * 128:fb * 128 + 64] = 1.0
        indt[2 * fb + 1, fb * 128 + 64:fb * 128 + 128] = 1.0
    c["INDT"] = indt.astype(bf)
    i64 = np.zeros((H, H * HD), dtype=np.float32)
    for h in range(H):
        i64[h, h * HD:(h + 1) * HD] = 1.0
    c["IND64"] = i64.astype(bf)
    ibd = np.zeros((H, H * 128), np.float32)
    for h in range(H):
        ibd[h, h * 128:(h + 1) * 128] = 1.0
    c["INDBD"] = ibd.astype(bf)
    c["ONES1"] = np.ones((1, 128), dtype=bf)
    o33 = np.zeros((33, 128), np.float32); o33[0] = 1; o33[32] = 1
    c["ONES33"] = o33.astype(bf)
    c["ONES2"] = np.ones((2, 128), dtype=bf)
    c["ONES16"] = np.ones((16, 128), dtype=bf)
    c["ONESC"] = np.ones((128, 1), dtype=bf)
    c["ONESF"] = np.ones((1, 128), dtype=np.float32)
    c["ID16"] = np.eye(16, dtype=np.float32)
    indp = np.zeros((H, 128), np.float32)
    for h in range(H):
        indp[h, 64 * (h % 2):64 * (h % 2) + 64] = 1.0
    c["INDP"] = indp.astype(bf)
    p16 = np.zeros((H, 8), np.float32)
    for h in range(H):
        p16[h, h // 2] = 1.0
    c["PATTERN16"] = p16
    cv = np.zeros((128, 4), np.float32)
    cv[:, 0] = 1e-5; cv[:, 1] = -1.0
    cv[0:64, 2] = 1.0
    cv[64:128, 3] = 1.0
    c["CVALS"] = cv
    return c


CONSTS = make_consts()


def prep_common(inputs):
    f32 = np.float32
    cm = {}
    for k, v in CONSTS.items():
        cm["c_" + k] = v

    def rw(w):
        return np.ascontiguousarray(np.asarray(w, f32).reshape(8, 128, -1))
    cm["wq"] = rw(inputs["L0_attn_Wq"]); cm["wk"] = rw(inputs["L0_attn_Wk"])
    cm["wv"] = rw(inputs["L0_attn_Wv"]); cm["wg"] = rw(inputs["L0_attn_Wg"])
    cm["wo"] = rw(inputs["L0_attn_Wo"])
    cm["wa"] = rw(inputs["L0_attn_Wa"]); cm["wb"] = rw(inputs["L0_attn_Wb"])
    cm["wffn"] = np.ascontiguousarray(np.asarray(inputs["L0_ffn_weight"], f32).reshape(8, 128, DM))

    def col8(b):
        return np.ascontiguousarray(np.asarray(b, f32).reshape(8, 128).T)
    cm["bq"] = col8(inputs["L0_attn_bq"]); cm["bk"] = col8(inputs["L0_attn_bk"])
    cm["bv"] = col8(inputs["L0_attn_bv"]); cm["bg"] = col8(inputs["L0_attn_bg"])
    cm["gns"] = col8(inputs["L0_attn_gn_scale"]); cm["gnb"] = col8(inputs["L0_attn_gn_bias"])
    cm["bo"] = np.ascontiguousarray(np.asarray(inputs["L0_attn_bo"], f32).reshape(1, DM))
    cm["bffn"] = np.ascontiguousarray(np.asarray(inputs["L0_ffn_bias"], f32).reshape(1, DM))
    cm["ba"] = np.ascontiguousarray(np.asarray(inputs["L0_attn_ba"], f32).reshape(H, 1))
    cm["bb"] = np.ascontiguousarray(np.asarray(inputs["L0_attn_bb"], f32).reshape(H, 1))
    cw = np.zeros((128, 8 * 3 * 4), f32)
    for pr, key in enumerate(("L0_attn_q_conv", "L0_attn_k_conv", "L0_attn_v_conv")):
        w = np.asarray(inputs[key], f32).reshape(8, 128, 4)
        for fb in range(8):
            cw[:, (pr * 8 + fb) * 4:(pr * 8 + fb) * 4 + 4] = w[fb]
    cm["convw"] = cw
    return cm


class Prog:
    ENG2BLK = {"pe": "tensor", "dve": "vector", "act": "scalar",
               "gps": "gpsimd", "dma": "sync", "gdma": "gpsimd"}

    def __init__(self, nc, block, sems):
        self.nc, self.block, self.sems = nc, block, sems
        self.cnt = {k: 0 for k in self.ENG2BLK}
        self.res = {}
        self.pend = {k: [] for k in self.ENG2BLK}

    FLUSH_N = 1
    FLUSH_EVERY = True

    def flush(self):
        for eng, bodies in self.pend.items():
            n = self.FLUSH_N
            for i in range(0, len(bodies), n):
                def mega(e, bs=tuple(bodies[i:i + n])):
                    for b in bs:
                        b(e)
                getattr(self.block, self.ENG2BLK[eng])(mega)
            self.pend[eng] = []

    def _get(self, r):
        if r not in self.res:
            self.res[r] = {"w": None, "r": []}
        return self.res[r]

    stage_limit = 99
    stage_cur = 0

    def sec(self, eng, fn, rd=(), wr=(), extra=(), n_dma=0, no_inc=False):
        if self.stage_cur > self.stage_limit:
            return (eng, self.cnt[eng])
        deps = []
        for r in rd:
            st = self._get(r)
            if st["w"] is not None:
                deps.append(st["w"])
        for r in wr:
            st = self._get(r)
            if st["w"] is not None:
                deps.append(st["w"])
            deps.extend(st["r"])
        deps.extend([d for d in extra if d is not None])
        is_dma = eng in ("dma", "gdma")
        if is_dma and n_dma > 0:
            deps.append((eng, self.cnt[eng]))
        dd = {}
        for (de, dc) in deps:
            if dc > 0 and dc > dd.get(de, 0):
                dd[de] = dc
        deps = list(dd.items())
        for (de, dc) in deps:
            assert dc <= self.cnt[de], (eng, de, dc, self.cnt[de])
        sems = self.sems

        def body(e):
            for (de, dc) in deps:
                e.wait_ge(sems[de], dc)
            last = fn(e)
            if is_dma:
                if isinstance(last, (list, tuple)):
                    for ins in last:
                        ins.then_inc(sems[eng], 16)
                elif last is not None:
                    last.then_inc(sems[eng], 16)
            elif last is not None:
                last.then_inc(sems[eng], 1)
            else:
                assert no_inc, "compute section returned None without no_inc"
        self.pend[eng].append(body)
        if self.FLUSH_EVERY:
            self.flush()
        if is_dma:
            self.cnt[eng] += 16 * n_dma
        elif not no_inc:
            self.cnt[eng] += 1
        tk = (eng, self.cnt[eng])
        for r in rd:
            self._get(r)["r"].append(tk)
        for r in wr:
            st = self._get(r)
            st["w"] = tk
            st["r"] = []
        return tk

    def t(self, eng):
        return (eng, self.cnt[eng])


def build_program(n_tiles=NT, do_attn=True, do_phase2=True, debug=(), n_cores=8, chunk_upto=99):
    nc = bass.Bass("TRN2", target_bir_lowering=False, detect_race_conditions=False)

    hx = nc.dram_tensor("x", [S, DM], F32, kind="ExternalInput").ap()
    hw = {w: nc.dram_tensor(w, [8, 128, DM], F32, kind="ExternalInput").ap()
          for w in ("wq", "wk", "wv", "wg", "wo", "wffn")}
    hw["wa"] = nc.dram_tensor("wa", [8, 128, H], F32, kind="ExternalInput").ap()
    hw["wb"] = nc.dram_tensor("wb", [8, 128, H], F32, kind="ExternalInput").ap()
    hbn = ("bq", "bk", "bv", "bg", "gns", "gnb")
    hb = {b: nc.dram_tensor(b, [128, 8], F32, kind="ExternalInput").ap() for b in hbn}
    hb["bo"] = nc.dram_tensor("bo", [1, DM], F32, kind="ExternalInput").ap()
    hb["bffn"] = nc.dram_tensor("bffn", [1, DM], F32, kind="ExternalInput").ap()
    hb["ba"] = nc.dram_tensor("ba", [H, 1], F32, kind="ExternalInput").ap()
    hb["bb"] = nc.dram_tensor("bb", [H, 1], F32, kind="ExternalInput").ap()
    hconv = nc.dram_tensor("convw", [128, 8 * 3 * 4], F32, kind="ExternalInput").ap()
    hc = {}
    for k, v in CONSTS.items():
        dt = BF16 if v.dtype == ml_dtypes.bfloat16 else F32
        hc[k] = nc.dram_tensor("c_" + k, list(v.shape), dt, kind="ExternalInput").ap()

    hy = nc.dram_tensor("y", [S, DM], F32, kind="ExternalOutput").ap()
    hxnf = nc.dram_tensor("xnf_sc", [NT, 128, 8 * TILE], F32, kind="Internal").ap()
    houts = nc.dram_tensor("outs_sc", [S, DM], BF16, kind="Internal").ap()
    hwt = nc.dram_tensor("wt_sc", [8, 128, DM], F32, kind="Internal").ap()
    hrow = nc.dram_tensor("rowbounce", [4, 128], F32, kind="Internal").ap()
    hlhl = nc.dram_tensor("lhl_sc", [48, TILE], BF16, kind="Internal").ap()
    hwam = nc.dram_tensor("wam", [1, 1], F32, kind="Internal").ap()
    hccin = nc.dram_tensor("cc_in", [1, 1], F32, kind="Internal").ap()
    hccout = nc.dram_tensor("cc_out", [1, 1], F32, kind="Internal").ap()

    dbg = {}
    def mkdbg(name, shape, dt=F32):
        if name in debug:
            dbg[name] = nc.dram_tensor("dbg_" + name, list(shape), dt,
                                       kind="ExternalOutput").ap()
    mkdbg("xnf", [128, 8 * TILE])
    mkdbg("kfn", [128, 8 * TILE], BF16)
    mkdbg("qfn", [128, 8 * TILE], BF16)
    mkdbg("vf", [128, 8 * TILE], BF16)
    mkdbg("khf", [128, 8 * TILE], BF16)
    mkdbg("ltile", [H, TILE])
    mkdbg("ee", [128, H * C], BF16)
    mkdbg("ptraw", [128, H * C], BF16)
    mkdbg("gt", [128, H * C], BF16)
    mkdbg("tfin", [128, H * B], BF16)
    mkdbg("uu", [128, H * HD], BF16)
    mkdbg("outs0", [128, DM], BF16)
    mkdbg("ssf", [128, H * HD])
    mkdbg("gnmul", [128, 8])
    mkdbg("gnadd", [128, 8])
    mkdbg("aval", [1, 4])

    eso = ExitStack()
    so = lambda n, s, dt=F32: eso.enter_context(nc.sbuf_tensor(n, s, dt))

    sc = {}
    for k, v in CONSTS.items():
        dt = BF16 if v.dtype == ml_dtypes.bfloat16 else F32
        sc[k] = so("s_" + k, list(v.shape), dt)
    sbias = {k: so("sb_" + k, [128, 8]) for k in hbn}
    sbrow = so("sbrow", [33, DM])
    sbrow16 = so("sbrow16", [33, DM], BF16)
    sba = so("sb_ba", [H, 1])
    sbb = so("sb_bb", [H, 1])
    sconv = so("sb_convw", [128, 8 * 3 * 4])
    amaxc = so("amaxc", [128, 40])
    srow = so("srow", [4, 128])
    sval = so("sval", [4, 12])
    sxcol = so("sxcol", [128, 4])
    gnmul = so("gnmul", [128, 8])
    gnadd = so("gnadd", [128, 8])
    gnc = so("gnc", [32, 64])
    gnr = so("gnr", [16, 8])
    vbr = so("vbr", [16, 8])
    vbm = so("vbm", [16, 8])
    INDPF = so("indpf_t", [16, 128])
    identf = so("identf", [128, 128])

    pa = eso.enter_context(nc.psum_tensor("pa", [128, 1024], F32))
    pb = eso.enter_context(nc.psum_tensor("pb", [128, 1024], F32))
    pc = eso.enter_context(nc.psum_tensor("pc", [128, 512], F32))
    pstat = eso.enter_context(nc.psum_tensor("pstat", [128, 512], F32))
    pt8 = eso.enter_context(nc.psum_tensor("pt8", [128, 2048], BF16))

    sems = {k: eso.enter_context(nc.semaphore("sem_" + k))
            for k in ("pe", "dve", "act", "gps", "dma", "gdma")}
    nc.const_aps.register(F32, 1e-5, sc["CVALS"][:, 0:1])
    nc.const_aps.register(F32, -1.0, sc["CVALS"][:, 1:2])

    esi = ExitStack()
    si = lambda n, s, dt=F32: esi.enter_context(nc.sbuf_tensor(n, s, dt))
    swq = si("swq", [128, 8 * DM], BF16)
    swk = si("swk", [128, 8 * DM], BF16)
    swv = si("swv", [128, 8 * DM], BF16)
    swa = si("swa", [128, 8 * H], BF16)
    swb = si("swb", [128, 8 * H], BF16)
    sdiag = si("sdiag", [128, 64 * 128], BF16)
    vcv = si("vcv", [128, 512], BF16)
    xts = si("xts", [128, DM])
    lntmp = si("lntmp", [128, DM])
    xnfs = xts
    xnfb = si("xnfb", [128, 8 * TILE], BF16)
    lncol = si("lncol", [128, 8])
    pre = si("pre", [128, 8 * P3], BF16)
    halo = si("halo", [128, 3 * 8 * 3], BF16)
    qfn = si("qfn", [128, 8 * TILE], BF16)
    kfn = si("kfn", [128, 8 * TILE], BF16)
    khfc = si("khfc", [128, 8 * C], BF16)
    khfcE = si("khfcE", [128, 8 * C], BF16)
    khfcO = si("khfcO", [128, 8 * C], BF16)
    vf = si("vf", [128, 8 * TILE], BF16)
    ksq = si("ksq", [128, TILE], BF16)
    nsqp = si("nsqp", [48, TILE])
    nsqq = nsqp[0:16, :]; nsqk = nsqp[32:48, :]
    negl = nsqp[0:16, :]
    rnq = si("rnq", [16, TILE], BF16)
    rnk = si("rnk", [16, TILE], BF16)
    lap = si("lap", [112, TILE])
    ltile = lap[0:16, :]
    atile = lap[96:112, :]
    abti = si("abti", [16, C], BF16)
    btile = si("btile", [16, TILE], BF16)
    lhl = si("lhl", [48, TILE], BF16)
    lhi = lhl[0:16, :]; llo = lhl[32:48, :]
    nlhi = si("nlhi", [16, TILE], BF16)
    nllo = si("nllo", [16, TILE], BF16)
    lfl2 = si("lfl2", [2, 2 * H * C], BF16)
    eet = si("eet", [128, H * C], BF16)
    uu = eet[:, 0:H * HD]
    eegh = si("eegh", [128, 1024], BF16)
    rcc = si("rcc", [128, 16], BF16)
    ptraw = si("ptraw", [128, H * C], BF16)
    ptod = ptraw
    gtt = si("gtt", [128, H * C], BF16)
    dd = None
    ddt = si("ddt", [128, H * B], BF16)
    dp = {1: si("dpow1", [128, H * B], BF16), 2: si("dpow2", [128, H * B], BF16),
          16: si("dpow16", [128, H * B], BF16)}
    dd = dp[16]
    dtp = {1: ddt, 2: si("dt2", [128, H * B], BF16),
           4: si("dt4", [128, H * B], BF16), 8: si("dt8", [128, H * B], BF16)}
    ggl = si("ggl", [128, H * B], BF16)
    ggr = si("ggr", [128, H * B], BF16)
    txx = dp[16]
    tfin = ggr

    kqs0 = si("kqs0", [128, H * HD], BF16)
    vtm = si("vtm", [128, H * HD], BF16)
    khtm = si("khtm", [128, H * HD], BF16)
    rhs_ = si("rhs_", [128, H * HD], BF16)
    preb = kqs0

    upb = vtm
    ssf = si("ssf", [128, H * HD])
    ssbE = si("ssbE", [128, H * HD], BF16)
    ssbO = si("ssbO", [128, H * HD], BF16)
    sst = lntmp
    outst = si("outst", [128, DM], BF16)
    acdg = si("acdg", [16, 16], BF16)
    wtf = lntmp

    def head_rb(h):
        return 64 * (h % 2)

    def fchunk(t, h, cc, w=C):
        fb = h // 2
        rb = head_rb(h)
        return t[rb:rb + 64, fb * TILE + cc * C: fb * TILE + cc * C + w]

    with eso, esi:
        with nc.Block() as block:
            p = Prog(nc, block, sems)

            def ld_consts(e):
                ins = []
                for k in sc:
                    t_ = sc[k]
                    sl = tuple(slice(0, d) for d in t_.shape)
                    ins.append(e.dma_start(out=t_[sl], in_=hc[k][sl]))
                for k in sbias:
                    ins.append(e.dma_start(out=sbias[k][:, :], in_=hb[k][:, :]))
                ins.append(e.dma_start(out=sbrow[0:1, :], in_=hb["bo"][:, :]))
                ins.append(e.dma_start(out=sbrow[32:33, :], in_=hb["bffn"][:, :]))
                ins.append(e.dma_start(out=sba[:, :], in_=hb["ba"][:, :]))
                ins.append(e.dma_start(out=sbb[:, :], in_=hb["bb"][:, :]))
                ins.append(e.dma_start(out=sconv[:, :], in_=hconv[:, :]))
                return ins
            p.sec("dma", ld_consts, wr=["consts"], n_dma=len(sc) + len(sbias) + 5)

            def ld_w(e):
                ins = []
                for w, dst in (("wq", swq), ("wk", swk), ("wv", swv)):
                    for cc in range(8):
                        ins.append(e.dma_start(out=dst[:, cc * DM:(cc + 1) * DM],
                                               in_=hw[w][cc, :, :]))
                for w, dst in (("wa", swa), ("wb", swb)):
                    for cc in range(8):
                        ins.append(e.dma_start(out=dst[:, cc * H:(cc + 1) * H],
                                               in_=hw[w][cc, :, :]))
                return ins
            p.sec("gdma", ld_w, wr=["w1"], n_dma=3 * 8 + 2 * 8)

            def mk_init(e):
                e.memset(eet[:, :], 0.0)
                e.memset(ssf[:, :], 0.0)
                e.memset(ssbE[:, :], 0.0)
                e.memset(ssbO[:, :], 0.0)
                e.memset(halo[:, :], 0.0)
                e.memset(amaxc[:, :], 0.0)
                e.memset(sval[:, :], 0.0)
                e.memset(gnc[:, :], 0.0)
                e.memset(srow[:, :], 0.0)
                return e.tensor_copy(out=identf[:, :], in_=sc["IDENT128"][:, :])
            p.sec("dve", mk_init, rd=["consts"],
                  wr=["uu", "ssf", "ssb", "halo", "amaxc", "sval", "gnc", "srow", "identf"])

            def mk_diag(e):
                last = None
                for idx in range(64):
                    last = e.tensor_scalar(
                        out=sdiag[:, idx * 128:(idx + 1) * 128],
                        in0=sc["IDENT128"][:, :],
                        scalar1=sconv[:, idx:idx + 1],
                        scalar2=None, op0=ALU.mult)
                return last
            p.sec("gps", mk_diag, rd=["consts"], wr=["sdiag"])

            for cc in range(8):
                wbuf, wkey = (wtf, "lntmp") if cc % 2 == 0 else (xnfs, "xts")
                pw_, pwk = (pa, "pa") if cc % 2 == 0 else (pb, "pb")

                def ld_wf(e, cc=cc, wbuf=wbuf):
                    ins = []
                    for oc in range(8):
                        ins.append(e.dma_start(
                            out=wbuf[:, oc * 128:(oc + 1) * 128],
                            in_=hw["wffn"][oc, :, cc * 128:(cc + 1) * 128]))
                    return ins
                p.sec("dma", ld_wf, wr=[wkey], n_dma=8)

                def tp_wf(e, wbuf=wbuf, pw_=pw_):
                    last = None
                    for oc in range(8):
                        last = e.transpose(pw_[:, oc * 128:(oc + 1) * 128],
                                           wbuf[:, oc * 128:(oc + 1) * 128],
                                           identf[:, :])
                    return last
                p.sec("pe", tp_wf, rd=[wkey, "identf"], wr=[pwk])

                def ev_wf1(e, cc=cc, wbuf=wbuf, pw_=pw_):
                    return e.tensor_copy(out=wbuf[:, :], in_=pw_[:, :])
                p.sec("dve", ev_wf1, rd=[pwk], wr=[wkey])

                def ev_wf2(e, cc=cc, wbuf=wbuf):
                    return e.reduce_max(out=amaxc[:, 34 + (cc % 2) * 4:35 + (cc % 2) * 4],
                                        in_=wbuf[:, :], axis=AX.X,
                                        apply_absolute_value=True)
                p.sec("dve", ev_wf2, rd=[wkey], wr=["amaxw_t%d" % (cc % 2)])

                def ev_wf3(e, cc=cc):
                    return e.tensor_tensor(out=amaxc[:, 35:36], in0=amaxc[:, 35:36],
                                           in1=amaxc[:, 34 + (cc % 2) * 4:35 + (cc % 2) * 4],
                                           op=ALU.max)
                p.sec("dve", ev_wf3, rd=["amaxw_t%d" % (cc % 2), "amaxc"], wr=["amaxc"])

                def st_wf(e, cc=cc, wbuf=wbuf):
                    return e.dma_start(out=hwt[cc, :, :], in_=wbuf[:, :])
                p.sec("dma", st_wf, rd=[wkey], wr=["hwt_%d" % cc], n_dma=1)

            stats_on = [False]

            for T in range(n_tiles):
                for s_ in range(NSUB):
                    def ld_x(e, T=T, s_=s_):
                        return e.dma_start(
                            out=xts[:, :],
                            in_=hx[T * TILE + s_ * 128: T * TILE + (s_ + 1) * 128, :])
                    p.sec("dma", ld_x, wr=["xts"], n_dma=1)

                    def ln_a1(e):
                        return e.reduce_sum(out=lncol[:, 0:1], in_=xts[:, :], axis=AX.X)
                    p.sec("dve", ln_a1, rd=["xts"], wr=["lncol_a"])

                    def ln_a2(e):
                        return e.tensor_scalar(out=lncol[:, 1:2], in0=lncol[:, 0:1],
                                               scalar1=1.0 / DM, scalar2=None, op0=ALU.mult)
                    p.sec("dve", ln_a2, rd=["lncol_a"], wr=["lncol"])

                    def ln_b(e):
                        return e.activation(out=lntmp[:, :], in_=xts[:, :], func=AF.Square,
                                            accum_out=lncol[:, 2:3])
                    p.sec("act", ln_b, rd=["xts"], wr=["lntmp", "lncol2"])

                    def ln_c1(e):
                        return e.tensor_scalar(out=lncol[:, 3:4], in0=lncol[:, 1:2],
                                               scalar1=lncol[:, 1:2], scalar2=None, op0=ALU.mult)
                    p.sec("dve", ln_c1, rd=["lncol"], wr=["lncol_c"])

                    def ln_c2(e):
                        return e.tensor_scalar(out=lncol[:, 4:5], in0=lncol[:, 2:3],
                                               scalar1=1.0 / DM, scalar2=lncol[:, 3:4],
                                               op0=ALU.mult, op1=ALU.subtract)
                    p.sec("dve", ln_c2, rd=["lncol_c", "lncol2"], wr=["lncol"])

                    def ln_d(e):
                        return e.activation(out=lncol[:, 5:6], in_=lncol[:, 4:5],
                                            func=AF.Sqrt, bias=1e-5)
                    p.sec("act", ln_d, rd=["lncol"], wr=["lncol3"])

                    def ln_e1(e):
                        return e.reciprocal(out=lncol[:, 6:7], in_=lncol[:, 5:6])
                    p.sec("dve", ln_e1, rd=["lncol3"], wr=["lncol_e"])

                    def ln_e2(e):
                        return e.tensor_scalar(out=lntmp[:, :], in0=xts[:, :],
                                               scalar1=lncol[:, 1:2], scalar2=lncol[:, 6:7],
                                               op0=ALU.subtract, op1=ALU.mult)
                    p.sec("dve", ln_e2, rd=["xts", "lncol", "lncol_e"], wr=["lntmp"])

                    def ln_e3(e, T=T, s_=s_):
                        return e.reduce_max(out=amaxc[:, T * NSUB + s_:T * NSUB + s_ + 1],
                                            in_=lntmp[:, :], axis=AX.X,
                                            apply_absolute_value=True)
                    p.sec("dve", ln_e3, rd=["lntmp"], wr=["amaxc"])

                    psx, pkx = (pa, "pa") if s_ % 2 == 0 else (pb, "pb")

                    def tp_x(e, psx=psx):
                        last = None
                        for fb in range(8):
                            last = e.transpose(psx[:, fb * 128:(fb + 1) * 128],
                                               lntmp[:, fb * 128:(fb + 1) * 128],
                                               identf[:, :])
                        return last
                    p.sec("pe", tp_x, rd=["lntmp", "identf"], wr=[pkx])

                    def ev_f(e, psx=psx):
                        return e.tensor_copy(out=xnfs[:, :], in_=psx[:, :])
                    p.sec("dve", ev_f, rd=[pkx], wr=["xts"])

                    def ev_b(e, s_=s_, psx=psx):
                        ap = xnfb[:, :].rearrange("p (fb t) -> p fb t", fb=8)[:, :, s_ * 128:(s_ + 1) * 128]
                        return e.tensor_copy(out=ap,
                                             in_=psx[:, :].rearrange("p (fb t) -> p fb t", fb=8))
                    p.sec("dve", ev_b, rd=[pkx], wr=["xnfb_%d" % s_])

                    def st_xnf(e, T=T, s_=s_):
                        ap = hxnf[T, :, :].rearrange("p (fb t) -> p fb t", fb=8)[:, :, s_ * 128:(s_ + 1) * 128]
                        return e.dma_start(out=ap, in_=xnfs[:, :].rearrange("p (fb t) -> p fb t", fb=8))
                    p.sec("dma", st_xnf, rd=["xts"], wr=["hxnf_%d_%d" % (T, s_)], n_dma=1)

                xnfb_r = ["xnfb_%d" % s_ for s_ in range(NSUB)]
                if "xnf" in dbg and T == 0:
                    p.sec("dma", lambda e: e.dma_start(out=dbg["xnf"][:, :], in_=hxnf[0, :, :]),
                          rd=["hxnf_0_0", "hxnf_0_1", "hxnf_0_2", "hxnf_0_3"], n_dma=1)

                for pi, (pname, wten, bkey, dstF) in enumerate(
                        (("q", swq, "bq", qfn), ("k", swk, "bk", kfn), ("v", swv, "bv", vf))):
                    for fbp in range(4):
                        ps_, pk_ = (pa, "pa") if fbp % 2 == 0 else (pb, "pb")

                        def mm_p(e, wten=wten, fbp=fbp, ps_=ps_):
                            last = None
                            for half in range(2):
                                fb = fbp * 2 + half
                                for cc in range(8):
                                    last = e.matmul(
                                        ps_[:, half * 512:(half + 1) * 512],
                                        wten[:, cc * DM + fb * 128: cc * DM + (fb + 1) * 128],
                                        xnfb[:, cc * 512:(cc + 1) * 512],
                                        start=(cc == 0), stop=(cc == 7))
                            return last
                        p.sec("pe", mm_p, rd=xnfb_r + ["w1"], wr=[pk_])

                        def ev_p(e, bkey=bkey, fbp=fbp, ps_=ps_):
                            last = None
                            for half in range(2):
                                fb = fbp * 2 + half
                                last = e.tensor_scalar(
                                    out=pre[:, fb * P3 + 3: fb * P3 + 3 + TILE],
                                    in0=ps_[:, half * 512:(half + 1) * 512],
                                    scalar1=sbias[bkey][:, fb:fb + 1], scalar2=None,
                                    op0=ALU.add)
                            return last
                        p.sec("dve", ev_p, rd=[pk_, "consts"], wr=["pre_%d" % fbp])

                    def h_in(e, pi=pi):
                        ap = pre[:, :].rearrange("p (fb t) -> p fb t", fb=8)[:, :, 0:3]
                        hap = halo[:, :].rearrange("p (pr fbt) -> p pr fbt", pr=3)[:, pi]\
                            .rearrange("p (fb t) -> p fb t", fb=8)
                        return e.tensor_copy(out=ap, in_=hap)
                    p.sec("dve", h_in, rd=["halo"], wr=["prehalo"],
                          extra=[p.res["pre_%d" % f]["w"] for f in range(4)])

                    if pi < 2:
                        for fbp in range(4):
                            ps_, pk_ = (pb, "pb") if fbp % 2 == 0 else (pa, "pa")

                            def mm_c(e, pi=pi, fbp=fbp, ps_=ps_):
                                last = None
                                for half in range(2):
                                    fb = fbp * 2 + half
                                    for tap in range(4):
                                        idx = (pi * 8 + fb) * 4 + tap
                                        last = e.matmul(
                                            ps_[:, half * 512:(half + 1) * 512],
                                            sdiag[:, idx * 128:(idx + 1) * 128],
                                            pre[:, fb * P3 + tap: fb * P3 + tap + TILE],
                                            start=(tap == 0), stop=(tap == 3))
                                return last
                            p.sec("pe", mm_c, rd=["prehalo", "sdiag", "pre_%d" % fbp], wr=[pk_])

                            def ev_c(e, dstF=dstF, fbp=fbp, ps_=ps_):
                                last = None
                                for half in range(2):
                                    fb = fbp * 2 + half
                                    last = e.activation(out=dstF[:, fb * 512:(fb + 1) * 512],
                                                        in_=ps_[:, half * 512:(half + 1) * 512],
                                                        func=AF.Silu)
                                return last
                            p.sec("act", ev_c, rd=[pk_], wr=["F%d_%d" % (pi, fbp)])
                    else:
                        for fb in range(8):
                            def cv_v0(e, fb=fb):
                                base = (2 * 8 + fb) * 4
                                return e.tensor_scalar(
                                    out=vcv[:, :],
                                    in0=pre[:, fb * P3 + 0: fb * P3 + TILE],
                                    scalar1=sconv[:, base:base + 1],
                                    scalar2=None, op0=ALU.mult)
                            p.sec("dve", cv_v0, rd=["prehalo", "consts", "pre_%d" % (fb // 2)],
                                  wr=["vcv"])
                            for tap in range(1, 4):
                                def cv_vt(e, fb=fb, tap=tap):
                                    base = (2 * 8 + fb) * 4
                                    return e.scalar_tensor_tensor(
                                        out=vcv[:, :],
                                        in0=pre[:, fb * P3 + tap: fb * P3 + tap + TILE],
                                        scalar=sconv[:, base + tap:base + tap + 1],
                                        in1=vcv[:, :], op0=ALU.mult, op1=ALU.add)
                                p.sec("dve", cv_vt,
                                      rd=["prehalo", "consts", "pre_%d" % (fb // 2), "vcv"],
                                      wr=["vcv"])

                            def sl_v(e, dstF=dstF, fb=fb):
                                return e.activation(out=dstF[:, fb * 512:(fb + 1) * 512],
                                                    in_=vcv[:, :], func=AF.Silu)
                            p.sec("act", sl_v, rd=["vcv"], wr=["F%d_%d" % (pi, fb // 2)])

                    def h_out(e, pi=pi):
                        ap = pre[:, :].rearrange("p (fb t) -> p fb t", fb=8)[:, :, TILE:TILE + 3]
                        hap = halo[:, :].rearrange("p (pr fbt) -> p pr fbt", pr=3)[:, pi]\
                            .rearrange("p (fb t) -> p fb t", fb=8)
                        return e.tensor_copy(out=hap, in_=ap)
                    p.sec("dve", h_out, rd=["pre_0", "pre_1", "pre_2", "pre_3"], wr=["halo"])

                    if pi < 2:
                        nsqt = nsqq if pi == 0 else nsqk
                        for fb in range(8):
                            def sq_f(e, dstF=dstF, fb=fb):
                                return e.tensor_mul(ksq[:, :], dstF[:, fb * 512:(fb + 1) * 512],
                                                    dstF[:, fb * 512:(fb + 1) * 512])
                            p.sec("gps", sq_f, rd=["F%d_%d" % (pi, fb // 2)], wr=["ksq"])

                            def mm_n(e, fb=fb):
                                return e.matmul(pc[0:16, 0:512], sc["IND2"][:, fb * H:(fb + 1) * H],
                                                ksq[:, :], start=(fb == 0), stop=(fb == 7))
                            p.sec("pe", mm_n, rd=["ksq", "consts"],
                                  wr=["pc"] if fb == 7 else [])

                        def ev_n(e, nsqt=nsqt):
                            return e.activation(out=nsqt[:, :], in_=pc[0:16, 0:512], func=AF.Sqrt)
                        p.sec("act", ev_n, rd=["pc"], wr=["nsq_%d" % pi])

                def rec_n1(e):
                    e.tensor_scalar(out=nsqq[:, :], in0=nsqq[:, :], scalar1=1e-6,
                                    scalar2=None, op0=ALU.add)
                    return e.tensor_scalar(out=nsqk[:, :], in0=nsqk[:, :], scalar1=1e-6,
                                           scalar2=None, op0=ALU.add)
                p.sec("dve", rec_n1, rd=["nsq_0", "nsq_1"], wr=["rn_a"])

                def rec_n2(e):
                    e.reciprocal(out=nsqq[:, :], in_=nsqq[:, :])
                    return e.reciprocal(out=nsqk[:, :], in_=nsqk[:, :])
                p.sec("dve", rec_n2, rd=["rn_a"], wr=["rn_b"])

                def rec_n3(e):
                    e.tensor_copy(out=rnq[:, :], in_=nsqq[:, :])
                    return e.tensor_copy(out=rnk[:, :], in_=nsqk[:, :])
                p.sec("dve", rec_n3, rd=["rn_b"], wr=["rn"])

                for ji, (srcT, dstT, vec) in enumerate(((qfn, qfn, rnq), (kfn, kfn, rnk))):
                    for fbp in range(4):
                        def mm_r(e, vec=vec, fbp=fbp):
                            last = None
                            for half in range(2):
                                fb = fbp * 2 + half
                                last = e.matmul(pa[:, half * 512:(half + 1) * 512],
                                                sc["INDT"][:, fb * 128:(fb + 1) * 128],
                                                vec[:, :], start=True, stop=True)
                            return last
                        p.sec("pe", mm_r, rd=["rn", "consts"], wr=["pa"])

                        def ap_r(e, srcT=srcT, dstT=dstT, fbp=fbp):
                            last = None
                            for half in range(2):
                                fb = fbp * 2 + half
                                last = e.tensor_tensor(out=dstT[:, fb * 512:(fb + 1) * 512],
                                                       in0=srcT[:, fb * 512:(fb + 1) * 512],
                                                       in1=pa[:, half * 512:(half + 1) * 512],
                                                       op=ALU.mult)
                            return last
                        wkey = ("Fq_n" if ji == 0 else "Fk_n") + "_%d" % fbp
                        rkey = ["F0_%d" % fbp] if ji == 0 else ["F1_%d" % fbp]
                        p.sec("dve", ap_r, rd=["pa"] + rkey, wr=[wkey])
                qfn_r = ["Fq_n_%d" % f for f in range(4)]
                kfn_r = ["Fk_n_%d" % f for f in range(4)]
                khf_r = ["khfc"]
                vf_r = ["F2_%d" % f for f in range(4)]

                def mm_ab(e):
                    last = None
                    for cc in range(8):
                        e.matmul(pb[0:16, 0:512], swa[:, cc * H:(cc + 1) * H],
                                 xnfb[:, cc * 512:(cc + 1) * 512],
                                 start=(cc == 0), stop=(cc == 7))
                    for cc in range(8):
                        last = e.matmul(pb[0:16, 512:1024], swb[:, cc * H:(cc + 1) * H],
                                        xnfb[:, cc * 512:(cc + 1) * 512],
                                        start=(cc == 0), stop=(cc == 7))
                    return last
                p.sec("pe", mm_ab, rd=xnfb_r + ["w1"], wr=["pb"])

                def ev_ab(e):
                    e.activation(out=atile[:, :], in_=pb[0:16, 0:512], func=AF.Sigmoid,
                                 bias=sba[:, 0:1])
                    return e.activation(out=btile[:, :], in_=pb[0:16, 512:1024],
                                        func=AF.Sigmoid, bias=sbb[:, 0:1])
                p.sec("act", ev_ab, rd=["pb", "consts"], wr=["atile", "btile"])

                def ev_ab2(e):
                    return e.activation(out=ltile[:, :], in_=atile[:, :], func=AF.Ln)
                p.sec("act", ev_ab2, rd=["atile"], wr=["ltile"])

                def cum_l1(e):
                    return e.tensor_tensor_scan(out=ltile[:, :], data0=ltile[:, :],
                                                data1=ltile[:, :], initial=0.0,
                                                op0=ALU.add, op1=ALU.min)
                p.sec("dve", cum_l1, rd=["ltile", "rn"], wr=["ltile"])

                def cum_l2(e):
                    e.tensor_scalar(out=negl[:, :], in0=ltile[:, :], scalar1=-1.0,
                                    scalar2=None, op0=ALU.mult)
                    return e.tensor_copy(out=lhi[:, :], in_=ltile[:, :])
                p.sec("dve", cum_l2, rd=["ltile"], wr=["negl", "lhi"])

                def cum_l3(e):
                    e.tensor_scalar(out=nlhi[:, :], in0=lhi[:, :], scalar1=-1.0,
                                    scalar2=None, op0=ALU.mult)
                    return e.tensor_tensor(out=sst[0:16, 0:TILE], in0=ltile[:, :],
                                           in1=lhi[:, :], op=ALU.subtract)
                p.sec("dve", cum_l3, rd=["ltile", "lhi"], wr=["nlhi_r", "lntmp"])

                def cum_l4(e):
                    return e.tensor_copy(out=llo[:, :], in_=sst[0:16, 0:TILE])
                p.sec("dve", cum_l4, rd=["lntmp"], wr=["llo_r"])

                def cum_l5(e):
                    return e.tensor_scalar(out=nllo[:, :], in0=llo[:, :], scalar1=-1.0,
                                           scalar2=None, op0=ALU.mult)
                p.sec("dve", cum_l5, rd=["llo_r"], wr=["lhl"],
                      extra=[p.res["nlhi_r"]["w"]])
                if "ltile" in dbg and T == 0:
                    p.sec("dma", lambda e: e.dma_start(out=dbg["ltile"][:, :], in_=ltile[:, :]),
                          rd=["ltile"], n_dma=1)

                def st_lhl(e):
                    a_ = e.dma_start(out=hlhl[0:16, :], in_=lhl[0:16, :])
                    b_ = e.dma_start(out=hlhl[32:48, :], in_=lhl[32:48, :])
                    return [a_, b_]
                p.sec("dma", st_lhl, rd=["lhl", "llo_r", "nlhi_r", "lhi"], wr=["hlhl"], n_dma=2)


                for nm, t_, rr in (("qfn", qfn, qfn_r), ("kfn", kfn, kfn_r),
                                   ("vf", vf, vf_r)):
                    if nm in dbg and T == 0:
                        p.sec("dma", (lambda t_, nm: lambda e: e.dma_start(out=dbg[nm][:, :], in_=t_[:, :]))(t_, nm),
                              rd=rr, n_dma=1)

                if not do_attn:
                    continue
                for cc in range(NCH):
                    p.stage_limit = chunk_upto
                    first_chunk = (T == 0 and cc == 0)
                    qr = qfn_r + kfn_r + khf_r + vf_r + ["ltile", "negl", "lflat"]

                    def mk_a(e, cc=cc):
                        if cc > 0:
                            return e.activation(out=atile[:, 0:C],
                                                in_=ltile[:, cc * C:(cc + 1) * C],
                                                func=AF.Exp, bias=negl[:, cc * C - 1:cc * C])
                        return e.activation(out=atile[:, 0:C], in_=ltile[:, 0:C], func=AF.Exp)
                    p.sec("act", mk_a, rd=["ltile", "negl"], wr=["achk_a"])

                    def mk_a2(e):
                        return e.tensor_copy(out=abti[:, :], in_=atile[:, 0:C])
                    p.sec("dve", mk_a2, rd=["achk_a"], wr=["achk"])

                    p.stage_cur = 1
                    if cc % 2 == 0:
                        def ld_lfl(e, cc=cc):
                            h4 = hlhl.rearrange("(o h) (c t) -> o h c t", o=3, c=NCH)
                            a_ = e.dma_start(
                                out=lfl2[0:1, :].rearrange("o (c h t) -> o c h t", c=2, h=16),
                                in_=h4[0:1, 0:16, cc:cc + 2, :].rearrange("o h c t -> o c h t"),
                                allow_slow_non_contiguous=True)
                            b_ = e.dma_start(
                                out=lfl2[1:2, :].rearrange("o (c h t) -> o c h t", c=2, h=16),
                                in_=h4[2:3, 0:16, cc:cc + 2, :].rearrange("o h c t -> o c h t"),
                                allow_slow_non_contiguous=True)
                            return [a_, b_]
                        p.sec("gdma", ld_lfl, rd=["hlhl"], wr=["lflat"], n_dma=2)
                    def mm_bhrep(e, cc=cc):
                        last = None
                        for fb in range(8):
                            last = e.matmul(pb[:, fb * 128:(fb + 1) * 128],
                                            sc["INDT"][:, fb * 128:(fb + 1) * 128],
                                            btile[:, cc * C:(cc + 1) * C],
                                            start=True, stop=True)
                        return last
                    p.sec("pe", mm_bhrep, rd=["btile", "consts"] + kfn_r, wr=["pb"])

                    def mk_khfc(e, cc=cc):
                        return e.tensor_tensor(
                            out=khfc[:, :],
                            in0=kfn[:, :].rearrange("p (fb t) -> p fb t", fb=8)[:, :, cc * C:(cc + 1) * C],
                            in1=pb[:, :].rearrange("p (fb t) -> p fb t", fb=8),
                            op=ALU.mult)
                    p.sec("dve", mk_khfc, rd=["pb"] + kfn_r, wr=["khfc"])

                    def mk_khfceo(e):
                        e.tensor_scalar(out=khfcE[:, :], in0=khfc[:, :],
                                        scalar1=sc["CVALS"][:, 2:3], scalar2=None,
                                        op0=ALU.mult)
                        return e.tensor_scalar(out=khfcO[:, :], in0=khfc[:, :],
                                               scalar1=sc["CVALS"][:, 3:4], scalar2=None,
                                               op0=ALU.mult)
                    p.sec("gps", mk_khfceo, rd=["khfc", "consts"], wr=["khfc2"])

                    def khsl(h, w=C):
                        rb = head_rb(h)
                        return khfc[rb:rb + 64, (h // 2) * 128:(h // 2) * 128 + w]

                    p.stage_cur = 2
                    def mm_ks0(e, cc=cc):
                        last = None
                        for h in range(H):
                            fb = h // 2
                            ssx = ssbE if h % 2 == 0 else ssbO
                            last = e.matmul(pa[:, h * HD:(h + 1) * HD],
                                            kfn[:, fb * TILE + cc * C: fb * TILE + cc * C + C],
                                            ssx[:, h * HD:(h + 1) * HD],
                                            start=True, stop=True)
                        return last
                    p.sec("pe", mm_ks0, rd=kfn_r + ["ssb"], wr=["pa"])

                    def ev_ks0(e):
                        return e.tensor_copy(out=kqs0[:, :], in_=pa[:, :])
                    p.sec("dve", ev_ks0, rd=["pa"], wr=["kqs0"])

                    def mm_qs0(e, cc=cc):
                        last = None
                        for h in range(H):
                            fb = h // 2
                            ssx = ssbE if h % 2 == 0 else ssbO
                            last = e.matmul(pb[:, h * HD:(h + 1) * HD],
                                            qfn[:, fb * TILE + cc * C: fb * TILE + cc * C + C],
                                            ssx[:, h * HD:(h + 1) * HD],
                                            start=True, stop=True)
                        return last
                    p.sec("pe", mm_qs0, rd=qfn_r + ["ssb"], wr=["pb"])

                    p.stage_cur = 3
                    def mm_arep2(e):
                        e.matmul(pa[:, 0:512], abti[:, :], sc["IND64"][:, 0:512],
                                 start=True, stop=True)
                        return e.matmul(pa[:, 512:1024], abti[:, :], sc["IND64"][:, 512:1024],
                                        start=True, stop=True)
                    p.sec("pe", mm_arep2, rd=["achk", "consts", "kqs0"], wr=["pa"])

                    p.stage_cur = 4
                    def tp_vk(e, cc=cc):
                        last = None
                        for fb in range(8):
                            e.transpose(pt8[:, fb * 128:(fb + 1) * 128],
                                        vf[:, fb * TILE + cc * C: fb * TILE + cc * C + C],
                                        sc["IDENT128"][:, :])
                            last = e.transpose(pt8[:, 1024 + fb * 128: 1024 + (fb + 1) * 128],
                                               khfc[:, fb * 128:(fb + 1) * 128],
                                               sc["IDENT128"][:, :])
                        return last
                    p.sec("pe", tp_vk, rd=vf_r + khf_r + ["consts"], wr=["pt8"])

                    def ev_vk(e):
                        e.tensor_copy(out=vtm[:, :], in_=pt8[:, 0:1024])
                        return e.tensor_copy(out=khtm[:, :], in_=pt8[:, 1024:2048])
                    p.sec("dve", ev_vk, rd=["pt8"], wr=["vtm", "khtm"])

                    p.stage_cur = 5
                    def mk_rhs1(e):
                        return e.tensor_tensor(out=sst[:, :], in0=pa[:, :], in1=kqs0[:, :],
                                               op=ALU.mult)
                    p.sec("dve", mk_rhs1, rd=["pa", "kqs0"], wr=["lntmp"])

                    def mk_rhs2(e):
                        return e.tensor_tensor(out=rhs_[:, :], in0=vtm[:, :], in1=sst[:, :],
                                               op=ALU.subtract)
                    p.sec("dve", mk_rhs2, rd=["vtm", "lntmp"], wr=["rhs"])

                    def mk_oa1(e):
                        return e.tensor_copy(out=kqs0[:, :], in_=pb[:, :])
                    p.sec("dve", mk_oa1, rd=["pb"], wr=["kqs0"])

                    def mk_oa2(e):
                        return e.tensor_tensor(out=outst[:, :], in0=pa[:, :], in1=kqs0[:, :],
                                               op=ALU.mult)
                    p.sec("dve", mk_oa2, rd=["pa", "kqs0"], wr=["oa", "outst"])

                    p.stage_cur = 6
                    def mm_ld(e, cc=cc):
                        last = None
                        for gh, ps in ((0, pb), (1, pa)):
                            for q in range(2):
                                off = (cc % 2) * 2048 + gh * 1024 + q * 512
                                tgt = ps[:, q * 512:(q + 1) * 512]
                                e.matmul(tgt, sc["ONES2"][:, :],
                                         lfl2[0:2, off:off + 512], start=True, stop=False)
                                e.matmul(tgt, nlhi[:, cc * C:(cc + 1) * C],
                                         sc["INDBD"][:, gh * 1024 + q * 512: gh * 1024 + (q + 1) * 512],
                                         start=False, stop=False)
                                last = e.matmul(tgt, nllo[:, cc * C:(cc + 1) * C],
                                                sc["INDBD"][:, gh * 1024 + q * 512: gh * 1024 + (q + 1) * 512],
                                                start=False, stop=True)
                        return last
                    p.sec("pe", mm_ld, rd=["lflat", "lhl", "nlhi_r", "consts", "oa"],
                          wr=["pb", "pa"])

                    def ee_a(e):
                        e.tensor_scalar(out=sst[:, :], in0=pb[:, :], scalar1=0.0,
                                        scalar2=None, op0=ALU.min)
                        return e.tensor_scalar(out=xts[:, :], in0=pa[:, :], scalar1=0.0,
                                               scalar2=None, op0=ALU.min)
                    p.sec("dve", ee_a, rd=["pb", "pa"], wr=["lntmp", "xts"])

                    def ee_b(e):
                        e.activation(out=eet[:, 0:1024], in_=sst[:, :], func=AF.Exp)
                        return e.activation(out=eet[:, 1024:2048], in_=xts[:, :],
                                            func=AF.Exp)
                    p.sec("act", ee_b, rd=["lntmp", "xts"], wr=["ee_0", "ee_1"])

                    p.stage_cur = 7
                    def mk_rcc(e):
                        return e.tensor_copy(
                            out=rcc[:, :],
                            in_=eet[:, :].rearrange("p (h t) -> p h t", h=16)[:, :, C - 1:C])
                    p.sec("dve", mk_rcc, rd=["ee_0", "ee_1"], wr=["rcc"])
                    if "ee" in dbg and first_chunk:
                        p.sec("dma", lambda e: e.dma_start(out=dbg["ee"][:, :], in_=eet[:, :]),
                              rd=["ee_0", "ee_1"], n_dma=1)

                    p.stage_cur = 8
                    def mk_eep(e):
                        v3 = eet[:, :].rearrange("p (h t) -> p h t", h=16)
                        return e.tensor_tensor(out=v3, in0=v3,
                                               in1=sc["MASK1"][:, :].rearrange("p (a t) -> p a t", a=1)
                                               .broadcast_to((128, 16, 128)),
                                               op=ALU.mult)
                    p.sec("dve", mk_eep, rd=["rcc", "consts"],
                          wr=["eep", "ee_0", "ee_1"])

                    p.stage_cur = 9
                    def mm_kqx(e, cc=cc, src=kfn):
                        last = None
                        for gh, ps in ((0, pb), (1, pa)):
                            for hh in range(8):
                                h = gh * 8 + hh
                                fb = h // 2
                                khx = khfcE if h % 2 == 0 else khfcO
                                last = e.matmul(ps[:, hh * 128:(hh + 1) * 128],
                                                khx[:, fb * 128:(fb + 1) * 128],
                                                src[:, fb * TILE + cc * C: fb * TILE + cc * C + C],
                                                start=True, stop=True)
                        return last
                    p.sec("pe", lambda e, cc=cc: mm_kqx(e, cc, kfn),
                          rd=["khfc2"] + kfn_r + ["rhs", "lntmp", "xts"],
                          wr=["pb", "pa"])

                    def mk_pt(e):
                        e.tensor_tensor(out=ptraw[:, 0:1024], in0=eet[:, 0:1024],
                                        in1=pb[:, :], op=ALU.mult)
                        return e.tensor_tensor(out=ptraw[:, 1024:2048], in0=eet[:, 1024:2048],
                                               in1=pa[:, :], op=ALU.mult)
                    p.sec("dve", mk_pt, rd=["pb", "pa", "eep"], wr=["ptraw_0", "ptraw_1"])

                    p.sec("pe", lambda e, cc=cc: mm_kqx(e, cc, qfn),
                          rd=["khfc2"] + qfn_r + ["ptraw_0", "ptraw_1"], wr=["pb", "pa"])

                    for gh in range(2):
                        def mk_eeg(e, gh=gh):
                            return e.tensor_tensor(out=eegh[:, :].rearrange("p (h t) -> p h t", h=8),
                                                   in0=eet[:, gh * 1024:(gh + 1) * 1024].rearrange("p (h t) -> p h t", h=8),
                                                   in1=sc["IDT1"][:, :].rearrange("p (a t) -> p a t", a=1)
                                                   .broadcast_to((128, 8, 128)),
                                                   op=ALU.add)
                        p.sec("gps", mk_eeg, rd=["eep", "consts"], wr=["eegh"])

                        def mk_gt(e, gh=gh):
                            return e.tensor_tensor(out=gtt[:, gh * 1024:(gh + 1) * 1024],
                                                   in0=eegh[:, :],
                                                   in1=(pb if gh == 0 else pa)[:, :],
                                                   op=ALU.mult)
                        p.sec("dve", mk_gt, rd=["pb", "pa", "eegh"], wr=["gt_%d" % gh])

                    if "ptraw" in dbg and first_chunk:
                        p.sec("dma", lambda e: e.dma_start(out=dbg["ptraw"][:, :], in_=ptraw[:, :]),
                              rd=["ptraw_0", "ptraw_1"], n_dma=1)
                    if "gt" in dbg and first_chunk:
                        p.sec("dma", lambda e: e.dma_start(out=dbg["gt"][:, :], in_=gtt[:, :]),
                              rd=["gt_0", "gt_1"], n_dma=1)

                    p.stage_cur = 10
                    def mk_dd(e):
                        last = None
                        for i in range(4):
                            last = e.tensor_copy(
                                out=dd[32 * i:32 * i + 32, :].rearrange("p (h t) -> p h t", h=16),
                                in_=ptraw[32 * i:32 * i + 32, :].rearrange("p (h t) -> p h t", h=16)[:, :, 32 * i:32 * i + 32])
                        return last
                    p.sec("dve", mk_dd, rd=["ptraw_0", "ptraw_1"], wr=["dd"])

                    p.stage_cur = 11
                    def mk_ptod(e):
                        v3 = ptod[:, :].rearrange("p (h t) -> p h t", h=16)
                        return e.tensor_tensor(out=v3, in0=v3,
                                               in1=sc["NEGBLK1"][:, :].rearrange("p (a t) -> p a t", a=1)
                                               .broadcast_to((128, 16, 128)),
                                               op=ALU.mult)
                    p.sec("gps", mk_ptod, rd=["ptraw_0", "ptraw_1", "dd", "consts"], wr=["ptod"])

                    p.stage_cur = 12
                    def vtp(e, dst, src):
                        return e.transpose(out=dst[:, :], in_=src[:, :])

                    p.sec("dve", lambda e: vtp(e, ddt, dd), rd=["dd"], wr=["ddt"])

                    sqr_seq = [(dd, ddt, dp[1], dtp[2], "d2"),
                               (dp[1], dtp[2], dp[2], dtp[4], "d4"),
                               (dp[2], dtp[4], dp[1], dtp[8], "d8")]
                    iddb = (sc["IDD1"][:, :].rearrange("p (a t) -> p a t", a=1)
                            .broadcast_to((128, 16, 32)))

                    def subIDD(e, dst, src):
                        return e.tensor_tensor(out=dst[:, :].rearrange("p (h t) -> p h t", h=16),
                                               in0=iddb,
                                               in1=src[:, :].rearrange("p (h t) -> p h t", h=16),
                                               op=ALU.subtract)

                    def mm_sq(e, srcT, src, out):
                        last = None
                        for i in range(4):
                            for h in range(H):
                                last = e.matmul(out[32 * i:32 * i + 32, h * 32:(h + 1) * 32],
                                                srcT[32 * i:32 * i + 32, h * 32:(h + 1) * 32],
                                                src[32 * i:32 * i + 32, h * 32:(h + 1) * 32],
                                                start=True, stop=True,
                                                tile_position=(32 * i, 32 * i))
                        return last
                    mm_prod = mm_sq
                    tAt = eegh[:, 512:1024]
                    p.sec("pe", lambda e: mm_sq(e, ddt, dd, pc), rd=["dd", "ddt"], wr=["pc"])
                    p.sec("dve", lambda e: e.tensor_copy(out=dp[1][:, :], in_=pc[:, :]),
                          rd=["pc"], wr=["d2"])
                    p.sec("dve", lambda e: vtp(e, dtp[2], dp[1]), rd=["d2"], wr=["d2t"])
                    p.sec("gps", lambda e: subIDD(e, ggl, ddt), rd=["ddt", "consts"], wr=["ggl"])
                    p.sec("gps", lambda e: subIDD(e, ggr, dp[1]), rd=["d2", "consts"], wr=["ggr"])
                    p.sec("pe", lambda e: mm_prod(e, ggl, ggr, pb[:, 0:512]),
                          rd=["ggl", "ggr"], wr=["pbA", "pb"])
                    p.sec("dve", lambda e: e.tensor_copy(out=dp[2][:, :], in_=pb[:, 0:512]),
                          rd=["pbA", "pb"], wr=["aev"])
                    p.sec("dve", lambda e: vtp(e, tAt, dp[2]), rd=["aev"], wr=["tAt", "eegh"])
                    p.sec("pe", lambda e: mm_sq(e, dtp[2], dp[1], pc), rd=["d2", "d2t"], wr=["pc"])
                    p.sec("dve", lambda e: e.tensor_copy(out=dp[2][:, :], in_=pc[:, :]),
                          rd=["pc", "aev", "tAt"], wr=["d4"])
                    p.sec("dve", lambda e: vtp(e, dtp[4], dp[2]), rd=["d4"], wr=["d4t"])
                    p.sec("pe", lambda e: mm_sq(e, dtp[4], dp[2], pc), rd=["d4", "d4t"], wr=["pc"])
                    p.sec("dve", lambda e: e.tensor_copy(out=dp[1][:, :], in_=pc[:, :]),
                          rd=["pc"], wr=["d8"])
                    p.sec("dve", lambda e: vtp(e, dtp[8], dp[1]), rd=["d8"], wr=["d8t"])
                    p.sec("gps", lambda e: subIDD(e, ggl, dtp[4]), rd=["d4t", "consts"], wr=["ggl"])
                    p.sec("gps", lambda e: subIDD(e, ggr, dp[1]), rd=["d8", "consts"], wr=["ggr"])
                    p.sec("pe", lambda e: mm_prod(e, ggl, ggr, pb[:, 512:1024]),
                          rd=["ggl", "ggr"], wr=["pbB", "pb"])
                    p.sec("dve", lambda e: e.tensor_copy(out=ggl[:, :], in_=pb[:, 512:1024]),
                          rd=["pbB", "pb"], wr=["tB", "ggl"])
                    p.sec("pe", lambda e: mm_sq(e, dtp[8], dp[1], pc), rd=["d8", "d8t"], wr=["pc"])
                    p.sec("dve", lambda e: e.tensor_copy(out=dp[16][:, :], in_=pc[:, :]),
                          rd=["pc"], wr=["d16"])
                    p.sec("pe", lambda e: mm_prod(e, tAt, ggl, pb[:, 0:512]),
                          rd=["tAt", "tB", "pbA", "pb", "eegh"], wr=["pbAB", "pbA", "pb"])
                    p.sec("dve", lambda e: e.tensor_copy(out=dp[1][:, :], in_=pb[:, 0:512]),
                          rd=["pbAB", "pb", "d8", "d8t"], wr=["ab"])
                    p.sec("dve", lambda e: vtp(e, dtp[4], dp[1]), rd=["ab", "tB"], wr=["abt"])
                    p.sec("gps", lambda e: subIDD(e, ggr, dp[16]), rd=["d16", "consts"], wr=["ggr"])
                    p.sec("pe", lambda e: mm_prod(e, dtp[4], ggr, pc),
                          rd=["abt", "ggr", "d16", "pc"], wr=["pc"])
                    p.sec("dve", lambda e: e.tensor_copy(out=tfin[:, :], in_=pc[:, :]),
                          rd=["pc"], wr=["tfin"])
                    if "tfin" in dbg and first_chunk:
                        p.sec("dma", lambda e: e.dma_start(out=dbg["tfin"][:, :], in_=tfin[:, :]),
                              rd=["tfin"], n_dma=1)

                    p.stage_cur = 13
                    for j in range(4):
                        def mm_od(e, j=j):
                            last = None
                            for h in range(H):
                                last = e.matmul(pa[32 * j:32 * j + 32, h * HD:(h + 1) * HD],
                                                ptod[:, h * 128 + 32 * j: h * 128 + 32 * j + 32],
                                                uu[:, h * HD:(h + 1) * HD],
                                                start=True, stop=True,
                                                tile_position=(0, 32 * j))
                            return last
                        p.sec("pe", mm_od, rd=["ptod", "uu"], wr=["pa"])

                        def mk_pre(e, j=j):
                            return e.scalar_tensor_tensor(
                                out=preb[32 * j:32 * j + 32, :],
                                in0=pa[32 * j:32 * j + 32, :], scalar=1.0,
                                in1=rhs_[32 * j:32 * j + 32, :],
                                op0=ALU.mult, op1=ALU.add)
                        p.sec("dve", mk_pre, rd=["pa", "rhs"], wr=["preb"])

                        def mm_t32(e, j=j):
                            last = None
                            for h in range(H):
                                last = e.matmul(pb[32 * j:32 * j + 32, h * HD:(h + 1) * HD],
                                                tfin[32 * j:32 * j + 32, h * 32:(h + 1) * 32],
                                                preb[32 * j:32 * j + 32, h * HD:(h + 1) * HD],
                                                start=True, stop=True,
                                                tile_position=(32 * j, 32 * j))
                            return last
                        p.sec("pe", mm_t32, rd=["tfin", "preb"], wr=["pb"])

                        def ev_u(e, j=j):
                            return e.tensor_copy(out=uu[32 * j:32 * j + 32, :],
                                                 in_=pb[32 * j:32 * j + 32, :])
                        p.sec("dve", ev_u, rd=["pb"], wr=["uu"])
                    if "uu" in dbg and first_chunk:
                        p.sec("dma", lambda e: e.dma_start(out=dbg["uu"][:, :], in_=uu[:, :]),
                              rd=["uu"], n_dma=1)

                    p.stage_cur = 14
                    def mm_o(e):
                        last = None
                        for h in range(H):
                            last = e.matmul(pa[:, h * HD:(h + 1) * HD],
                                            gtt[:, h * 128:(h + 1) * 128],
                                            uu[:, h * HD:(h + 1) * HD],
                                            start=True, stop=True)
                        return last
                    p.sec("pe", mm_o, rd=["gt_0", "gt_1", "uu"], wr=["pa"])

                    def mk_of(e, cc=cc):
                        return e.tensor_tensor(out=outst[:, :], in0=outst[:, :], in1=pa[:, :],
                                               op=ALU.add)
                    p.sec("dve", mk_of, rd=["pa", "oa", "outst"], wr=["outst"])

                    p.stage_cur = 17 if chunk_upto >= 16 else 99
                    def st_outs(e, T=T, cc=cc):
                        return e.dma_start(out=houts[T * TILE + cc * C: T * TILE + (cc + 1) * C, :],
                                           in_=outst[:, :])
                    p.sec("dma", st_outs, rd=["outst"], wr=["houts_%d_%d" % (T, cc)], n_dma=1)
                    if "outs0" in dbg and first_chunk:
                        p.sec("dma", lambda e: e.dma_start(out=dbg["outs0"][:, :], in_=outst[:, :]),
                              rd=["outst"], n_dma=1)

                    p.stage_cur = 15
                    def sq_o(e):
                        return e.tensor_mul(eegh[:, :], outst[:, :], outst[:, :])
                    p.sec("gps", sq_o, rd=["outst"], wr=["eegh"])

                    def mm_st(e, first=not stats_on[0]):
                        e.matmul(pstat[0:1, 0:512], sc["ONESC"][:, :], outst[:, 0:512],
                                 start=first, stop=True, skip_group_check=True)
                        e.matmul(pstat[32:33, 0:512], sc["ONESC"][:, :], outst[:, 512:1024],
                                 start=first, stop=True, skip_group_check=True)
                        e.matmul(pstat[64:65, 0:512], sc["ONESC"][:, :], eegh[:, 0:512],
                                 start=first, stop=True, skip_group_check=True)
                        return e.matmul(pstat[96:97, 0:512], sc["ONESC"][:, :], eegh[:, 512:1024],
                                        start=first, stop=True, skip_group_check=True,
                                        tile_position=(0, 96))
                    p.sec("pe", mm_st, rd=["outst", "eegh", "consts"], wr=["pstat"])
                    stats_on[0] = True

                    p.stage_cur = 16
                    def mk_up(e):
                        return e.tensor_tensor(
                            out=upb[:, :].rearrange("p (h d) -> p h d", h=16),
                            in0=uu[:, :].rearrange("p (h d) -> p h d", h=16),
                            in1=rcc[:, :].rearrange("p (h a) -> p h a", a=1)
                                .broadcast_to((128, 16, 64)),
                            op=ALU.mult)
                    p.sec("gps", mk_up, rd=["uu", "rcc"], wr=["upb"])

                    def mk_acdg(e):
                        return e.tensor_tensor(
                            out=acdg[:, :], in0=sc["ID16"][:, :],
                            in1=abti[:, C - 1:C].broadcast_to((16, 16)),
                            op=ALU.mult)
                    p.sec("gps", mk_acdg, rd=["achk", "consts"], wr=["acdg"])

                    def mm_acv(e):
                        return e.matmul(pc[:, 0:16], sc["ONES16"][:, :], acdg[:, :],
                                        start=True, stop=True)
                    p.sec("pe", mm_acv, rd=["acdg", "consts", "tfin"], wr=["pc"])

                    def mm_dss(e):
                        last = None
                        for h in range(H):
                            for rb in (0, 64):
                                last = e.matmul(pa[rb:rb + 64, h * HD:(h + 1) * HD],
                                                khtm[:, h * HD:(h + 1) * HD],
                                                upb[:, h * HD:(h + 1) * HD],
                                                start=True, stop=True)
                        return last
                    p.sec("pe", mm_dss, rd=["khtm", "upb", "outst"], wr=["pa"])

                    def upd_ss1(e):
                        return e.tensor_tensor(
                            out=sst[:, :].rearrange("p (h d) -> p h d", h=16),
                            in0=ssf[:, :].rearrange("p (h d) -> p h d", h=16),
                            in1=pc[:, 0:16].rearrange("p (h a) -> p h a", a=1)
                            .broadcast_to((128, 16, 64)), op=ALU.mult)
                    p.sec("dve", upd_ss1, rd=["pc", "ssf"], wr=["lntmp"])

                    def upd_ss2(e):
                        return e.tensor_tensor(out=ssf[:, :], in0=sst[:, :], in1=pa[:, :],
                                               op=ALU.add)
                    p.sec("dve", upd_ss2, rd=["pa", "lntmp"], wr=["ssf"])

                    def cast_ss(e):
                        e.tensor_copy(out=ssbE[0:64, :], in_=ssf[0:64, :])
                        return e.tensor_copy(out=ssbO[64:128, :], in_=ssf[64:128, :])
                    p.sec("gps", cast_ss, rd=["ssf"], wr=["ssb"])
                    if "ssf" in dbg and T == 0 and cc == NCH - 1:
                        p.sec("dma", lambda e: e.dma_start(out=dbg["ssf"][:, :], in_=ssf[:, :]),
                              rd=["ssf"], n_dma=1)

            def amax_loc(e):
                return e.reduce_max(out=amaxc[:, 36:37], in_=amaxc[:, 0:32], axis=AX.X)
            p.sec("dve", amax_loc, rd=["amaxc"], wr=["amaxv"])

            def st_am(e):
                a = e.dma_start(out=hrow[0:1, :].rearrange("o (p a) -> (o p) a", a=1), in_=amaxc[:, 36:37],
                                allow_slow_non_contiguous=True)
                b = e.dma_start(out=hrow[1:2, :].rearrange("o (p a) -> (o p) a", a=1), in_=amaxc[:, 35:36],
                                allow_slow_non_contiguous=True)
                return [a, b]
            p.sec("dma", st_am, rd=["amaxv", "amaxc"], wr=["hrow"], n_dma=2)

            def ld_am(e):
                return e.dma_start(out=srow[0:2, :], in_=hrow[0:2, :])
            p.sec("dma", ld_am, rd=["hrow"], wr=["srow"], n_dma=1)

            def red_am(e):
                return e.reduce_max(out=sval[0:2, 0:1], in_=srow[0:2, :], axis=AX.X)
            p.sec("dve", red_am, rd=["srow"], wr=["sval"])

            def st_cc(e):
                return e.dma_start(out=hccin[:, :], in_=sval[0:1, 0:1])
            p.sec("dma", st_cc, rd=["sval"], wr=["hccin"], n_dma=1)

            if n_cores == 8:
                def do_cc(e):
                    return e.collective_compute(
                        "AllReduce", ALU.max, replica_groups=[[0, 1, 2, 3, 4, 5, 6, 7]],
                        ins=[hccin[:, :]], outs=[hccout[:, :]])
                p.sec("gps", do_cc, rd=["hccin"], wr=["hccout"])
            else:
                def do_cc(e):
                    return e.dma_start(out=hccout[:, :], in_=hccin[:, :])
                p.sec("dma", do_cc, rd=["hccin"], wr=["hccout"], n_dma=1)

            def ld_cc(e):
                return e.dma_start(out=sval[0:1, 1:2], in_=hccout[:, :])
            p.sec("dma", ld_cc, rd=["hccout"], wr=["sval_g"], n_dma=1)

            def st_wam(e):
                return e.dma_start(out=hwam[:, :], in_=sval[1:2, 0:1])
            p.sec("dma", st_wam, rd=["sval"], wr=["hwam"], n_dma=1)

            def ld_wam(e):
                return e.dma_start(out=sval[0:1, 5:6], in_=hwam[:, :])
            p.sec("dma", ld_wam, rd=["hwam"], wr=["sval_w"], n_dma=1)

            scale_steps = [
                lambda e: e.tensor_scalar(out=sval[0:1, 2:3], in0=sval[0:1, 1:2],
                                          scalar1=1e-12, scalar2=None, op0=ALU.max),
                lambda e: e.reciprocal(out=sval[0:1, 3:4], in_=sval[0:1, 2:3]),
                lambda e: e.tensor_scalar(out=sval[0:1, 4:5], in0=sval[0:1, 3:4],
                                          scalar1=224.0, scalar2=None, op0=ALU.mult),
                lambda e: e.tensor_scalar(out=sval[0:1, 9:10], in0=sval[0:1, 5:6],
                                          scalar1=1e-12, scalar2=None, op0=ALU.max),
                lambda e: e.reciprocal(out=sval[0:1, 10:11], in_=sval[0:1, 9:10]),
                lambda e: e.tensor_scalar(out=sval[0:1, 6:7], in0=sval[0:1, 10:11],
                                          scalar1=224.0, scalar2=None, op0=ALU.mult),
                lambda e: e.tensor_scalar(out=sval[0:1, 7:8], in0=sval[0:1, 4:5],
                                          scalar1=sval[0:1, 6:7], scalar2=None, op0=ALU.mult),
                lambda e: e.tensor_scalar(out=sval[0:1, 8:9], in0=sval[0:1, 2:3],
                                          scalar1=sval[0:1, 9:10], scalar2=None, op0=ALU.mult),
                lambda e: e.tensor_scalar(out=sval[0:1, 8:9], in0=sval[0:1, 8:9],
                                          scalar1=1.0 / (224.0 * 224.0), scalar2=None,
                                          op0=ALU.mult),
            ]
            for si_, stp in enumerate(scale_steps):
                p.sec("dve", stp, rd=["sval_g", "sval_w", "sval", "scales"], wr=["scales"])
            if "aval" in dbg:
                p.sec("dma", lambda e: e.dma_start(out=dbg["aval"][:, :], in_=sval[0:1, 4:8]),
                      rd=["scales"], n_dma=1)

            def mm_bcast(e):
                e.matmul(pc[:, 0:1], sc["ONESF"][:, :], sval[0:1, 4:5], start=True, stop=True)
                e.matmul(pc[:, 1:2], sc["ONESF"][:, :], sval[0:1, 8:9], start=True, stop=True)
                return e.matmul(pc[:, 2:3], sc["ONESF"][:, :], sval[0:1, 6:7],
                                start=True, stop=True)
            p.sec("pe", mm_bcast, rd=["scales", "consts"], wr=["pc"])

            def ev_bcast(e):
                return e.tensor_copy(out=sxcol[:, 0:3], in_=pc[:, 0:3])
            p.sec("dve", ev_bcast, rd=["pc"], wr=["sxcol"])

            if do_attn:
                def ev_stats(e):
                    e.tensor_copy(out=sst[0:1, 0:512], in_=pstat[0:1, 0:512])
                    e.tensor_copy(out=sst[32:33, 0:512], in_=pstat[32:33, 0:512])
                    e.tensor_copy(out=sst[64:65, 0:512], in_=pstat[64:65, 0:512])
                    return e.tensor_copy(out=sst[96:97, 0:512], in_=pstat[96:97, 0:512])
                p.sec("dve", ev_stats, rd=["pstat"], wr=["lntmp"])

                def red_stats(e):
                    last = None
                    order = ((0, 0), (32, 8), (64, 16), (96, 24))
                    for r, c0 in order:
                        last = e.tensor_reduce(out=gnc[0:1, c0:c0 + 8],
                                               in_=sst[r:r + 1, 0:512].rearrange("p (h d) -> p h d", h=8),
                                               axis=AX.X, op=ALU.add)
                    return last
                p.sec("dve", red_stats, rd=["lntmp"], wr=["gnc"])

                def tp_stats(e):
                    return e.transpose(out=gnc[0:32, 32:64], in_=gnc[0:32, 0:32])
                p.sec("dve", tp_stats, rd=["gnc"], wr=["gnc"])
                NTOK = float(n_tiles * TILE * HD)

                def mk_mu(e):
                    return e.tensor_scalar(out=gnc[0:32, 33:34], in0=gnc[0:32, 32:33],
                                           scalar1=1.0 / NTOK, scalar2=None, op0=ALU.mult)
                p.sec("dve", mk_mu, rd=["gnc"], wr=["gnc"])

                def st_gn(e):
                    return e.dma_start(out=hrow[2:3, 0:32].rearrange("o (p a) -> (o p) a", a=1),
                                       in_=gnc[0:32, 33:34],
                                       allow_slow_non_contiguous=True)
                p.sec("dma", st_gn, rd=["gnc"], wr=["hrow2"], n_dma=1)

                def ld_gn(e):
                    a = e.dma_start(out=gnr[:, 0:1],
                                    in_=hrow[2:3, 0:16].rearrange("o (f a) -> (o f) a", a=1),
                                    allow_slow_non_contiguous=True)
                    b = e.dma_start(out=gnr[:, 2:3],
                                    in_=hrow[2:3, 16:32].rearrange("o (f a) -> (o f) a", a=1),
                                    allow_slow_non_contiguous=True)
                    return [a, b]
                p.sec("dma", ld_gn, rd=["hrow2"], wr=["gnr"], n_dma=2)

                def mk_gn1a(e):
                    return e.tensor_scalar(out=gnr[:, 4:5], in0=gnr[:, 0:1],
                                           scalar1=gnr[:, 0:1], scalar2=None, op0=ALU.mult)
                p.sec("dve", mk_gn1a, rd=["gnr"], wr=["gnr_a"])

                def mk_gn1b(e):
                    return e.tensor_tensor(out=gnr[:, 4:5], in0=gnr[:, 2:3], in1=gnr[:, 4:5],
                                           op=ALU.subtract)
                p.sec("dve", mk_gn1b, rd=["gnr_a", "gnr"], wr=["gnr"])

                def mk_gn2(e):
                    return e.activation(out=gnr[:, 5:6], in_=gnr[:, 4:5], func=AF.Sqrt,
                                        bias=1e-5)
                p.sec("act", mk_gn2, rd=["gnr"], wr=["gnr"])

                def mk_vb0(e):
                    return e.reciprocal(out=gnr[:, 5:6], in_=gnr[:, 5:6])
                p.sec("dve", mk_vb0, rd=["gnr"], wr=["gnr_c"])

                def mk_vb1(e):
                    e.tensor_tensor(out=vbr[:, :], in0=sc["PATTERN16"][:, :],
                                    in1=gnr[:, 5:6].broadcast_to((16, 8)), op=ALU.mult)
                    return e.tensor_tensor(out=vbm[:, :], in0=sc["PATTERN16"][:, :],
                                           in1=gnr[:, 0:1].broadcast_to((16, 8)), op=ALU.mult)
                p.sec("dve", mk_vb1, rd=["gnr_c", "gnr", "consts"], wr=["vb"])

                def mk_indpf(e):
                    return e.tensor_copy(out=INDPF[:, :], in_=sc["INDP"][:, :])
                p.sec("dve", mk_indpf, rd=["consts"], wr=["indpf"])

                def mm_vb2(e):
                    e.matmul(pc[:, 4:12], INDPF[:, :], vbr[:, :], start=True, stop=True)
                    return e.matmul(pc[:, 12:20], INDPF[:, :], vbm[:, :], start=True, stop=True)
                p.sec("pe", mm_vb2, rd=["vb", "indpf"], wr=["pc"])

                def mk_gnf1(e):
                    return e.tensor_tensor(out=gnmul[:, :], in0=pc[:, 4:12],
                                           in1=sbias["gns"][:, :], op=ALU.mult)
                p.sec("dve", mk_gnf1, rd=["pc", "consts"], wr=["gnf_a"])

                def mk_gnf2(e):
                    return e.tensor_tensor(out=gnadd[:, :], in0=pc[:, 12:20], in1=gnmul[:, :],
                                           op=ALU.mult)
                p.sec("dve", mk_gnf2, rd=["pc", "gnf_a"], wr=["gnf_b"])

                def mk_gnf3(e):
                    return e.tensor_tensor(out=gnadd[:, :], in0=sbias["gnb"][:, :],
                                           in1=gnadd[:, :], op=ALU.subtract)
                p.sec("dve", mk_gnf3, rd=["gnf_b", "consts"], wr=["gnfin"])
                if "gnmul" in dbg:
                    p.sec("dma", lambda e: e.dma_start(out=dbg["gnmul"][:, :], in_=gnmul[:, :]),
                          rd=["gnfin"], n_dma=1)
                if "gnadd" in dbg:
                    p.sec("dma", lambda e: e.dma_start(out=dbg["gnadd"][:, :], in_=gnadd[:, :]),
                          rd=["gnfin"], n_dma=1)

            def mk_brow(e):
                e.tensor_copy(out=sbrow16[0:1, :], in_=sbrow[0:1, :])
                return e.tensor_scalar(out=sbrow16[32:33, :], in0=sbrow[32:33, :],
                                       scalar1=sval[0:1, 7:8], scalar2=None, op0=ALU.mult)
            p.sec("dve", mk_brow, rd=["consts", "scales"], wr=["brow"])

            p.flush()
            if do_phase2:
                esi.close()
                es2 = ExitStack()
                s2 = lambda n, s, dt=F32: es2.enter_context(nc.sbuf_tensor(n, s, dt))
                swg = s2("swg", [128, 8 * DM], BF16)
                swo = s2("swo", [128, 8 * DM], BF16)
                wt8 = s2("wt8", [128, 8 * DM], FP8)
                wtf2 = s2("wtf2", [128, DM])
                xnf2 = s2("xnf2", [128, 8 * TILE])
                xnfb2 = s2("xnfb2", [128, 8 * TILE], BF16)
                xq8 = s2("xq8", [128, 8 * TILE], FP8)
                ofm = s2("ofm", [128, 8 * TILE], BF16)
                gf = s2("gf", [128, 8 * TILE], BF16)
                yf = s2("yf", [128, 8 * TILE], BF16)
                gyf = s2("gyf", [128, 8 * TILE], BF16)
                xt2 = s2("xt2", [128, DM])
                outf = s2("outf", [128, DM])

                with es2:
                    def ld_w2(e):
                        ins = []
                        for w, dst in (("wg", swg), ("wo", swo)):
                            for cc2 in range(8):
                                ins.append(e.dma_start(out=dst[:, cc2 * DM:(cc2 + 1) * DM],
                                                       in_=hw[w][cc2, :, :]))
                        return ins
                    p.sec("gdma", ld_w2, wr=["w2"], n_dma=16)

                    for cc2 in range(8):
                        def ld_wt(e, cc2=cc2):
                            return e.dma_start(out=wtf2[:, :], in_=hwt[cc2, :, :])
                        p.sec("dma", ld_wt, rd=["hwt_%d" % cc2], wr=["wtf2"], n_dma=1)

                        def q_wt(e, cc2=cc2):
                            return e.tensor_scalar(out=wt8[:, cc2 * DM:(cc2 + 1) * DM],
                                                   in0=wtf2[:, :], scalar1=sxcol[:, 2:3],
                                                   scalar2=None, op0=ALU.mult)
                        p.sec("dve", q_wt, rd=["wtf2", "sxcol"], wr=["wt8_%d" % cc2])
                    wt8_r = ["wt8_%d" % i for i in range(8)]

                    for T in range(n_tiles):
                        def ld_p2(e, T=T):
                            ins = [e.dma_start(out=xnf2[:, :], in_=hxnf[T, :, :])]
                            for fb in range(8):
                                ins.append(e.dma_start(
                                    out=ofm[:, fb * 512:(fb + 1) * 512],
                                    in_=houts[T * TILE:(T + 1) * TILE, fb * 128:(fb + 1) * 128],
                                    transpose=True))
                            return ins
                        deps_outs = ["houts_%d_%d" % (T, cc3) for cc3 in range(NCH)] if do_attn else []
                        p.sec("dma", ld_p2, rd=["hxnf_%d_0" % T, "hxnf_%d_1" % T,
                                                "hxnf_%d_2" % T, "hxnf_%d_3" % T] + deps_outs,
                              wr=["xnf2", "ofm"], n_dma=9)

                        def cast_x2(e):
                            return e.tensor_copy(out=xnfb2[:, :], in_=xnf2[:, :])
                        p.sec("dve", cast_x2, rd=["xnf2"], wr=["xnfb2"])

                        def q_x(e):
                            return e.activation(out=xq8[:, :], in_=xnf2[:, :], func=AF.Identity,
                                                scale=sxcol[:, 0:1])
                        p.sec("act", q_x, rd=["xnf2", "sxcol"], wr=["xq8"])

                        for fbp in range(4):
                            psg, pkg = (pa, "pa") if fbp % 2 == 0 else (pb, "pb")

                            def mm_g(e, fbp=fbp, psg=psg):
                                last = None
                                for half in range(2):
                                    fb = fbp * 2 + half
                                    for cc2 in range(8):
                                        last = e.matmul(
                                            psg[:, half * 512:(half + 1) * 512],
                                            swg[:, cc2 * DM + fb * 128: cc2 * DM + (fb + 1) * 128],
                                            xnfb2[:, cc2 * 512:(cc2 + 1) * 512],
                                            start=(cc2 == 0), stop=(cc2 == 7))
                                return last
                            p.sec("pe", mm_g, rd=["xnfb2", "w2"], wr=[pkg])

                            def ev_g(e, fbp=fbp, psg=psg):
                                last = None
                                for half in range(2):
                                    fb = fbp * 2 + half
                                    last = e.activation(out=gf[:, fb * 512:(fb + 1) * 512],
                                                        in_=psg[:, half * 512:(half + 1) * 512],
                                                        func=AF.Silu, bias=sbias["bg"][:, fb:fb + 1])
                                return last
                            p.sec("act", ev_g, rd=[pkg, "consts"], wr=["gf_%d" % fbp])

                        def mk_y(e):
                            last = None
                            for fb in range(8):
                                last = e.tensor_scalar(out=yf[:, fb * 512:(fb + 1) * 512],
                                                       in0=ofm[:, fb * 512:(fb + 1) * 512],
                                                       scalar1=gnmul[:, fb:fb + 1],
                                                       scalar2=gnadd[:, fb:fb + 1],
                                                       op0=ALU.mult, op1=ALU.add)
                            return last
                        p.sec("dve", mk_y, rd=["ofm", "gnfin"], wr=["yf"])

                        def mk_gy(e):
                            return e.tensor_mul(gyf[:, :], gf[:, :], yf[:, :])
                        p.sec("dve", mk_gy, rd=["yf"] + ["gf_%d" % i for i in range(4)],
                              wr=["gyf"])

                        for s_ in range(NSUB):
                            def ld_x2(e, T=T, s_=s_):
                                return e.dma_start(
                                    out=xt2[:, :],
                                    in_=hx[T * TILE + s_ * 128: T * TILE + (s_ + 1) * 128, :])
                            p.sec("dma", ld_x2, wr=["xt2"], n_dma=1)

                            atth = ((pa[:, 0:512], pa[:, 512:1024]) if s_ % 2 == 0
                                    else (pc[:, 0:512], pstat[:, 0:512]))
                            attk = "pa" if s_ % 2 == 0 else "pcs"

                            def mm_att(e, s_=s_, atth=atth):
                                last = None
                                for half in range(2):
                                    for cc2 in range(8):
                                        e.matmul(
                                            atth[half],
                                            gyf[:, cc2 * 512 + s_ * 128: cc2 * 512 + (s_ + 1) * 128],
                                            swo[:, cc2 * DM + half * 512: cc2 * DM + (half + 1) * 512],
                                            start=(cc2 == 0), stop=False)
                                    last = e.matmul(atth[half], sc["ONES1"][:, :],
                                                    sbrow16[0:1, half * 512:(half + 1) * 512],
                                                    start=False, stop=True)
                                return last
                            p.sec("pe", mm_att, rd=["gyf", "w2", "consts", "brow"], wr=[attk])

                            def mm_ffn(e, s_=s_):
                                last = None
                                for half in range(2):
                                    for cc2 in range(8):
                                        e.matmul(
                                            pb[:, half * 512:(half + 1) * 512],
                                            xq8[:, cc2 * 512 + s_ * 128: cc2 * 512 + (s_ + 1) * 128],
                                            wt8[:, cc2 * DM + half * 512: cc2 * DM + (half + 1) * 512],
                                            start=(cc2 == 0), stop=False)
                                    last = e.matmul(pb[:, half * 512:(half + 1) * 512],
                                                    sc["ONES33"][32:33, :],
                                                    sbrow16[32:33, half * 512:(half + 1) * 512],
                                                    start=False, stop=True)
                                return last
                            p.sec("pe", mm_ffn, rd=["xq8", "brow", "consts"] + wt8_r, wr=["pb"])

                            def mk_out1(e, s_=s_):
                                return e.scalar_tensor_tensor(
                                    out=outf[:, :], in0=pb[:, :], scalar=sxcol[:, 1:2],
                                    in1=xt2[:, :], op0=ALU.mult, op1=ALU.add)
                            p.sec("dve", mk_out1, rd=["pb", "xt2", "sxcol"], wr=["outf_a"])

                            def mk_out2(e, s_=s_, atth=atth):
                                e.tensor_tensor(out=outf[:, 0:512], in0=outf[:, 0:512],
                                                in1=atth[0], op=ALU.add)
                                return e.tensor_tensor(out=outf[:, 512:1024],
                                                       in0=outf[:, 512:1024],
                                                       in1=atth[1], op=ALU.add)
                            p.sec("dve", mk_out2, rd=[attk, "outf_a"], wr=["outf"])

                            def st_y(e, T=T, s_=s_):
                                return e.dma_start(
                                    out=hy[T * TILE + s_ * 128: T * TILE + (s_ + 1) * 128, :],
                                    in_=outf[:, :])
                            p.sec("dma", st_y, rd=["outf"], wr=["y_%d_%d" % (T, s_)], n_dma=1)

            def fin(e):
                return None
            p.sec("dma", fin, no_inc=True,
                  extra=[p.t("gdma"), p.t("pe"), p.t("dve"), p.t("act"),
                         p.t("gps"), p.t("dma")])
            p.flush()

    return nc


def kernel(**inputs):
    cm = prep_common(inputs)
    x = np.asarray(inputs["L0_attn_x"], np.float32)
    nc = build_program()
    in_maps = []
    for i in range(8):
        m = dict(cm)
        m["x"] = np.ascontiguousarray(x[i])
        in_maps.append(m)
    res = run_bass_kernel_spmd(nc, in_maps, core_ids=list(range(8)))
    return np.stack([r["y"] for r in res.results], axis=0)


# version 18
# speedup vs baseline: 1.0389x; 1.0389x over previous
import numpy as np
import ml_dtypes
from contextlib import ExitStack
import concourse.bass as bass
import concourse.mybir as mybir
from concourse.bass_utils import run_bass_kernel_spmd

F32 = mybir.dt.float32
BF16 = mybir.dt.bfloat16
FP8 = mybir.dt.float8e4
AF = mybir.ActivationFunctionType
ALU = mybir.AluOpType
AX = mybir.AxisListType

S, DM, H, HD = 4096, 1024, 16, 64
TILE, NSUB, C, B = 512, 4, 128, 32
NCH = TILE // C
NT = S // TILE
P3 = TILE + 3


def make_consts():
    c = {}
    ii = np.arange(128)
    bf = ml_dtypes.bfloat16
    c["IDENT128"] = np.eye(128, dtype=bf)
    id64 = np.zeros((128, 64), dtype=bf)
    id64[:64, :] = np.eye(64)
    id64[64:, :] = np.eye(64)
    c["ID64"] = id64
    c["MASK1"] = (ii[:, None] < ii[None, :]).astype(np.float32).astype(bf)
    c["IDT1"] = np.eye(128, dtype=np.float32).astype(bf)
    c["NEGBLK1"] = (-(ii[:, None] < (ii[None, :] // B) * B).astype(np.float32)).astype(bf)
    idd = np.zeros((128, B), dtype=np.float32)
    for i in range(4):
        idd[32 * i:32 * i + 32, :] = np.eye(32)
    c["IDD1"] = idd.astype(bf)
    ind2 = np.zeros((128, 8 * H), dtype=np.float32)
    for fb in range(8):
        ind2[0:64, fb * H + 2 * fb] = 1.0
        ind2[64:128, fb * H + 2 * fb + 1] = 1.0
    c["IND2"] = ind2.astype(bf)
    indt = np.zeros((H, 8 * 128), dtype=np.float32)
    for fb in range(8):
        indt[2 * fb, fb * 128:fb * 128 + 64] = 1.0
        indt[2 * fb + 1, fb * 128 + 64:fb * 128 + 128] = 1.0
    c["INDT"] = indt.astype(bf)
    i64 = np.zeros((H, H * HD), dtype=np.float32)
    for h in range(H):
        i64[h, h * HD:(h + 1) * HD] = 1.0
    c["IND64"] = i64.astype(bf)
    ibd = np.zeros((H, H * 128), np.float32)
    for h in range(H):
        ibd[h, h * 128:(h + 1) * 128] = 1.0
    c["INDBD"] = ibd.astype(bf)
    c["ONES1"] = np.ones((1, 128), dtype=bf)
    o33 = np.zeros((33, 128), np.float32); o33[0] = 1; o33[32] = 1
    c["ONES33"] = o33.astype(bf)
    c["ONES2"] = np.ones((2, 128), dtype=bf)
    c["ONES16"] = np.ones((16, 128), dtype=bf)
    c["ONESC"] = np.ones((128, 1), dtype=bf)
    c["ONESF"] = np.ones((1, 128), dtype=np.float32)
    c["ID16"] = np.eye(16, dtype=np.float32)
    indp = np.zeros((H, 128), np.float32)
    for h in range(H):
        indp[h, 64 * (h % 2):64 * (h % 2) + 64] = 1.0
    c["INDP"] = indp.astype(bf)
    p16 = np.zeros((H, 8), np.float32)
    for h in range(H):
        p16[h, h // 2] = 1.0
    c["PATTERN16"] = p16
    cv = np.zeros((128, 4), np.float32)
    cv[:, 0] = 1e-5; cv[:, 1] = -1.0
    cv[0:64, 2] = 1.0
    cv[64:128, 3] = 1.0
    c["CVALS"] = cv
    return c


CONSTS = make_consts()


def prep_common(inputs):
    f32 = np.float32
    cm = {}
    for k, v in CONSTS.items():
        cm["c_" + k] = v

    def rw(w):
        return np.ascontiguousarray(np.asarray(w, f32).reshape(8, 128, -1))
    cm["wq"] = rw(inputs["L0_attn_Wq"]); cm["wk"] = rw(inputs["L0_attn_Wk"])
    cm["wv"] = rw(inputs["L0_attn_Wv"]); cm["wg"] = rw(inputs["L0_attn_Wg"])
    cm["wo"] = rw(inputs["L0_attn_Wo"])
    cm["wa"] = rw(inputs["L0_attn_Wa"]); cm["wb"] = rw(inputs["L0_attn_Wb"])
    cm["wffn"] = np.ascontiguousarray(np.asarray(inputs["L0_ffn_weight"], f32).reshape(8, 128, DM))

    def col8(b):
        return np.ascontiguousarray(np.asarray(b, f32).reshape(8, 128).T)
    cm["bq"] = col8(inputs["L0_attn_bq"]); cm["bk"] = col8(inputs["L0_attn_bk"])
    cm["bv"] = col8(inputs["L0_attn_bv"]); cm["bg"] = col8(inputs["L0_attn_bg"])
    cm["gns"] = col8(inputs["L0_attn_gn_scale"]); cm["gnb"] = col8(inputs["L0_attn_gn_bias"])
    cm["bo"] = np.ascontiguousarray(np.asarray(inputs["L0_attn_bo"], f32).reshape(1, DM))
    cm["bffn"] = np.ascontiguousarray(np.asarray(inputs["L0_ffn_bias"], f32).reshape(1, DM))
    cm["ba"] = np.ascontiguousarray(np.asarray(inputs["L0_attn_ba"], f32).reshape(H, 1))
    cm["bb"] = np.ascontiguousarray(np.asarray(inputs["L0_attn_bb"], f32).reshape(H, 1))
    cw = np.zeros((128, 8 * 3 * 4), f32)
    for pr, key in enumerate(("L0_attn_q_conv", "L0_attn_k_conv", "L0_attn_v_conv")):
        w = np.asarray(inputs[key], f32).reshape(8, 128, 4)
        for fb in range(8):
            cw[:, (pr * 8 + fb) * 4:(pr * 8 + fb) * 4 + 4] = w[fb]
    cm["convw"] = cw
    return cm


class Prog:
    ENG2BLK = {"pe": "tensor", "dve": "vector", "act": "scalar",
               "gps": "gpsimd", "dma": "sync", "gdma": "gpsimd"}

    def __init__(self, nc, block, sems):
        self.nc, self.block, self.sems = nc, block, sems
        self.cnt = {k: 0 for k in self.ENG2BLK}
        self.res = {}
        self.pend = {k: [] for k in self.ENG2BLK}

    FLUSH_N = 1
    FLUSH_EVERY = True

    def flush(self):
        for eng, bodies in self.pend.items():
            n = self.FLUSH_N
            for i in range(0, len(bodies), n):
                def mega(e, bs=tuple(bodies[i:i + n])):
                    for b in bs:
                        b(e)
                getattr(self.block, self.ENG2BLK[eng])(mega)
            self.pend[eng] = []

    def _get(self, r):
        if r not in self.res:
            self.res[r] = {"w": None, "r": []}
        return self.res[r]

    stage_limit = 99
    stage_cur = 0

    def sec(self, eng, fn, rd=(), wr=(), extra=(), n_dma=0, no_inc=False):
        if self.stage_cur > self.stage_limit:
            return (eng, self.cnt[eng])
        deps = []
        for r in rd:
            st = self._get(r)
            if st["w"] is not None:
                deps.append(st["w"])
        for r in wr:
            st = self._get(r)
            if st["w"] is not None:
                deps.append(st["w"])
            deps.extend(st["r"])
        deps.extend([d for d in extra if d is not None])
        is_dma = eng in ("dma", "gdma")
        if is_dma and n_dma > 0:
            deps.append((eng, self.cnt[eng]))
        dd = {}
        for (de, dc) in deps:
            if dc > 0 and dc > dd.get(de, 0):
                dd[de] = dc
        deps = list(dd.items())
        for (de, dc) in deps:
            assert dc <= self.cnt[de], (eng, de, dc, self.cnt[de])
        sems = self.sems

        def body(e):
            for (de, dc) in deps:
                e.wait_ge(sems[de], dc)
            last = fn(e)
            if is_dma:
                if isinstance(last, (list, tuple)):
                    for ins in last:
                        ins.then_inc(sems[eng], 16)
                elif last is not None:
                    last.then_inc(sems[eng], 16)
            elif last is not None:
                last.then_inc(sems[eng], 1)
            else:
                assert no_inc, "compute section returned None without no_inc"
        self.pend[eng].append(body)
        if self.FLUSH_EVERY:
            self.flush()
        if is_dma:
            self.cnt[eng] += 16 * n_dma
        elif not no_inc:
            self.cnt[eng] += 1
        tk = (eng, self.cnt[eng])
        for r in rd:
            self._get(r)["r"].append(tk)
        for r in wr:
            st = self._get(r)
            st["w"] = tk
            st["r"] = []
        return tk

    def t(self, eng):
        return (eng, self.cnt[eng])


def build_program(n_tiles=NT, do_attn=True, do_phase2=True, debug=(), n_cores=8, chunk_upto=99):
    nc = bass.Bass("TRN2", target_bir_lowering=False, detect_race_conditions=False)

    hx = nc.dram_tensor("x", [S, DM], F32, kind="ExternalInput").ap()
    hw = {w: nc.dram_tensor(w, [8, 128, DM], F32, kind="ExternalInput").ap()
          for w in ("wq", "wk", "wv", "wg", "wo", "wffn")}
    hw["wa"] = nc.dram_tensor("wa", [8, 128, H], F32, kind="ExternalInput").ap()
    hw["wb"] = nc.dram_tensor("wb", [8, 128, H], F32, kind="ExternalInput").ap()
    hbn = ("bq", "bk", "bv", "bg", "gns", "gnb")
    hb = {b: nc.dram_tensor(b, [128, 8], F32, kind="ExternalInput").ap() for b in hbn}
    hb["bo"] = nc.dram_tensor("bo", [1, DM], F32, kind="ExternalInput").ap()
    hb["bffn"] = nc.dram_tensor("bffn", [1, DM], F32, kind="ExternalInput").ap()
    hb["ba"] = nc.dram_tensor("ba", [H, 1], F32, kind="ExternalInput").ap()
    hb["bb"] = nc.dram_tensor("bb", [H, 1], F32, kind="ExternalInput").ap()
    hconv = nc.dram_tensor("convw", [128, 8 * 3 * 4], F32, kind="ExternalInput").ap()
    hc = {}
    for k, v in CONSTS.items():
        dt = BF16 if v.dtype == ml_dtypes.bfloat16 else F32
        hc[k] = nc.dram_tensor("c_" + k, list(v.shape), dt, kind="ExternalInput").ap()

    hy = nc.dram_tensor("y", [S, DM], F32, kind="ExternalOutput").ap()
    hxnf = nc.dram_tensor("xnf_sc", [NT, 128, 8 * TILE], F32, kind="Internal").ap()
    houts = nc.dram_tensor("outs_sc", [S, DM], BF16, kind="Internal").ap()
    hwt = nc.dram_tensor("wt_sc", [8, 128, DM], F32, kind="Internal").ap()
    hrow = nc.dram_tensor("rowbounce", [4, 128], F32, kind="Internal").ap()
    hlhl = nc.dram_tensor("lhl_sc", [48, TILE], BF16, kind="Internal").ap()
    hwam = nc.dram_tensor("wam", [1, 1], F32, kind="Internal").ap()
    hccin = nc.dram_tensor("cc_in", [1, 1], F32, kind="Internal").ap()
    hccout = nc.dram_tensor("cc_out", [1, 1], F32, kind="Internal").ap()

    dbg = {}
    def mkdbg(name, shape, dt=F32):
        if name in debug:
            dbg[name] = nc.dram_tensor("dbg_" + name, list(shape), dt,
                                       kind="ExternalOutput").ap()
    mkdbg("xnf", [128, 8 * TILE])
    mkdbg("kfn", [128, 8 * TILE], BF16)
    mkdbg("qfn", [128, 8 * TILE], BF16)
    mkdbg("vf", [128, 8 * TILE], BF16)
    mkdbg("khf", [128, 8 * TILE], BF16)
    mkdbg("ltile", [H, TILE])
    mkdbg("ee", [128, H * C], BF16)
    mkdbg("ptraw", [128, H * C], BF16)
    mkdbg("gt", [128, H * C], BF16)
    mkdbg("tfin", [128, H * B], BF16)
    mkdbg("uu", [128, H * HD], BF16)
    mkdbg("outs0", [128, DM], BF16)
    mkdbg("ssf", [128, H * HD])
    mkdbg("gnmul", [128, 8])
    mkdbg("gnadd", [128, 8])
    mkdbg("aval", [1, 4])

    eso = ExitStack()
    so = lambda n, s, dt=F32: eso.enter_context(nc.sbuf_tensor(n, s, dt))

    sc = {}
    for k, v in CONSTS.items():
        dt = BF16 if v.dtype == ml_dtypes.bfloat16 else F32
        sc[k] = so("s_" + k, list(v.shape), dt)
    sbias = {k: so("sb_" + k, [128, 8]) for k in hbn}
    sbrow = so("sbrow", [33, DM])
    sbrow16 = so("sbrow16", [33, DM], BF16)
    sba = so("sb_ba", [H, 1])
    sbb = so("sb_bb", [H, 1])
    sconv = so("sb_convw", [128, 8 * 3 * 4])
    amaxc = so("amaxc", [128, 40])
    srow = so("srow", [4, 128])
    sval = so("sval", [4, 12])
    sxcol = so("sxcol", [128, 4])
    gnmul = so("gnmul", [128, 8])
    gnadd = so("gnadd", [128, 8])
    gnc = so("gnc", [32, 64])
    gnr = so("gnr", [16, 8])
    vbr = so("vbr", [16, 8])
    vbm = so("vbm", [16, 8])
    INDPF = so("indpf_t", [16, 128])
    identf = so("identf", [128, 128])

    pa = eso.enter_context(nc.psum_tensor("pa", [128, 1024], F32))
    pb = eso.enter_context(nc.psum_tensor("pb", [128, 1024], F32))
    pc = eso.enter_context(nc.psum_tensor("pc", [128, 512], F32))
    pstat = eso.enter_context(nc.psum_tensor("pstat", [128, 512], F32))
    pt8 = eso.enter_context(nc.psum_tensor("pt8", [128, 2048], BF16))

    sems = {k: eso.enter_context(nc.semaphore("sem_" + k))
            for k in ("pe", "dve", "act", "gps", "dma", "gdma")}
    nc.const_aps.register(F32, 1e-5, sc["CVALS"][:, 0:1])
    nc.const_aps.register(F32, -1.0, sc["CVALS"][:, 1:2])

    esi = ExitStack()
    si = lambda n, s, dt=F32: esi.enter_context(nc.sbuf_tensor(n, s, dt))
    swq = si("swq", [128, 8 * DM], BF16)
    swk = si("swk", [128, 8 * DM], BF16)
    swv = si("swv", [128, 8 * DM], BF16)
    swa = si("swa", [128, 8 * H], BF16)
    swb = si("swb", [128, 8 * H], BF16)
    sdiag = si("sdiag", [128, 64 * 128], BF16)
    vcv = si("vcv", [128, 512], BF16)
    xts = si("xts", [128, DM])
    lntmp = si("lntmp", [128, DM])
    xnfs = xts
    xnfb = si("xnfb", [128, 8 * TILE], BF16)
    lncol = si("lncol", [128, 8])
    pre = si("pre", [128, 8 * P3], BF16)
    halo = si("halo", [128, 3 * 8 * 3], BF16)
    qfn = si("qfn", [128, 8 * TILE], BF16)
    kfn = si("kfn", [128, 8 * TILE], BF16)
    khfc = si("khfc", [128, 8 * C], BF16)
    khfcE = si("khfcE", [128, 8 * C], BF16)
    khfcO = si("khfcO", [128, 8 * C], BF16)
    vf = si("vf", [128, 8 * TILE], BF16)
    ksq = si("ksq", [128, TILE], BF16)
    nsqp = si("nsqp", [48, TILE])
    nsqq = nsqp[0:16, :]; nsqk = nsqp[32:48, :]
    negl = nsqp[0:16, :]
    rnq = si("rnq", [16, TILE], BF16)
    rnk = si("rnk", [16, TILE], BF16)
    lap = si("lap", [112, TILE])
    ltile = lap[0:16, :]
    atile = lap[96:112, :]
    abti = si("abti", [16, C], BF16)
    btile = si("btile", [16, TILE], BF16)
    lhl = si("lhl", [48, TILE], BF16)
    lhi = lhl[0:16, :]; llo = lhl[32:48, :]
    nlhi = si("nlhi", [16, TILE], BF16)
    nllo = si("nllo", [16, TILE], BF16)
    lfl2 = si("lfl2", [2, 2 * H * C], BF16)
    eet = si("eet", [128, H * C], BF16)
    uu = eet[:, 0:H * HD]
    eegh = si("eegh", [128, 1024], BF16)
    rcc = si("rcc", [128, 16], BF16)
    ptraw = si("ptraw", [128, H * C], BF16)
    ptod = ptraw
    gtt = si("gtt", [128, H * C], BF16)
    dd = None
    ddt = si("ddt", [128, H * B], BF16)
    dp = {1: si("dpow1", [128, H * B], BF16), 2: si("dpow2", [128, H * B], BF16),
          16: si("dpow16", [128, H * B], BF16)}
    dd = dp[16]
    dtp = {1: ddt, 2: si("dt2", [128, H * B], BF16),
           4: si("dt4", [128, H * B], BF16), 8: si("dt8", [128, H * B], BF16)}
    ggl = si("ggl", [128, H * B], BF16)
    ggr = si("ggr", [128, H * B], BF16)
    txx = dp[16]
    tfin = ggr

    kqs0 = si("kqs0", [128, H * HD], BF16)
    vtm = si("vtm", [128, H * HD], BF16)
    khtm = si("khtm", [128, H * HD], BF16)
    rhs_ = si("rhs_", [128, H * HD], BF16)
    preb = kqs0

    upb = vtm
    ssf = si("ssf", [128, H * HD])
    ssbE = si("ssbE", [128, H * HD], BF16)
    ssbO = si("ssbO", [128, H * HD], BF16)
    sst = lntmp
    outst = si("outst", [128, DM], BF16)
    acdg = si("acdg", [16, 16], BF16)
    wtf = lntmp

    def head_rb(h):
        return 64 * (h % 2)

    def fchunk(t, h, cc, w=C):
        fb = h // 2
        rb = head_rb(h)
        return t[rb:rb + 64, fb * TILE + cc * C: fb * TILE + cc * C + w]

    with eso, esi:
        with nc.Block() as block:
            p = Prog(nc, block, sems)

            def ld_consts(e):
                ins = []
                for k in sc:
                    t_ = sc[k]
                    sl = tuple(slice(0, d) for d in t_.shape)
                    ins.append(e.dma_start(out=t_[sl], in_=hc[k][sl]))
                for k in sbias:
                    ins.append(e.dma_start(out=sbias[k][:, :], in_=hb[k][:, :]))
                ins.append(e.dma_start(out=sbrow[0:1, :], in_=hb["bo"][:, :]))
                ins.append(e.dma_start(out=sbrow[32:33, :], in_=hb["bffn"][:, :]))
                ins.append(e.dma_start(out=sba[:, :], in_=hb["ba"][:, :]))
                ins.append(e.dma_start(out=sbb[:, :], in_=hb["bb"][:, :]))
                ins.append(e.dma_start(out=sconv[:, :], in_=hconv[:, :]))
                return ins
            p.sec("dma", ld_consts, wr=["consts"], n_dma=len(sc) + len(sbias) + 5)

            def ld_w(e):
                ins = []
                for w, dst in (("wq", swq), ("wk", swk), ("wv", swv)):
                    for cc in range(8):
                        ins.append(e.dma_start(out=dst[:, cc * DM:(cc + 1) * DM],
                                               in_=hw[w][cc, :, :]))
                for w, dst in (("wa", swa), ("wb", swb)):
                    for cc in range(8):
                        ins.append(e.dma_start(out=dst[:, cc * H:(cc + 1) * H],
                                               in_=hw[w][cc, :, :]))
                return ins
            p.sec("gdma", ld_w, wr=["w1"], n_dma=3 * 8 + 2 * 8)

            def mk_init(e):
                e.memset(eet[:, :], 0.0)
                e.memset(ssf[:, :], 0.0)
                e.memset(ssbE[:, :], 0.0)
                e.memset(ssbO[:, :], 0.0)
                e.memset(halo[:, :], 0.0)
                e.memset(amaxc[:, :], 0.0)
                e.memset(sval[:, :], 0.0)
                e.memset(gnc[:, :], 0.0)
                e.memset(srow[:, :], 0.0)
                return e.tensor_copy(out=identf[:, :], in_=sc["IDENT128"][:, :])
            p.sec("dve", mk_init, rd=["consts"],
                  wr=["uu", "ssf", "ssb", "halo", "amaxc", "sval", "gnc", "srow", "identf"])

            def mk_diag(e):
                last = None
                for idx in range(64):
                    last = e.tensor_scalar(
                        out=sdiag[:, idx * 128:(idx + 1) * 128],
                        in0=sc["IDENT128"][:, :],
                        scalar1=sconv[:, idx:idx + 1],
                        scalar2=None, op0=ALU.mult)
                return last
            p.sec("gps", mk_diag, rd=["consts"], wr=["sdiag"])

            for cc in range(8):
                wbuf, wkey = (wtf, "lntmp") if cc % 2 == 0 else (xnfs, "xts")
                pw_, pwk = (pa, "pa") if cc % 2 == 0 else (pb, "pb")

                def ld_wf(e, cc=cc, wbuf=wbuf):
                    ins = []
                    for oc in range(8):
                        ins.append(e.dma_start(
                            out=wbuf[:, oc * 128:(oc + 1) * 128],
                            in_=hw["wffn"][oc, :, cc * 128:(cc + 1) * 128]))
                    return ins
                p.sec("dma", ld_wf, wr=[wkey], n_dma=8)

                def tp_wf(e, wbuf=wbuf, pw_=pw_):
                    last = None
                    for oc in range(8):
                        last = e.transpose(pw_[:, oc * 128:(oc + 1) * 128],
                                           wbuf[:, oc * 128:(oc + 1) * 128],
                                           identf[:, :])
                    return last
                p.sec("pe", tp_wf, rd=[wkey, "identf"], wr=[pwk])

                def ev_wf1(e, cc=cc, wbuf=wbuf, pw_=pw_):
                    return e.tensor_copy(out=wbuf[:, :], in_=pw_[:, :])
                p.sec("dve", ev_wf1, rd=[pwk], wr=[wkey])

                def ev_wf2(e, cc=cc, wbuf=wbuf):
                    return e.reduce_max(out=amaxc[:, 34 + (cc % 2) * 4:35 + (cc % 2) * 4],
                                        in_=wbuf[:, :], axis=AX.X,
                                        apply_absolute_value=True)
                p.sec("dve", ev_wf2, rd=[wkey], wr=["amaxw_t%d" % (cc % 2)])

                def ev_wf3(e, cc=cc):
                    return e.tensor_tensor(out=amaxc[:, 35:36], in0=amaxc[:, 35:36],
                                           in1=amaxc[:, 34 + (cc % 2) * 4:35 + (cc % 2) * 4],
                                           op=ALU.max)
                p.sec("dve", ev_wf3, rd=["amaxw_t%d" % (cc % 2), "amaxc"], wr=["amaxc"])

                def st_wf(e, cc=cc, wbuf=wbuf):
                    return e.dma_start(out=hwt[cc, :, :], in_=wbuf[:, :])
                p.sec("dma", st_wf, rd=[wkey], wr=["hwt_%d" % cc], n_dma=1)

            stats_on = [False]

            for T in range(n_tiles):
                for s_ in range(NSUB):
                    def ld_x(e, T=T, s_=s_):
                        return e.dma_start(
                            out=xts[:, :],
                            in_=hx[T * TILE + s_ * 128: T * TILE + (s_ + 1) * 128, :])
                    p.sec("dma", ld_x, wr=["xts"], n_dma=1)

                    def ln_a1(e):
                        return e.reduce_sum(out=lncol[:, 0:1], in_=xts[:, :], axis=AX.X)
                    p.sec("dve", ln_a1, rd=["xts"], wr=["lncol_a"])

                    def ln_a2(e):
                        return e.tensor_scalar(out=lncol[:, 1:2], in0=lncol[:, 0:1],
                                               scalar1=1.0 / DM, scalar2=None, op0=ALU.mult)
                    p.sec("dve", ln_a2, rd=["lncol_a"], wr=["lncol"])

                    def ln_b(e):
                        return e.activation(out=lntmp[:, :], in_=xts[:, :], func=AF.Square,
                                            accum_out=lncol[:, 2:3])
                    p.sec("act", ln_b, rd=["xts"], wr=["lntmp", "lncol2"])

                    def ln_c1(e):
                        return e.tensor_scalar(out=lncol[:, 3:4], in0=lncol[:, 1:2],
                                               scalar1=lncol[:, 1:2], scalar2=None, op0=ALU.mult)
                    p.sec("dve", ln_c1, rd=["lncol"], wr=["lncol_c"])

                    def ln_c2(e):
                        return e.tensor_scalar(out=lncol[:, 4:5], in0=lncol[:, 2:3],
                                               scalar1=1.0 / DM, scalar2=lncol[:, 3:4],
                                               op0=ALU.mult, op1=ALU.subtract)
                    p.sec("dve", ln_c2, rd=["lncol_c", "lncol2"], wr=["lncol"])

                    def ln_d(e):
                        return e.activation(out=lncol[:, 5:6], in_=lncol[:, 4:5],
                                            func=AF.Sqrt, bias=1e-5)
                    p.sec("act", ln_d, rd=["lncol"], wr=["lncol3"])

                    def ln_e1(e):
                        return e.reciprocal(out=lncol[:, 6:7], in_=lncol[:, 5:6])
                    p.sec("dve", ln_e1, rd=["lncol3"], wr=["lncol_e"])

                    def ln_e2(e):
                        return e.tensor_scalar(out=lntmp[:, :], in0=xts[:, :],
                                               scalar1=lncol[:, 1:2], scalar2=lncol[:, 6:7],
                                               op0=ALU.subtract, op1=ALU.mult)
                    p.sec("dve", ln_e2, rd=["xts", "lncol", "lncol_e"], wr=["lntmp"])

                    def ln_e3(e, T=T, s_=s_):
                        return e.reduce_max(out=amaxc[:, T * NSUB + s_:T * NSUB + s_ + 1],
                                            in_=lntmp[:, :], axis=AX.X,
                                            apply_absolute_value=True)
                    p.sec("dve", ln_e3, rd=["lntmp"], wr=["amaxc"])

                    psx, pkx = (pa, "pa") if s_ % 2 == 0 else (pb, "pb")

                    def tp_x(e, psx=psx):
                        last = None
                        for fb in range(8):
                            last = e.transpose(psx[:, fb * 128:(fb + 1) * 128],
                                               lntmp[:, fb * 128:(fb + 1) * 128],
                                               identf[:, :])
                        return last
                    p.sec("pe", tp_x, rd=["lntmp", "identf"], wr=[pkx])

                    def ev_f(e, psx=psx):
                        return e.tensor_copy(out=xnfs[:, :], in_=psx[:, :])
                    p.sec("dve", ev_f, rd=[pkx], wr=["xts"])

                    def ev_b(e, s_=s_, psx=psx):
                        ap = xnfb[:, :].rearrange("p (fb t) -> p fb t", fb=8)[:, :, s_ * 128:(s_ + 1) * 128]
                        return e.tensor_copy(out=ap,
                                             in_=psx[:, :].rearrange("p (fb t) -> p fb t", fb=8))
                    p.sec("dve", ev_b, rd=[pkx], wr=["xnfb_%d" % s_])

                    def st_xnf(e, T=T, s_=s_):
                        ap = hxnf[T, :, :].rearrange("p (fb t) -> p fb t", fb=8)[:, :, s_ * 128:(s_ + 1) * 128]
                        return e.dma_start(out=ap, in_=xnfs[:, :].rearrange("p (fb t) -> p fb t", fb=8))
                    p.sec("dma", st_xnf, rd=["xts"], wr=["hxnf_%d_%d" % (T, s_)], n_dma=1)

                xnfb_r = ["xnfb_%d" % s_ for s_ in range(NSUB)]
                if "xnf" in dbg and T == 0:
                    p.sec("dma", lambda e: e.dma_start(out=dbg["xnf"][:, :], in_=hxnf[0, :, :]),
                          rd=["hxnf_0_0", "hxnf_0_1", "hxnf_0_2", "hxnf_0_3"], n_dma=1)

                for pi, (pname, wten, bkey, dstF) in enumerate(
                        (("q", swq, "bq", qfn), ("k", swk, "bk", kfn), ("v", swv, "bv", vf))):
                    for fbp in range(4):
                        ps_, pk_ = (pa, "pa") if fbp % 2 == 0 else (pb, "pb")

                        def mm_p(e, wten=wten, fbp=fbp, ps_=ps_):
                            last = None
                            for half in range(2):
                                fb = fbp * 2 + half
                                for cc in range(8):
                                    last = e.matmul(
                                        ps_[:, half * 512:(half + 1) * 512],
                                        wten[:, cc * DM + fb * 128: cc * DM + (fb + 1) * 128],
                                        xnfb[:, cc * 512:(cc + 1) * 512],
                                        start=(cc == 0), stop=(cc == 7))
                            return last
                        p.sec("pe", mm_p, rd=xnfb_r + ["w1"], wr=[pk_])

                        def ev_p(e, bkey=bkey, fbp=fbp, ps_=ps_):
                            last = None
                            for half in range(2):
                                fb = fbp * 2 + half
                                last = e.tensor_scalar(
                                    out=pre[:, fb * P3 + 3: fb * P3 + 3 + TILE],
                                    in0=ps_[:, half * 512:(half + 1) * 512],
                                    scalar1=sbias[bkey][:, fb:fb + 1], scalar2=None,
                                    op0=ALU.add)
                            return last
                        p.sec("dve", ev_p, rd=[pk_, "consts"], wr=["pre_%d" % fbp])

                    def h_in(e, pi=pi):
                        ap = pre[:, :].rearrange("p (fb t) -> p fb t", fb=8)[:, :, 0:3]
                        hap = halo[:, :].rearrange("p (pr fbt) -> p pr fbt", pr=3)[:, pi]\
                            .rearrange("p (fb t) -> p fb t", fb=8)
                        return e.tensor_copy(out=ap, in_=hap)
                    p.sec("dve", h_in, rd=["halo"], wr=["prehalo"],
                          extra=[p.res["pre_%d" % f]["w"] for f in range(4)])

                    if pi < 2:
                        for fbp in range(4):
                            ps_, pk_ = (pb, "pb") if fbp % 2 == 0 else (pa, "pa")

                            def mm_c(e, pi=pi, fbp=fbp, ps_=ps_):
                                last = None
                                for half in range(2):
                                    fb = fbp * 2 + half
                                    for tap in range(4):
                                        idx = (pi * 8 + fb) * 4 + tap
                                        last = e.matmul(
                                            ps_[:, half * 512:(half + 1) * 512],
                                            sdiag[:, idx * 128:(idx + 1) * 128],
                                            pre[:, fb * P3 + tap: fb * P3 + tap + TILE],
                                            start=(tap == 0), stop=(tap == 3))
                                return last
                            p.sec("pe", mm_c, rd=["prehalo", "sdiag", "pre_%d" % fbp], wr=[pk_])

                            def ev_c(e, dstF=dstF, fbp=fbp, ps_=ps_):
                                last = None
                                for half in range(2):
                                    fb = fbp * 2 + half
                                    last = e.activation(out=dstF[:, fb * 512:(fb + 1) * 512],
                                                        in_=ps_[:, half * 512:(half + 1) * 512],
                                                        func=AF.Silu)
                                return last
                            p.sec("act", ev_c, rd=[pk_], wr=["F%d_%d" % (pi, fbp)])
                    else:
                        for fb in range(8):
                            def cv_v0(e, fb=fb):
                                base = (2 * 8 + fb) * 4
                                return e.tensor_scalar(
                                    out=vcv[:, :],
                                    in0=pre[:, fb * P3 + 0: fb * P3 + TILE],
                                    scalar1=sconv[:, base:base + 1],
                                    scalar2=None, op0=ALU.mult)
                            p.sec("dve", cv_v0, rd=["prehalo", "consts", "pre_%d" % (fb // 2)],
                                  wr=["vcv"])
                            for tap in range(1, 4):
                                def cv_vt(e, fb=fb, tap=tap):
                                    base = (2 * 8 + fb) * 4
                                    return e.scalar_tensor_tensor(
                                        out=vcv[:, :],
                                        in0=pre[:, fb * P3 + tap: fb * P3 + tap + TILE],
                                        scalar=sconv[:, base + tap:base + tap + 1],
                                        in1=vcv[:, :], op0=ALU.mult, op1=ALU.add)
                                p.sec("dve", cv_vt,
                                      rd=["prehalo", "consts", "pre_%d" % (fb // 2), "vcv"],
                                      wr=["vcv"])

                            def sl_v(e, dstF=dstF, fb=fb):
                                return e.activation(out=dstF[:, fb * 512:(fb + 1) * 512],
                                                    in_=vcv[:, :], func=AF.Silu)
                            p.sec("act", sl_v, rd=["vcv"], wr=["F%d_%d" % (pi, fb // 2)])

                    def h_out(e, pi=pi):
                        ap = pre[:, :].rearrange("p (fb t) -> p fb t", fb=8)[:, :, TILE:TILE + 3]
                        hap = halo[:, :].rearrange("p (pr fbt) -> p pr fbt", pr=3)[:, pi]\
                            .rearrange("p (fb t) -> p fb t", fb=8)
                        return e.tensor_copy(out=hap, in_=ap)
                    p.sec("dve", h_out, rd=["pre_0", "pre_1", "pre_2", "pre_3"], wr=["halo"])

                    if pi < 2:
                        nsqt = nsqq if pi == 0 else nsqk
                        for fb in range(8):
                            def sq_f(e, dstF=dstF, fb=fb):
                                return e.tensor_mul(ksq[:, :], dstF[:, fb * 512:(fb + 1) * 512],
                                                    dstF[:, fb * 512:(fb + 1) * 512])
                            p.sec("gps", sq_f, rd=["F%d_%d" % (pi, fb // 2)], wr=["ksq"])

                            def mm_n(e, fb=fb):
                                return e.matmul(pc[0:16, 0:512], sc["IND2"][:, fb * H:(fb + 1) * H],
                                                ksq[:, :], start=(fb == 0), stop=(fb == 7))
                            p.sec("pe", mm_n, rd=["ksq", "consts"],
                                  wr=["pc"] if fb == 7 else [])

                        def ev_n(e, nsqt=nsqt):
                            return e.activation(out=nsqt[:, :], in_=pc[0:16, 0:512], func=AF.Sqrt)
                        p.sec("act", ev_n, rd=["pc"], wr=["nsq_%d" % pi])

                def rec_n1(e):
                    e.tensor_scalar(out=nsqq[:, :], in0=nsqq[:, :], scalar1=1e-6,
                                    scalar2=None, op0=ALU.add)
                    return e.tensor_scalar(out=nsqk[:, :], in0=nsqk[:, :], scalar1=1e-6,
                                           scalar2=None, op0=ALU.add)
                p.sec("dve", rec_n1, rd=["nsq_0", "nsq_1"], wr=["rn_a"])

                def rec_n2(e):
                    e.reciprocal(out=nsqq[:, :], in_=nsqq[:, :])
                    return e.reciprocal(out=nsqk[:, :], in_=nsqk[:, :])
                p.sec("dve", rec_n2, rd=["rn_a"], wr=["rn_b"])

                def rec_n3(e):
                    e.tensor_copy(out=rnq[:, :], in_=nsqq[:, :])
                    return e.tensor_copy(out=rnk[:, :], in_=nsqk[:, :])
                p.sec("dve", rec_n3, rd=["rn_b"], wr=["rn"])

                for ji, (srcT, dstT, vec) in enumerate(((qfn, qfn, rnq), (kfn, kfn, rnk))):
                    for fbp in range(4):
                        def mm_r(e, vec=vec, fbp=fbp):
                            last = None
                            for half in range(2):
                                fb = fbp * 2 + half
                                last = e.matmul(pa[:, half * 512:(half + 1) * 512],
                                                sc["INDT"][:, fb * 128:(fb + 1) * 128],
                                                vec[:, :], start=True, stop=True)
                            return last
                        p.sec("pe", mm_r, rd=["rn", "consts"], wr=["pa"])

                        def ap_r(e, srcT=srcT, dstT=dstT, fbp=fbp):
                            last = None
                            for half in range(2):
                                fb = fbp * 2 + half
                                last = e.tensor_tensor(out=dstT[:, fb * 512:(fb + 1) * 512],
                                                       in0=srcT[:, fb * 512:(fb + 1) * 512],
                                                       in1=pa[:, half * 512:(half + 1) * 512],
                                                       op=ALU.mult)
                            return last
                        wkey = ("Fq_n" if ji == 0 else "Fk_n") + "_%d" % fbp
                        rkey = ["F0_%d" % fbp] if ji == 0 else ["F1_%d" % fbp]
                        p.sec("dve", ap_r, rd=["pa"] + rkey, wr=[wkey])
                qfn_r = ["Fq_n_%d" % f for f in range(4)]
                kfn_r = ["Fk_n_%d" % f for f in range(4)]
                khf_r = ["khfc"]
                vf_r = ["F2_%d" % f for f in range(4)]

                def mm_ab(e):
                    last = None
                    for cc in range(8):
                        e.matmul(pb[0:16, 0:512], swa[:, cc * H:(cc + 1) * H],
                                 xnfb[:, cc * 512:(cc + 1) * 512],
                                 start=(cc == 0), stop=(cc == 7))
                    for cc in range(8):
                        last = e.matmul(pb[0:16, 512:1024], swb[:, cc * H:(cc + 1) * H],
                                        xnfb[:, cc * 512:(cc + 1) * 512],
                                        start=(cc == 0), stop=(cc == 7))
                    return last
                p.sec("pe", mm_ab, rd=xnfb_r + ["w1"], wr=["pb"])

                def ev_ab(e):
                    e.activation(out=atile[:, :], in_=pb[0:16, 0:512], func=AF.Sigmoid,
                                 bias=sba[:, 0:1])
                    return e.activation(out=btile[:, :], in_=pb[0:16, 512:1024],
                                        func=AF.Sigmoid, bias=sbb[:, 0:1])
                p.sec("act", ev_ab, rd=["pb", "consts"], wr=["atile", "btile"])

                def ev_ab2(e):
                    return e.activation(out=ltile[:, :], in_=atile[:, :], func=AF.Ln)
                p.sec("act", ev_ab2, rd=["atile"], wr=["ltile"])

                def cum_l1(e):
                    return e.tensor_tensor_scan(out=ltile[:, :], data0=ltile[:, :],
                                                data1=ltile[:, :], initial=0.0,
                                                op0=ALU.add, op1=ALU.min)
                p.sec("dve", cum_l1, rd=["ltile", "rn"], wr=["ltile"])

                def cum_l2(e):
                    e.tensor_scalar(out=negl[:, :], in0=ltile[:, :], scalar1=-1.0,
                                    scalar2=None, op0=ALU.mult)
                    return e.tensor_copy(out=lhi[:, :], in_=ltile[:, :])
                p.sec("dve", cum_l2, rd=["ltile"], wr=["negl", "lhi"])

                def cum_l3(e):
                    e.tensor_scalar(out=nlhi[:, :], in0=lhi[:, :], scalar1=-1.0,
                                    scalar2=None, op0=ALU.mult)
                    return e.tensor_tensor(out=sst[0:16, 0:TILE], in0=ltile[:, :],
                                           in1=lhi[:, :], op=ALU.subtract)
                p.sec("dve", cum_l3, rd=["ltile", "lhi"], wr=["nlhi_r", "lntmp"])

                def cum_l4(e):
                    return e.tensor_copy(out=llo[:, :], in_=sst[0:16, 0:TILE])
                p.sec("dve", cum_l4, rd=["lntmp"], wr=["llo_r"])

                def cum_l5(e):
                    return e.tensor_scalar(out=nllo[:, :], in0=llo[:, :], scalar1=-1.0,
                                           scalar2=None, op0=ALU.mult)
                p.sec("dve", cum_l5, rd=["llo_r"], wr=["lhl"],
                      extra=[p.res["nlhi_r"]["w"]])
                if "ltile" in dbg and T == 0:
                    p.sec("dma", lambda e: e.dma_start(out=dbg["ltile"][:, :], in_=ltile[:, :]),
                          rd=["ltile"], n_dma=1)

                def st_lhl(e):
                    a_ = e.dma_start(out=hlhl[0:16, :], in_=lhl[0:16, :])
                    b_ = e.dma_start(out=hlhl[32:48, :], in_=lhl[32:48, :])
                    return [a_, b_]
                p.sec("dma", st_lhl, rd=["lhl", "llo_r", "nlhi_r", "lhi"], wr=["hlhl"], n_dma=2)


                for nm, t_, rr in (("qfn", qfn, qfn_r), ("kfn", kfn, kfn_r),
                                   ("vf", vf, vf_r)):
                    if nm in dbg and T == 0:
                        p.sec("dma", (lambda t_, nm: lambda e: e.dma_start(out=dbg[nm][:, :], in_=t_[:, :]))(t_, nm),
                              rd=rr, n_dma=1)

                if not do_attn:
                    continue
                for cc in range(NCH):
                    p.stage_limit = chunk_upto
                    first_chunk = (T == 0 and cc == 0)
                    qr = qfn_r + kfn_r + khf_r + vf_r + ["ltile", "negl", "lflat"]

                    def mk_a(e, cc=cc):
                        if cc > 0:
                            return e.activation(out=atile[:, 0:C],
                                                in_=ltile[:, cc * C:(cc + 1) * C],
                                                func=AF.Exp, bias=negl[:, cc * C - 1:cc * C])
                        return e.activation(out=atile[:, 0:C], in_=ltile[:, 0:C], func=AF.Exp)
                    p.sec("act", mk_a, rd=["ltile", "negl"], wr=["achk_a"])

                    def mk_a2(e):
                        return e.tensor_copy(out=abti[:, :], in_=atile[:, 0:C])
                    p.sec("dve", mk_a2, rd=["achk_a"], wr=["achk"])

                    p.stage_cur = 1
                    if cc % 2 == 0:
                        def ld_lfl(e, cc=cc):
                            h4 = hlhl.rearrange("(o h) (c t) -> o h c t", o=3, c=NCH)
                            a_ = e.dma_start(
                                out=lfl2[0:1, :].rearrange("o (c h t) -> o c h t", c=2, h=16),
                                in_=h4[0:1, 0:16, cc:cc + 2, :].rearrange("o h c t -> o c h t"),
                                allow_slow_non_contiguous=True)
                            b_ = e.dma_start(
                                out=lfl2[1:2, :].rearrange("o (c h t) -> o c h t", c=2, h=16),
                                in_=h4[2:3, 0:16, cc:cc + 2, :].rearrange("o h c t -> o c h t"),
                                allow_slow_non_contiguous=True)
                            return [a_, b_]
                        p.sec("gdma", ld_lfl, rd=["hlhl"], wr=["lflat"], n_dma=2)
                    def mm_bhrep(e, cc=cc):
                        last = None
                        for fb in range(8):
                            last = e.matmul(pb[:, fb * 128:(fb + 1) * 128],
                                            sc["INDT"][:, fb * 128:(fb + 1) * 128],
                                            btile[:, cc * C:(cc + 1) * C],
                                            start=True, stop=True)
                        return last
                    p.sec("pe", mm_bhrep, rd=["btile", "consts"] + kfn_r, wr=["pb"])

                    def mk_khfc(e, cc=cc):
                        return e.tensor_tensor(
                            out=khfc[:, :],
                            in0=kfn[:, :].rearrange("p (fb t) -> p fb t", fb=8)[:, :, cc * C:(cc + 1) * C],
                            in1=pb[:, :].rearrange("p (fb t) -> p fb t", fb=8),
                            op=ALU.mult)
                    p.sec("dve", mk_khfc, rd=["pb"] + kfn_r, wr=["khfc"])

                    def mk_khfceo(e):
                        e.tensor_scalar(out=khfcE[:, :], in0=khfc[:, :],
                                        scalar1=sc["CVALS"][:, 2:3], scalar2=None,
                                        op0=ALU.mult)
                        return e.tensor_scalar(out=khfcO[:, :], in0=khfc[:, :],
                                               scalar1=sc["CVALS"][:, 3:4], scalar2=None,
                                               op0=ALU.mult)
                    p.sec("dve", mk_khfceo, rd=["khfc", "consts"], wr=["khfc2"])

                    def khsl(h, w=C):
                        rb = head_rb(h)
                        return khfc[rb:rb + 64, (h // 2) * 128:(h // 2) * 128 + w]

                    p.stage_cur = 2
                    def mm_ks0(e, cc=cc):
                        last = None
                        for h in range(H):
                            fb = h // 2
                            ssx = ssbE if h % 2 == 0 else ssbO
                            last = e.matmul(pa[:, h * HD:(h + 1) * HD],
                                            kfn[:, fb * TILE + cc * C: fb * TILE + cc * C + C],
                                            ssx[:, h * HD:(h + 1) * HD],
                                            start=True, stop=True)
                        return last
                    p.sec("pe", mm_ks0, rd=kfn_r + ["ssb"], wr=["pa"])

                    def ev_ks0(e):
                        return e.tensor_copy(out=kqs0[:, :], in_=pa[:, :])
                    p.sec("dve", ev_ks0, rd=["pa"], wr=["kqs0"])

                    def mm_qs0(e, cc=cc):
                        last = None
                        for h in range(H):
                            fb = h // 2
                            ssx = ssbE if h % 2 == 0 else ssbO
                            last = e.matmul(pb[:, h * HD:(h + 1) * HD],
                                            qfn[:, fb * TILE + cc * C: fb * TILE + cc * C + C],
                                            ssx[:, h * HD:(h + 1) * HD],
                                            start=True, stop=True)
                        return last
                    p.sec("pe", mm_qs0, rd=qfn_r + ["ssb"], wr=["pb"])

                    p.stage_cur = 3
                    def mm_arep2(e):
                        e.matmul(pa[:, 0:512], abti[:, :], sc["IND64"][:, 0:512],
                                 start=True, stop=True)
                        return e.matmul(pa[:, 512:1024], abti[:, :], sc["IND64"][:, 512:1024],
                                        start=True, stop=True)
                    p.sec("pe", mm_arep2, rd=["achk", "consts", "kqs0"], wr=["pa"])

                    p.stage_cur = 4
                    def tp_vk(e, cc=cc):
                        last = None
                        for fb in range(8):
                            e.transpose(pt8[:, fb * 128:(fb + 1) * 128],
                                        vf[:, fb * TILE + cc * C: fb * TILE + cc * C + C],
                                        sc["IDENT128"][:, :])
                            last = e.transpose(pt8[:, 1024 + fb * 128: 1024 + (fb + 1) * 128],
                                               khfc[:, fb * 128:(fb + 1) * 128],
                                               sc["IDENT128"][:, :])
                        return last
                    p.sec("pe", tp_vk, rd=vf_r + khf_r + ["consts"], wr=["pt8"])

                    def ev_vk(e):
                        e.tensor_copy(out=vtm[:, :], in_=pt8[:, 0:1024])
                        return e.tensor_copy(out=khtm[:, :], in_=pt8[:, 1024:2048])
                    p.sec("dve", ev_vk, rd=["pt8"], wr=["vtm", "khtm"])

                    p.stage_cur = 5
                    def mk_rhs1(e):
                        return e.tensor_tensor(out=sst[:, :], in0=pa[:, :], in1=kqs0[:, :],
                                               op=ALU.mult)
                    p.sec("dve", mk_rhs1, rd=["pa", "kqs0"], wr=["lntmp"])

                    def mk_rhs2(e):
                        return e.tensor_tensor(out=rhs_[:, :], in0=vtm[:, :], in1=sst[:, :],
                                               op=ALU.subtract)
                    p.sec("dve", mk_rhs2, rd=["vtm", "lntmp"], wr=["rhs"])

                    def mk_oa1(e):
                        return e.tensor_copy(out=kqs0[:, :], in_=pb[:, :])
                    p.sec("dve", mk_oa1, rd=["pb"], wr=["kqs0"])

                    def mk_oa2(e):
                        return e.tensor_tensor(out=outst[:, :], in0=pa[:, :], in1=kqs0[:, :],
                                               op=ALU.mult)
                    p.sec("dve", mk_oa2, rd=["pa", "kqs0"], wr=["oa", "outst"])

                    p.stage_cur = 6
                    def mm_ld(e, cc=cc):
                        last = None
                        for gh, ps in ((0, pb), (1, pa)):
                            for q in range(2):
                                off = (cc % 2) * 2048 + gh * 1024 + q * 512
                                tgt = ps[:, q * 512:(q + 1) * 512]
                                e.matmul(tgt, sc["ONES2"][:, :],
                                         lfl2[0:2, off:off + 512], start=True, stop=False)
                                e.matmul(tgt, nlhi[:, cc * C:(cc + 1) * C],
                                         sc["INDBD"][:, gh * 1024 + q * 512: gh * 1024 + (q + 1) * 512],
                                         start=False, stop=False)
                                last = e.matmul(tgt, nllo[:, cc * C:(cc + 1) * C],
                                                sc["INDBD"][:, gh * 1024 + q * 512: gh * 1024 + (q + 1) * 512],
                                                start=False, stop=True)
                        return last
                    p.sec("pe", mm_ld, rd=["lflat", "lhl", "nlhi_r", "consts", "oa"],
                          wr=["pb", "pa"])

                    def ee_a(e):
                        e.tensor_scalar(out=sst[:, :], in0=pb[:, :], scalar1=0.0,
                                        scalar2=None, op0=ALU.min)
                        return e.tensor_scalar(out=xts[:, :], in0=pa[:, :], scalar1=0.0,
                                               scalar2=None, op0=ALU.min)
                    p.sec("dve", ee_a, rd=["pb", "pa"], wr=["lntmp", "xts"])

                    def ee_b(e):
                        e.activation(out=eet[:, 0:1024], in_=sst[:, :], func=AF.Exp)
                        return e.activation(out=eet[:, 1024:2048], in_=xts[:, :],
                                            func=AF.Exp)
                    p.sec("act", ee_b, rd=["lntmp", "xts"], wr=["ee_0", "ee_1"])

                    p.stage_cur = 7
                    def mk_rcc(e):
                        return e.tensor_copy(
                            out=rcc[:, :],
                            in_=eet[:, :].rearrange("p (h t) -> p h t", h=16)[:, :, C - 1:C])
                    p.sec("dve", mk_rcc, rd=["ee_0", "ee_1"], wr=["rcc"])
                    if "ee" in dbg and first_chunk:
                        p.sec("dma", lambda e: e.dma_start(out=dbg["ee"][:, :], in_=eet[:, :]),
                              rd=["ee_0", "ee_1"], n_dma=1)

                    p.stage_cur = 8
                    def mk_eep(e):
                        v3 = eet[:, :].rearrange("p (h t) -> p h t", h=16)
                        return e.tensor_tensor(out=v3, in0=v3,
                                               in1=sc["MASK1"][:, :].rearrange("p (a t) -> p a t", a=1)
                                               .broadcast_to((128, 16, 128)),
                                               op=ALU.mult)
                    p.sec("dve", mk_eep, rd=["rcc", "consts"],
                          wr=["eep", "ee_0", "ee_1"])

                    p.stage_cur = 9
                    def mm_kqx(e, cc=cc, src=kfn):
                        last = None
                        for gh, ps in ((0, pb), (1, pa)):
                            for hh in range(8):
                                h = gh * 8 + hh
                                fb = h // 2
                                khx = khfcE if h % 2 == 0 else khfcO
                                last = e.matmul(ps[:, hh * 128:(hh + 1) * 128],
                                                khx[:, fb * 128:(fb + 1) * 128],
                                                src[:, fb * TILE + cc * C: fb * TILE + cc * C + C],
                                                start=True, stop=True)
                        return last
                    p.sec("pe", lambda e, cc=cc: mm_kqx(e, cc, kfn),
                          rd=["khfc2"] + kfn_r + ["rhs", "lntmp", "xts"],
                          wr=["pb", "pa"])

                    def mk_pt(e):
                        e.tensor_tensor(out=ptraw[:, 0:1024], in0=eet[:, 0:1024],
                                        in1=pb[:, :], op=ALU.mult)
                        return e.tensor_tensor(out=ptraw[:, 1024:2048], in0=eet[:, 1024:2048],
                                               in1=pa[:, :], op=ALU.mult)
                    p.sec("dve", mk_pt, rd=["pb", "pa", "eep"], wr=["ptraw_0", "ptraw_1"])

                    p.sec("pe", lambda e, cc=cc: mm_kqx(e, cc, qfn),
                          rd=["khfc2"] + qfn_r + ["ptraw_0", "ptraw_1"], wr=["pb", "pa"])

                    for gh in range(2):
                        def mk_eeg(e, gh=gh):
                            return e.tensor_tensor(out=eegh[:, :].rearrange("p (h t) -> p h t", h=8),
                                                   in0=eet[:, gh * 1024:(gh + 1) * 1024].rearrange("p (h t) -> p h t", h=8),
                                                   in1=sc["IDT1"][:, :].rearrange("p (a t) -> p a t", a=1)
                                                   .broadcast_to((128, 8, 128)),
                                                   op=ALU.add)
                        p.sec("gps", mk_eeg, rd=["eep", "consts"], wr=["eegh"])

                        def mk_gt(e, gh=gh):
                            return e.tensor_tensor(out=gtt[:, gh * 1024:(gh + 1) * 1024],
                                                   in0=eegh[:, :],
                                                   in1=(pb if gh == 0 else pa)[:, :],
                                                   op=ALU.mult)
                        p.sec("dve", mk_gt, rd=["pb", "pa", "eegh"], wr=["gt_%d" % gh])

                    if "ptraw" in dbg and first_chunk:
                        p.sec("dma", lambda e: e.dma_start(out=dbg["ptraw"][:, :], in_=ptraw[:, :]),
                              rd=["ptraw_0", "ptraw_1"], n_dma=1)
                    if "gt" in dbg and first_chunk:
                        p.sec("dma", lambda e: e.dma_start(out=dbg["gt"][:, :], in_=gtt[:, :]),
                              rd=["gt_0", "gt_1"], n_dma=1)

                    p.stage_cur = 10
                    def mk_dd(e):
                        last = None
                        for i in range(4):
                            last = e.tensor_copy(
                                out=dd[32 * i:32 * i + 32, :].rearrange("p (h t) -> p h t", h=16),
                                in_=ptraw[32 * i:32 * i + 32, :].rearrange("p (h t) -> p h t", h=16)[:, :, 32 * i:32 * i + 32])
                        return last
                    p.sec("dve", mk_dd, rd=["ptraw_0", "ptraw_1"], wr=["dd"])

                    p.stage_cur = 11
                    def mk_ptod(e):
                        v3 = ptod[:, :].rearrange("p (h t) -> p h t", h=16)
                        return e.tensor_tensor(out=v3, in0=v3,
                                               in1=sc["NEGBLK1"][:, :].rearrange("p (a t) -> p a t", a=1)
                                               .broadcast_to((128, 16, 128)),
                                               op=ALU.mult)
                    p.sec("gps", mk_ptod, rd=["ptraw_0", "ptraw_1", "dd", "consts"], wr=["ptod"])

                    p.stage_cur = 12
                    def vtp(e, dst, src):
                        return e.transpose(out=dst[:, :], in_=src[:, :])

                    p.sec("dve", lambda e: vtp(e, ddt, dd), rd=["dd"], wr=["ddt"])

                    sqr_seq = [(dd, ddt, dp[1], dtp[2], "d2"),
                               (dp[1], dtp[2], dp[2], dtp[4], "d4"),
                               (dp[2], dtp[4], dp[1], dtp[8], "d8")]
                    iddb = (sc["IDD1"][:, :].rearrange("p (a t) -> p a t", a=1)
                            .broadcast_to((128, 16, 32)))

                    def subIDD(e, dst, src):
                        return e.tensor_tensor(out=dst[:, :].rearrange("p (h t) -> p h t", h=16),
                                               in0=iddb,
                                               in1=src[:, :].rearrange("p (h t) -> p h t", h=16),
                                               op=ALU.subtract)

                    def mm_sq(e, srcT, src, out):
                        last = None
                        for i in range(4):
                            for h in range(H):
                                last = e.matmul(out[32 * i:32 * i + 32, h * 32:(h + 1) * 32],
                                                srcT[32 * i:32 * i + 32, h * 32:(h + 1) * 32],
                                                src[32 * i:32 * i + 32, h * 32:(h + 1) * 32],
                                                start=True, stop=True,
                                                tile_position=(32 * i, 32 * i))
                        return last
                    mm_prod = mm_sq
                    tAt = eegh[:, 512:1024]
                    p.sec("pe", lambda e: mm_sq(e, ddt, dd, pc), rd=["dd", "ddt"], wr=["pc"])
                    p.sec("dve", lambda e: e.tensor_copy(out=dp[1][:, :], in_=pc[:, :]),
                          rd=["pc"], wr=["d2"])
                    p.sec("dve", lambda e: vtp(e, dtp[2], dp[1]), rd=["d2"], wr=["d2t"])
                    p.sec("gps", lambda e: subIDD(e, ggl, ddt), rd=["ddt", "consts"], wr=["ggl"])
                    p.sec("gps", lambda e: subIDD(e, ggr, dp[1]), rd=["d2", "consts"], wr=["ggr"])
                    p.sec("pe", lambda e: mm_prod(e, ggl, ggr, pb[:, 0:512]),
                          rd=["ggl", "ggr"], wr=["pbA", "pb"])
                    p.sec("dve", lambda e: e.tensor_copy(out=dp[2][:, :], in_=pb[:, 0:512]),
                          rd=["pbA", "pb"], wr=["aev"])
                    p.sec("dve", lambda e: vtp(e, tAt, dp[2]), rd=["aev"], wr=["tAt", "eegh"])
                    p.sec("pe", lambda e: mm_sq(e, dtp[2], dp[1], pc), rd=["d2", "d2t"], wr=["pc"])
                    p.sec("dve", lambda e: e.tensor_copy(out=dp[2][:, :], in_=pc[:, :]),
                          rd=["pc", "aev", "tAt"], wr=["d4"])
                    p.sec("dve", lambda e: vtp(e, dtp[4], dp[2]), rd=["d4"], wr=["d4t"])
                    p.sec("pe", lambda e: mm_sq(e, dtp[4], dp[2], pc), rd=["d4", "d4t"], wr=["pc"])
                    p.sec("dve", lambda e: e.tensor_copy(out=dp[1][:, :], in_=pc[:, :]),
                          rd=["pc"], wr=["d8"])
                    p.sec("dve", lambda e: vtp(e, dtp[8], dp[1]), rd=["d8"], wr=["d8t"])
                    p.sec("gps", lambda e: subIDD(e, ggl, dtp[4]), rd=["d4t", "consts"], wr=["ggl"])
                    p.sec("gps", lambda e: subIDD(e, ggr, dp[1]), rd=["d8", "consts"], wr=["ggr"])
                    p.sec("pe", lambda e: mm_prod(e, ggl, ggr, pb[:, 512:1024]),
                          rd=["ggl", "ggr"], wr=["pbB", "pb"])
                    p.sec("dve", lambda e: e.tensor_copy(out=ggl[:, :], in_=pb[:, 512:1024]),
                          rd=["pbB", "pb"], wr=["tB", "ggl"])
                    p.sec("pe", lambda e: mm_sq(e, dtp[8], dp[1], pc), rd=["d8", "d8t"], wr=["pc"])
                    p.sec("dve", lambda e: e.tensor_copy(out=dp[16][:, :], in_=pc[:, :]),
                          rd=["pc"], wr=["d16"])
                    p.sec("pe", lambda e: mm_prod(e, tAt, ggl, pb[:, 0:512]),
                          rd=["tAt", "tB", "pbA", "pb", "eegh"], wr=["pbAB", "pbA", "pb"])
                    p.sec("dve", lambda e: e.tensor_copy(out=dp[1][:, :], in_=pb[:, 0:512]),
                          rd=["pbAB", "pb", "d8", "d8t"], wr=["ab"])
                    p.sec("dve", lambda e: vtp(e, dtp[4], dp[1]), rd=["ab", "tB"], wr=["abt"])
                    p.sec("gps", lambda e: subIDD(e, ggr, dp[16]), rd=["d16", "consts"], wr=["ggr"])
                    p.sec("pe", lambda e: mm_prod(e, dtp[4], ggr, pc),
                          rd=["abt", "ggr", "d16", "pc"], wr=["pc"])
                    p.sec("dve", lambda e: e.tensor_copy(out=tfin[:, :], in_=pc[:, :]),
                          rd=["pc"], wr=["tfin"])
                    if "tfin" in dbg and first_chunk:
                        p.sec("dma", lambda e: e.dma_start(out=dbg["tfin"][:, :], in_=tfin[:, :]),
                              rd=["tfin"], n_dma=1)

                    p.stage_cur = 13
                    for j in range(4):
                        def mm_od(e, j=j):
                            last = None
                            for h in range(H):
                                last = e.matmul(pa[32 * j:32 * j + 32, h * HD:(h + 1) * HD],
                                                ptod[:, h * 128 + 32 * j: h * 128 + 32 * j + 32],
                                                uu[:, h * HD:(h + 1) * HD],
                                                start=True, stop=True,
                                                tile_position=(0, 32 * j))
                            return last
                        p.sec("pe", mm_od, rd=["ptod", "uu"], wr=["pa"])

                        def mk_pre(e, j=j):
                            return e.scalar_tensor_tensor(
                                out=preb[32 * j:32 * j + 32, :],
                                in0=pa[32 * j:32 * j + 32, :], scalar=1.0,
                                in1=rhs_[32 * j:32 * j + 32, :],
                                op0=ALU.mult, op1=ALU.add)
                        p.sec("dve", mk_pre, rd=["pa", "rhs"], wr=["preb"])

                        def mm_t32(e, j=j):
                            last = None
                            for h in range(H):
                                last = e.matmul(pb[32 * j:32 * j + 32, h * HD:(h + 1) * HD],
                                                tfin[32 * j:32 * j + 32, h * 32:(h + 1) * 32],
                                                preb[32 * j:32 * j + 32, h * HD:(h + 1) * HD],
                                                start=True, stop=True,
                                                tile_position=(32 * j, 32 * j))
                            return last
                        p.sec("pe", mm_t32, rd=["tfin", "preb"], wr=["pb"])

                        def ev_u(e, j=j):
                            return e.tensor_copy(out=uu[32 * j:32 * j + 32, :],
                                                 in_=pb[32 * j:32 * j + 32, :])
                        p.sec("dve", ev_u, rd=["pb"], wr=["uu"])
                    if "uu" in dbg and first_chunk:
                        p.sec("dma", lambda e: e.dma_start(out=dbg["uu"][:, :], in_=uu[:, :]),
                              rd=["uu"], n_dma=1)

                    p.stage_cur = 14
                    def mm_o(e):
                        last = None
                        for h in range(H):
                            last = e.matmul(pa[:, h * HD:(h + 1) * HD],
                                            gtt[:, h * 128:(h + 1) * 128],
                                            uu[:, h * HD:(h + 1) * HD],
                                            start=True, stop=True)
                        return last
                    p.sec("pe", mm_o, rd=["gt_0", "gt_1", "uu"], wr=["pa"])

                    def mk_of(e, cc=cc):
                        return e.tensor_tensor(out=outst[:, :], in0=outst[:, :], in1=pa[:, :],
                                               op=ALU.add)
                    p.sec("dve", mk_of, rd=["pa", "oa", "outst"], wr=["outst"])

                    p.stage_cur = 17 if chunk_upto >= 16 else 99
                    def st_outs(e, T=T, cc=cc):
                        return e.dma_start(out=houts[T * TILE + cc * C: T * TILE + (cc + 1) * C, :],
                                           in_=outst[:, :])
                    p.sec("dma", st_outs, rd=["outst"], wr=["houts_%d_%d" % (T, cc)], n_dma=1)
                    if "outs0" in dbg and first_chunk:
                        p.sec("dma", lambda e: e.dma_start(out=dbg["outs0"][:, :], in_=outst[:, :]),
                              rd=["outst"], n_dma=1)

                    p.stage_cur = 15
                    def sq_o(e):
                        return e.tensor_mul(eegh[:, :], outst[:, :], outst[:, :])
                    p.sec("gps", sq_o, rd=["outst"], wr=["eegh"])

                    def mm_st(e, first=not stats_on[0]):
                        e.matmul(pstat[0:1, 0:512], sc["ONESC"][:, :], outst[:, 0:512],
                                 start=first, stop=True, skip_group_check=True)
                        e.matmul(pstat[32:33, 0:512], sc["ONESC"][:, :], outst[:, 512:1024],
                                 start=first, stop=True, skip_group_check=True)
                        e.matmul(pstat[64:65, 0:512], sc["ONESC"][:, :], eegh[:, 0:512],
                                 start=first, stop=True, skip_group_check=True)
                        return e.matmul(pstat[96:97, 0:512], sc["ONESC"][:, :], eegh[:, 512:1024],
                                        start=first, stop=True, skip_group_check=True,
                                        tile_position=(0, 96))
                    p.sec("pe", mm_st, rd=["outst", "eegh", "consts"], wr=["pstat"])
                    stats_on[0] = True

                    p.stage_cur = 16
                    def mk_up(e):
                        return e.tensor_tensor(
                            out=upb[:, :].rearrange("p (h d) -> p h d", h=16),
                            in0=uu[:, :].rearrange("p (h d) -> p h d", h=16),
                            in1=rcc[:, :].rearrange("p (h a) -> p h a", a=1)
                                .broadcast_to((128, 16, 64)),
                            op=ALU.mult)
                    p.sec("dve", mk_up, rd=["uu", "rcc"], wr=["upb"])

                    def mk_acdg(e):
                        return e.tensor_tensor(
                            out=acdg[:, :], in0=sc["ID16"][:, :],
                            in1=abti[:, C - 1:C].broadcast_to((16, 16)),
                            op=ALU.mult)
                    p.sec("dve", mk_acdg, rd=["achk", "consts"], wr=["acdg"])

                    def mm_acv(e):
                        return e.matmul(pc[:, 0:16], sc["ONES16"][:, :], acdg[:, :],
                                        start=True, stop=True)
                    p.sec("pe", mm_acv, rd=["acdg", "consts", "tfin"], wr=["pc"])

                    def mm_dss(e):
                        last = None
                        for h in range(H):
                            for rb in (0, 64):
                                last = e.matmul(pa[rb:rb + 64, h * HD:(h + 1) * HD],
                                                khtm[:, h * HD:(h + 1) * HD],
                                                upb[:, h * HD:(h + 1) * HD],
                                                start=True, stop=True)
                        return last
                    p.sec("pe", mm_dss, rd=["khtm", "upb", "outst"], wr=["pa"])

                    def upd_ss1(e):
                        return e.tensor_tensor(
                            out=sst[:, :].rearrange("p (h d) -> p h d", h=16),
                            in0=ssf[:, :].rearrange("p (h d) -> p h d", h=16),
                            in1=pc[:, 0:16].rearrange("p (h a) -> p h a", a=1)
                            .broadcast_to((128, 16, 64)), op=ALU.mult)
                    p.sec("dve", upd_ss1, rd=["pc", "ssf"], wr=["lntmp"])

                    def upd_ss2(e):
                        return e.tensor_tensor(out=ssf[:, :], in0=sst[:, :], in1=pa[:, :],
                                               op=ALU.add)
                    p.sec("dve", upd_ss2, rd=["pa", "lntmp"], wr=["ssf"])

                    def cast_ss(e):
                        e.tensor_copy(out=ssbE[0:64, :], in_=ssf[0:64, :])
                        return e.tensor_copy(out=ssbO[64:128, :], in_=ssf[64:128, :])
                    p.sec("gps", cast_ss, rd=["ssf"], wr=["ssb"])
                    if "ssf" in dbg and T == 0 and cc == NCH - 1:
                        p.sec("dma", lambda e: e.dma_start(out=dbg["ssf"][:, :], in_=ssf[:, :]),
                              rd=["ssf"], n_dma=1)

            if do_attn:
                def ev_stats(e):
                    e.tensor_copy(out=sst[0:1, 0:512], in_=pstat[0:1, 0:512])
                    e.tensor_copy(out=sst[32:33, 0:512], in_=pstat[32:33, 0:512])
                    e.tensor_copy(out=sst[64:65, 0:512], in_=pstat[64:65, 0:512])
                    return e.tensor_copy(out=sst[96:97, 0:512], in_=pstat[96:97, 0:512])
                p.sec("dve", ev_stats, rd=["pstat"], wr=["lntmp"])

                def red_stats(e):
                    last = None
                    order = ((0, 0), (32, 8), (64, 16), (96, 24))
                    for r, c0 in order:
                        last = e.tensor_reduce(out=gnc[0:1, c0:c0 + 8],
                                               in_=sst[r:r + 1, 0:512].rearrange("p (h d) -> p h d", h=8),
                                               axis=AX.X, op=ALU.add)
                    return last
                p.sec("dve", red_stats, rd=["lntmp"], wr=["gnc"])

                def tp_stats(e):
                    return e.transpose(out=gnc[0:32, 32:64], in_=gnc[0:32, 0:32])
                p.sec("dve", tp_stats, rd=["gnc"], wr=["gnc"])
                NTOK = float(n_tiles * TILE * HD)

                def mk_mu(e):
                    return e.tensor_scalar(out=gnc[0:32, 33:34], in0=gnc[0:32, 32:33],
                                           scalar1=1.0 / NTOK, scalar2=None, op0=ALU.mult)
                p.sec("dve", mk_mu, rd=["gnc"], wr=["gnc"])

                def st_gn(e):
                    return e.dma_start(out=hrow[2:3, 0:32].rearrange("o (p a) -> (o p) a", a=1),
                                       in_=gnc[0:32, 33:34],
                                       allow_slow_non_contiguous=True)
                p.sec("dma", st_gn, rd=["gnc"], wr=["hrow2"], n_dma=1)

                def ld_gn(e):
                    a = e.dma_start(out=gnr[:, 0:1],
                                    in_=hrow[2:3, 0:16].rearrange("o (f a) -> (o f) a", a=1),
                                    allow_slow_non_contiguous=True)
                    b = e.dma_start(out=gnr[:, 2:3],
                                    in_=hrow[2:3, 16:32].rearrange("o (f a) -> (o f) a", a=1),
                                    allow_slow_non_contiguous=True)
                    return [a, b]
                p.sec("dma", ld_gn, rd=["hrow2"], wr=["gnr"], n_dma=2)

                def mk_gn1a(e):
                    return e.tensor_scalar(out=gnr[:, 4:5], in0=gnr[:, 0:1],
                                           scalar1=gnr[:, 0:1], scalar2=None, op0=ALU.mult)
                p.sec("dve", mk_gn1a, rd=["gnr"], wr=["gnr_a"])

                def mk_gn1b(e):
                    return e.tensor_tensor(out=gnr[:, 4:5], in0=gnr[:, 2:3], in1=gnr[:, 4:5],
                                           op=ALU.subtract)
                p.sec("dve", mk_gn1b, rd=["gnr_a", "gnr"], wr=["gnr"])

                def mk_gn2(e):
                    return e.activation(out=gnr[:, 5:6], in_=gnr[:, 4:5], func=AF.Sqrt,
                                        bias=1e-5)
                p.sec("act", mk_gn2, rd=["gnr"], wr=["gnr"])

                def mk_vb0(e):
                    return e.reciprocal(out=gnr[:, 5:6], in_=gnr[:, 5:6])
                p.sec("dve", mk_vb0, rd=["gnr"], wr=["gnr_c"])

                def mk_vb1(e):
                    e.tensor_tensor(out=vbr[:, :], in0=sc["PATTERN16"][:, :],
                                    in1=gnr[:, 5:6].broadcast_to((16, 8)), op=ALU.mult)
                    return e.tensor_tensor(out=vbm[:, :], in0=sc["PATTERN16"][:, :],
                                           in1=gnr[:, 0:1].broadcast_to((16, 8)), op=ALU.mult)
                p.sec("dve", mk_vb1, rd=["gnr_c", "gnr", "consts"], wr=["vb"])

                def mk_indpf(e):
                    return e.tensor_copy(out=INDPF[:, :], in_=sc["INDP"][:, :])
                p.sec("dve", mk_indpf, rd=["consts"], wr=["indpf"])

                def mm_vb2(e):
                    e.matmul(pc[:, 4:12], INDPF[:, :], vbr[:, :], start=True, stop=True)
                    return e.matmul(pc[:, 12:20], INDPF[:, :], vbm[:, :], start=True, stop=True)
                p.sec("pe", mm_vb2, rd=["vb", "indpf"], wr=["pc"])

                def mk_gnf1(e):
                    return e.tensor_tensor(out=gnmul[:, :], in0=pc[:, 4:12],
                                           in1=sbias["gns"][:, :], op=ALU.mult)
                p.sec("dve", mk_gnf1, rd=["pc", "consts"], wr=["gnf_a"])

                def mk_gnf2(e):
                    return e.tensor_tensor(out=gnadd[:, :], in0=pc[:, 12:20], in1=gnmul[:, :],
                                           op=ALU.mult)
                p.sec("dve", mk_gnf2, rd=["pc", "gnf_a"], wr=["gnf_b"])

                def mk_gnf3(e):
                    return e.tensor_tensor(out=gnadd[:, :], in0=sbias["gnb"][:, :],
                                           in1=gnadd[:, :], op=ALU.subtract)
                p.sec("dve", mk_gnf3, rd=["gnf_b", "consts"], wr=["gnfin"])
                if "gnmul" in dbg:
                    p.sec("dma", lambda e: e.dma_start(out=dbg["gnmul"][:, :], in_=gnmul[:, :]),
                          rd=["gnfin"], n_dma=1)
                if "gnadd" in dbg:
                    p.sec("dma", lambda e: e.dma_start(out=dbg["gnadd"][:, :], in_=gnadd[:, :]),
                          rd=["gnfin"], n_dma=1)

            def amax_loc(e):
                return e.reduce_max(out=amaxc[:, 36:37], in_=amaxc[:, 0:32], axis=AX.X)
            p.sec("dve", amax_loc, rd=["amaxc"], wr=["amaxv"])

            def st_am(e):
                a = e.dma_start(out=hrow[0:1, :].rearrange("o (p a) -> (o p) a", a=1), in_=amaxc[:, 36:37],
                                allow_slow_non_contiguous=True)
                b = e.dma_start(out=hrow[1:2, :].rearrange("o (p a) -> (o p) a", a=1), in_=amaxc[:, 35:36],
                                allow_slow_non_contiguous=True)
                return [a, b]
            p.sec("dma", st_am, rd=["amaxv", "amaxc"], wr=["hrow"], n_dma=2)

            def ld_am(e):
                return e.dma_start(out=srow[0:2, :], in_=hrow[0:2, :])
            p.sec("dma", ld_am, rd=["hrow"], wr=["srow"], n_dma=1)

            def red_am(e):
                return e.reduce_max(out=sval[0:2, 0:1], in_=srow[0:2, :], axis=AX.X)
            p.sec("dve", red_am, rd=["srow"], wr=["sval"])

            def st_cc(e):
                return e.dma_start(out=hccin[:, :], in_=sval[0:1, 0:1])
            p.sec("dma", st_cc, rd=["sval"], wr=["hccin"], n_dma=1)

            if n_cores == 8:
                def do_cc(e):
                    return e.collective_compute(
                        "AllReduce", ALU.max, replica_groups=[[0, 1, 2, 3, 4, 5, 6, 7]],
                        ins=[hccin[:, :]], outs=[hccout[:, :]])
                p.sec("gps", do_cc, rd=["hccin"], wr=["hccout"])
            else:
                def do_cc(e):
                    return e.dma_start(out=hccout[:, :], in_=hccin[:, :])
                p.sec("dma", do_cc, rd=["hccin"], wr=["hccout"], n_dma=1)

            def ld_cc(e):
                return e.dma_start(out=sval[0:1, 1:2], in_=hccout[:, :])
            p.sec("dma", ld_cc, rd=["hccout"], wr=["sval_g"], n_dma=1)

            def st_wam(e):
                return e.dma_start(out=hwam[:, :], in_=sval[1:2, 0:1])
            p.sec("dma", st_wam, rd=["sval"], wr=["hwam"], n_dma=1)

            def ld_wam(e):
                return e.dma_start(out=sval[0:1, 5:6], in_=hwam[:, :])
            p.sec("dma", ld_wam, rd=["hwam"], wr=["sval_w"], n_dma=1)

            scale_steps = [
                lambda e: e.tensor_scalar(out=sval[0:1, 2:3], in0=sval[0:1, 1:2],
                                          scalar1=1e-12, scalar2=None, op0=ALU.max),
                lambda e: e.reciprocal(out=sval[0:1, 3:4], in_=sval[0:1, 2:3]),
                lambda e: e.tensor_scalar(out=sval[0:1, 4:5], in0=sval[0:1, 3:4],
                                          scalar1=224.0, scalar2=None, op0=ALU.mult),
                lambda e: e.tensor_scalar(out=sval[0:1, 9:10], in0=sval[0:1, 5:6],
                                          scalar1=1e-12, scalar2=None, op0=ALU.max),
                lambda e: e.reciprocal(out=sval[0:1, 10:11], in_=sval[0:1, 9:10]),
                lambda e: e.tensor_scalar(out=sval[0:1, 6:7], in0=sval[0:1, 10:11],
                                          scalar1=224.0, scalar2=None, op0=ALU.mult),
                lambda e: e.tensor_scalar(out=sval[0:1, 7:8], in0=sval[0:1, 4:5],
                                          scalar1=sval[0:1, 6:7], scalar2=None, op0=ALU.mult),
                lambda e: e.tensor_scalar(out=sval[0:1, 8:9], in0=sval[0:1, 2:3],
                                          scalar1=sval[0:1, 9:10], scalar2=None, op0=ALU.mult),
                lambda e: e.tensor_scalar(out=sval[0:1, 8:9], in0=sval[0:1, 8:9],
                                          scalar1=1.0 / (224.0 * 224.0), scalar2=None,
                                          op0=ALU.mult),
            ]
            for si_, stp in enumerate(scale_steps):
                p.sec("dve", stp, rd=["sval_g", "sval_w", "sval", "scales"], wr=["scales"])
            if "aval" in dbg:
                p.sec("dma", lambda e: e.dma_start(out=dbg["aval"][:, :], in_=sval[0:1, 4:8]),
                      rd=["scales"], n_dma=1)

            def mm_bcast(e):
                e.matmul(pc[:, 0:1], sc["ONESF"][:, :], sval[0:1, 4:5], start=True, stop=True)
                e.matmul(pc[:, 1:2], sc["ONESF"][:, :], sval[0:1, 8:9], start=True, stop=True)
                return e.matmul(pc[:, 2:3], sc["ONESF"][:, :], sval[0:1, 6:7],
                                start=True, stop=True)
            p.sec("pe", mm_bcast, rd=["scales", "consts"], wr=["pc"])

            def ev_bcast(e):
                return e.tensor_copy(out=sxcol[:, 0:3], in_=pc[:, 0:3])
            p.sec("dve", ev_bcast, rd=["pc"], wr=["sxcol"])

            def mk_brow(e):
                e.tensor_copy(out=sbrow16[0:1, :], in_=sbrow[0:1, :])
                return e.tensor_scalar(out=sbrow16[32:33, :], in0=sbrow[32:33, :],
                                       scalar1=sval[0:1, 7:8], scalar2=None, op0=ALU.mult)
            p.sec("dve", mk_brow, rd=["consts", "scales"], wr=["brow"])

            p.flush()
            if do_phase2:
                esi.close()
                es2 = ExitStack()
                s2 = lambda n, s, dt=F32: es2.enter_context(nc.sbuf_tensor(n, s, dt))
                swg = s2("swg", [128, 8 * DM], BF16)
                swo = s2("swo", [128, 8 * DM], BF16)
                wt8 = s2("wt8", [128, 8 * DM], FP8)
                wtf2 = s2("wtf2", [128, DM])
                xnf2 = s2("xnf2", [128, 8 * TILE])
                xnfb2 = s2("xnfb2", [128, 8 * TILE], BF16)
                xq8 = s2("xq8", [128, 8 * TILE], FP8)
                ofm = s2("ofm", [128, 8 * TILE], BF16)
                gf = s2("gf", [128, 8 * TILE], BF16)
                yf = s2("yf", [128, 8 * TILE], BF16)
                gyf = s2("gyf", [128, 8 * TILE], BF16)
                xt2 = s2("xt2", [128, DM])
                outf = s2("outf", [128, DM])

                with es2:
                    def ld_w2(e):
                        ins = []
                        for w, dst in (("wg", swg), ("wo", swo)):
                            for cc2 in range(8):
                                ins.append(e.dma_start(out=dst[:, cc2 * DM:(cc2 + 1) * DM],
                                                       in_=hw[w][cc2, :, :]))
                        return ins
                    p.sec("gdma", ld_w2, wr=["w2"], n_dma=16)

                    for cc2 in range(8):
                        def ld_wt(e, cc2=cc2):
                            return e.dma_start(out=wtf2[:, :], in_=hwt[cc2, :, :])
                        p.sec("dma", ld_wt, rd=["hwt_%d" % cc2], wr=["wtf2"], n_dma=1)

                        def q_wt(e, cc2=cc2):
                            return e.tensor_scalar(out=wt8[:, cc2 * DM:(cc2 + 1) * DM],
                                                   in0=wtf2[:, :], scalar1=sxcol[:, 2:3],
                                                   scalar2=None, op0=ALU.mult)
                        p.sec("dve", q_wt, rd=["wtf2", "sxcol"], wr=["wt8_%d" % cc2])
                    wt8_r = ["wt8_%d" % i for i in range(8)]

                    for T in range(n_tiles):
                        def ld_p2(e, T=T):
                            ins = [e.dma_start(out=xnf2[:, :], in_=hxnf[T, :, :])]
                            for fb in range(8):
                                ins.append(e.dma_start(
                                    out=ofm[:, fb * 512:(fb + 1) * 512],
                                    in_=houts[T * TILE:(T + 1) * TILE, fb * 128:(fb + 1) * 128],
                                    transpose=True))
                            return ins
                        deps_outs = ["houts_%d_%d" % (T, cc3) for cc3 in range(NCH)] if do_attn else []
                        p.sec("dma", ld_p2, rd=["hxnf_%d_0" % T, "hxnf_%d_1" % T,
                                                "hxnf_%d_2" % T, "hxnf_%d_3" % T] + deps_outs,
                              wr=["xnf2", "ofm"], n_dma=9)

                        def cast_x2(e):
                            return e.tensor_copy(out=xnfb2[:, :], in_=xnf2[:, :])
                        p.sec("dve", cast_x2, rd=["xnf2"], wr=["xnfb2"])

                        def q_x(e):
                            return e.activation(out=xq8[:, :], in_=xnf2[:, :], func=AF.Identity,
                                                scale=sxcol[:, 0:1])
                        p.sec("act", q_x, rd=["xnf2", "sxcol"], wr=["xq8"])

                        for fbp in range(4):
                            psg, pkg = (pa, "pa") if fbp % 2 == 0 else (pb, "pb")

                            def mm_g(e, fbp=fbp, psg=psg):
                                last = None
                                for half in range(2):
                                    fb = fbp * 2 + half
                                    for cc2 in range(8):
                                        last = e.matmul(
                                            psg[:, half * 512:(half + 1) * 512],
                                            swg[:, cc2 * DM + fb * 128: cc2 * DM + (fb + 1) * 128],
                                            xnfb2[:, cc2 * 512:(cc2 + 1) * 512],
                                            start=(cc2 == 0), stop=(cc2 == 7))
                                return last
                            p.sec("pe", mm_g, rd=["xnfb2", "w2"], wr=[pkg])

                            def ev_g(e, fbp=fbp, psg=psg):
                                last = None
                                for half in range(2):
                                    fb = fbp * 2 + half
                                    last = e.activation(out=gf[:, fb * 512:(fb + 1) * 512],
                                                        in_=psg[:, half * 512:(half + 1) * 512],
                                                        func=AF.Silu, bias=sbias["bg"][:, fb:fb + 1])
                                return last
                            p.sec("act", ev_g, rd=[pkg, "consts"], wr=["gf_%d" % fbp])

                        def mk_y(e):
                            last = None
                            for fb in range(8):
                                last = e.tensor_scalar(out=yf[:, fb * 512:(fb + 1) * 512],
                                                       in0=ofm[:, fb * 512:(fb + 1) * 512],
                                                       scalar1=gnmul[:, fb:fb + 1],
                                                       scalar2=gnadd[:, fb:fb + 1],
                                                       op0=ALU.mult, op1=ALU.add)
                            return last
                        p.sec("dve", mk_y, rd=["ofm", "gnfin"], wr=["yf"])

                        def mk_gy(e):
                            return e.tensor_mul(gyf[:, :], gf[:, :], yf[:, :])
                        p.sec("dve", mk_gy, rd=["yf"] + ["gf_%d" % i for i in range(4)],
                              wr=["gyf"])

                        for s_ in range(NSUB):
                            def ld_x2(e, T=T, s_=s_):
                                return e.dma_start(
                                    out=xt2[:, :],
                                    in_=hx[T * TILE + s_ * 128: T * TILE + (s_ + 1) * 128, :])
                            p.sec("dma", ld_x2, wr=["xt2"], n_dma=1)

                            atth = ((pa[:, 0:512], pa[:, 512:1024]) if s_ % 2 == 0
                                    else (pc[:, 0:512], pstat[:, 0:512]))
                            attk = "pa" if s_ % 2 == 0 else "pcs"

                            def mm_att(e, s_=s_, atth=atth):
                                last = None
                                for half in range(2):
                                    for cc2 in range(8):
                                        e.matmul(
                                            atth[half],
                                            gyf[:, cc2 * 512 + s_ * 128: cc2 * 512 + (s_ + 1) * 128],
                                            swo[:, cc2 * DM + half * 512: cc2 * DM + (half + 1) * 512],
                                            start=(cc2 == 0), stop=False)
                                    last = e.matmul(atth[half], sc["ONES1"][:, :],
                                                    sbrow16[0:1, half * 512:(half + 1) * 512],
                                                    start=False, stop=True)
                                return last
                            p.sec("pe", mm_att, rd=["gyf", "w2", "consts", "brow"], wr=[attk])

                            def mm_ffn(e, s_=s_):
                                last = None
                                for half in range(2):
                                    for cc2 in range(8):
                                        e.matmul(
                                            pb[:, half * 512:(half + 1) * 512],
                                            xq8[:, cc2 * 512 + s_ * 128: cc2 * 512 + (s_ + 1) * 128],
                                            wt8[:, cc2 * DM + half * 512: cc2 * DM + (half + 1) * 512],
                                            start=(cc2 == 0), stop=False)
                                    last = e.matmul(pb[:, half * 512:(half + 1) * 512],
                                                    sc["ONES33"][32:33, :],
                                                    sbrow16[32:33, half * 512:(half + 1) * 512],
                                                    start=False, stop=True)
                                return last
                            p.sec("pe", mm_ffn, rd=["xq8", "brow", "consts"] + wt8_r, wr=["pb"])

                            def mk_out1(e, s_=s_):
                                return e.scalar_tensor_tensor(
                                    out=outf[:, :], in0=pb[:, :], scalar=sxcol[:, 1:2],
                                    in1=xt2[:, :], op0=ALU.mult, op1=ALU.add)
                            p.sec("dve", mk_out1, rd=["pb", "xt2", "sxcol"], wr=["outf_a"])

                            def mk_out2(e, s_=s_, atth=atth):
                                e.tensor_tensor(out=outf[:, 0:512], in0=outf[:, 0:512],
                                                in1=atth[0], op=ALU.add)
                                return e.tensor_tensor(out=outf[:, 512:1024],
                                                       in0=outf[:, 512:1024],
                                                       in1=atth[1], op=ALU.add)
                            p.sec("dve", mk_out2, rd=[attk, "outf_a"], wr=["outf"])

                            def st_y(e, T=T, s_=s_):
                                return e.dma_start(
                                    out=hy[T * TILE + s_ * 128: T * TILE + (s_ + 1) * 128, :],
                                    in_=outf[:, :])
                            p.sec("dma", st_y, rd=["outf"], wr=["y_%d_%d" % (T, s_)], n_dma=1)

            def fin(e):
                return None
            p.sec("dma", fin, no_inc=True,
                  extra=[p.t("gdma"), p.t("pe"), p.t("dve"), p.t("act"),
                         p.t("gps"), p.t("dma")])
            p.flush()

    return nc


def kernel(**inputs):
    cm = prep_common(inputs)
    x = np.asarray(inputs["L0_attn_x"], np.float32)
    nc = build_program()
    in_maps = []
    for i in range(8):
        m = dict(cm)
        m["x"] = np.ascontiguousarray(x[i])
        in_maps.append(m)
    res = run_bass_kernel_spmd(nc, in_maps, core_ids=list(range(8)))
    return np.stack([r["y"] for r in res.results], axis=0)


# version 19
# speedup vs baseline: 1.0595x; 1.0199x over previous
import numpy as np
import ml_dtypes
from contextlib import ExitStack
import concourse.bass as bass
import concourse.mybir as mybir
from concourse.bass_utils import run_bass_kernel_spmd

F32 = mybir.dt.float32
BF16 = mybir.dt.bfloat16
FP8 = mybir.dt.float8e4
AF = mybir.ActivationFunctionType
ALU = mybir.AluOpType
AX = mybir.AxisListType

S, DM, H, HD = 4096, 1024, 16, 64
TILE, NSUB, C, B = 512, 4, 128, 32
NCH = TILE // C
NT = S // TILE
P3 = TILE + 3


def make_consts():
    c = {}
    ii = np.arange(128)
    bf = ml_dtypes.bfloat16
    c["IDENT128"] = np.eye(128, dtype=bf)
    id64 = np.zeros((128, 64), dtype=bf)
    id64[:64, :] = np.eye(64)
    id64[64:, :] = np.eye(64)
    c["ID64"] = id64
    c["MASK1"] = (ii[:, None] < ii[None, :]).astype(np.float32).astype(bf)
    c["IDT1"] = np.eye(128, dtype=np.float32).astype(bf)
    c["NEGBLK1"] = (-(ii[:, None] < (ii[None, :] // B) * B).astype(np.float32)).astype(bf)
    idd = np.zeros((128, B), dtype=np.float32)
    for i in range(4):
        idd[32 * i:32 * i + 32, :] = np.eye(32)
    c["IDD1"] = idd.astype(bf)
    ind2 = np.zeros((128, 8 * H), dtype=np.float32)
    for fb in range(8):
        ind2[0:64, fb * H + 2 * fb] = 1.0
        ind2[64:128, fb * H + 2 * fb + 1] = 1.0
    c["IND2"] = ind2.astype(bf)
    indt = np.zeros((H, 8 * 128), dtype=np.float32)
    for fb in range(8):
        indt[2 * fb, fb * 128:fb * 128 + 64] = 1.0
        indt[2 * fb + 1, fb * 128 + 64:fb * 128 + 128] = 1.0
    c["INDT"] = indt.astype(bf)
    i64 = np.zeros((H, H * HD), dtype=np.float32)
    for h in range(H):
        i64[h, h * HD:(h + 1) * HD] = 1.0
    c["IND64"] = i64.astype(bf)
    ibd = np.zeros((H, H * 128), np.float32)
    for h in range(H):
        ibd[h, h * 128:(h + 1) * 128] = 1.0
    c["INDBD"] = ibd.astype(bf)
    c["ONES1"] = np.ones((1, 128), dtype=bf)
    o33 = np.zeros((33, 128), np.float32); o33[0] = 1; o33[32] = 1
    c["ONES33"] = o33.astype(bf)
    c["ONES2"] = np.ones((2, 128), dtype=bf)
    c["ONES16"] = np.ones((16, 128), dtype=bf)
    c["ONESC"] = np.ones((128, 1), dtype=bf)
    c["ONESF"] = np.ones((1, 128), dtype=np.float32)
    c["ID16"] = np.eye(16, dtype=np.float32)
    indp = np.zeros((H, 128), np.float32)
    for h in range(H):
        indp[h, 64 * (h % 2):64 * (h % 2) + 64] = 1.0
    c["INDP"] = indp.astype(bf)
    p16 = np.zeros((H, 8), np.float32)
    for h in range(H):
        p16[h, h // 2] = 1.0
    c["PATTERN16"] = p16
    cv = np.zeros((128, 4), np.float32)
    cv[:, 0] = 1e-5; cv[:, 1] = -1.0
    cv[0:64, 2] = 1.0
    cv[64:128, 3] = 1.0
    c["CVALS"] = cv
    return c


CONSTS = make_consts()


def prep_common(inputs):
    f32 = np.float32
    cm = {}
    for k, v in CONSTS.items():
        cm["c_" + k] = v

    def rw(w):
        return np.ascontiguousarray(np.asarray(w, f32).reshape(8, 128, -1))
    cm["wq"] = rw(inputs["L0_attn_Wq"]); cm["wk"] = rw(inputs["L0_attn_Wk"])
    cm["wv"] = rw(inputs["L0_attn_Wv"]); cm["wg"] = rw(inputs["L0_attn_Wg"])
    cm["wo"] = rw(inputs["L0_attn_Wo"])
    cm["wa"] = rw(inputs["L0_attn_Wa"]); cm["wb"] = rw(inputs["L0_attn_Wb"])
    cm["wffn"] = np.ascontiguousarray(np.asarray(inputs["L0_ffn_weight"], f32).reshape(8, 128, DM))

    def col8(b):
        return np.ascontiguousarray(np.asarray(b, f32).reshape(8, 128).T)
    cm["bq"] = col8(inputs["L0_attn_bq"]); cm["bk"] = col8(inputs["L0_attn_bk"])
    cm["bv"] = col8(inputs["L0_attn_bv"]); cm["bg"] = col8(inputs["L0_attn_bg"])
    cm["gns"] = col8(inputs["L0_attn_gn_scale"]); cm["gnb"] = col8(inputs["L0_attn_gn_bias"])
    cm["bo"] = np.ascontiguousarray(np.asarray(inputs["L0_attn_bo"], f32).reshape(1, DM))
    cm["bffn"] = np.ascontiguousarray(np.asarray(inputs["L0_ffn_bias"], f32).reshape(1, DM))
    cm["ba"] = np.ascontiguousarray(np.asarray(inputs["L0_attn_ba"], f32).reshape(H, 1))
    cm["bb"] = np.ascontiguousarray(np.asarray(inputs["L0_attn_bb"], f32).reshape(H, 1))
    cw = np.zeros((128, 8 * 3 * 4), f32)
    for pr, key in enumerate(("L0_attn_q_conv", "L0_attn_k_conv", "L0_attn_v_conv")):
        w = np.asarray(inputs[key], f32).reshape(8, 128, 4)
        for fb in range(8):
            cw[:, (pr * 8 + fb) * 4:(pr * 8 + fb) * 4 + 4] = w[fb]
    cm["convw"] = cw
    return cm


class Prog:
    ENG2BLK = {"pe": "tensor", "dve": "vector", "act": "scalar",
               "gps": "gpsimd", "dma": "sync", "gdma": "gpsimd"}

    def __init__(self, nc, block, sems):
        self.nc, self.block, self.sems = nc, block, sems
        self.cnt = {k: 0 for k in self.ENG2BLK}
        self.res = {}
        self.pend = {k: [] for k in self.ENG2BLK}

    FLUSH_N = 1
    FLUSH_EVERY = True

    def flush(self):
        for eng, bodies in self.pend.items():
            n = self.FLUSH_N
            for i in range(0, len(bodies), n):
                def mega(e, bs=tuple(bodies[i:i + n])):
                    for b in bs:
                        b(e)
                getattr(self.block, self.ENG2BLK[eng])(mega)
            self.pend[eng] = []

    def _get(self, r):
        if r not in self.res:
            self.res[r] = {"w": None, "r": []}
        return self.res[r]

    stage_limit = 99
    stage_cur = 0

    def sec(self, eng, fn, rd=(), wr=(), extra=(), n_dma=0, no_inc=False):
        if self.stage_cur > self.stage_limit:
            return (eng, self.cnt[eng])
        deps = []
        for r in rd:
            st = self._get(r)
            if st["w"] is not None:
                deps.append(st["w"])
        for r in wr:
            st = self._get(r)
            if st["w"] is not None:
                deps.append(st["w"])
            deps.extend(st["r"])
        deps.extend([d for d in extra if d is not None])
        is_dma = eng in ("dma", "gdma")
        if is_dma and n_dma > 0:
            deps.append((eng, self.cnt[eng]))
        dd = {}
        for (de, dc) in deps:
            if dc > 0 and dc > dd.get(de, 0):
                dd[de] = dc
        deps = list(dd.items())
        for (de, dc) in deps:
            assert dc <= self.cnt[de], (eng, de, dc, self.cnt[de])
        sems = self.sems

        def body(e):
            for (de, dc) in deps:
                e.wait_ge(sems[de], dc)
            last = fn(e)
            if is_dma:
                if isinstance(last, (list, tuple)):
                    for ins in last:
                        ins.then_inc(sems[eng], 16)
                elif last is not None:
                    last.then_inc(sems[eng], 16)
            elif last is not None:
                last.then_inc(sems[eng], 1)
            else:
                assert no_inc, "compute section returned None without no_inc"
        self.pend[eng].append(body)
        if self.FLUSH_EVERY:
            self.flush()
        if is_dma:
            self.cnt[eng] += 16 * n_dma
        elif not no_inc:
            self.cnt[eng] += 1
        tk = (eng, self.cnt[eng])
        for r in rd:
            self._get(r)["r"].append(tk)
        for r in wr:
            st = self._get(r)
            st["w"] = tk
            st["r"] = []
        return tk

    def t(self, eng):
        return (eng, self.cnt[eng])


def build_program(n_tiles=NT, do_attn=True, do_phase2=True, debug=(), n_cores=8, chunk_upto=99):
    nc = bass.Bass("TRN2", target_bir_lowering=False, detect_race_conditions=False)

    hx = nc.dram_tensor("x", [S, DM], F32, kind="ExternalInput").ap()
    hw = {w: nc.dram_tensor(w, [8, 128, DM], F32, kind="ExternalInput").ap()
          for w in ("wq", "wk", "wv", "wg", "wo", "wffn")}
    hw["wa"] = nc.dram_tensor("wa", [8, 128, H], F32, kind="ExternalInput").ap()
    hw["wb"] = nc.dram_tensor("wb", [8, 128, H], F32, kind="ExternalInput").ap()
    hbn = ("bq", "bk", "bv", "bg", "gns", "gnb")
    hb = {b: nc.dram_tensor(b, [128, 8], F32, kind="ExternalInput").ap() for b in hbn}
    hb["bo"] = nc.dram_tensor("bo", [1, DM], F32, kind="ExternalInput").ap()
    hb["bffn"] = nc.dram_tensor("bffn", [1, DM], F32, kind="ExternalInput").ap()
    hb["ba"] = nc.dram_tensor("ba", [H, 1], F32, kind="ExternalInput").ap()
    hb["bb"] = nc.dram_tensor("bb", [H, 1], F32, kind="ExternalInput").ap()
    hconv = nc.dram_tensor("convw", [128, 8 * 3 * 4], F32, kind="ExternalInput").ap()
    hc = {}
    for k, v in CONSTS.items():
        dt = BF16 if v.dtype == ml_dtypes.bfloat16 else F32
        hc[k] = nc.dram_tensor("c_" + k, list(v.shape), dt, kind="ExternalInput").ap()

    hy = nc.dram_tensor("y", [S, DM], F32, kind="ExternalOutput").ap()
    hxnf = nc.dram_tensor("xnf_sc", [NT, 128, 8 * TILE], F32, kind="Internal").ap()
    houts = nc.dram_tensor("outs_sc", [S, DM], BF16, kind="Internal").ap()
    hwt = nc.dram_tensor("wt_sc", [8, 128, DM], F32, kind="Internal").ap()
    hrow = nc.dram_tensor("rowbounce", [4, 128], F32, kind="Internal").ap()
    hlhl = nc.dram_tensor("lhl_sc", [48, TILE], BF16, kind="Internal").ap()
    hwam = nc.dram_tensor("wam", [1, 1], F32, kind="Internal").ap()
    hccin = nc.dram_tensor("cc_in", [1, 1], F32, kind="Internal").ap()
    hccout = nc.dram_tensor("cc_out", [1, 1], F32, kind="Internal").ap()

    dbg = {}
    def mkdbg(name, shape, dt=F32):
        if name in debug:
            dbg[name] = nc.dram_tensor("dbg_" + name, list(shape), dt,
                                       kind="ExternalOutput").ap()
    mkdbg("xnf", [128, 8 * TILE])
    mkdbg("kfn", [128, 8 * TILE], BF16)
    mkdbg("qfn", [128, 8 * TILE], BF16)
    mkdbg("vf", [128, 8 * TILE], BF16)
    mkdbg("khf", [128, 8 * TILE], BF16)
    mkdbg("ltile", [H, TILE])
    mkdbg("ee", [128, H * C], BF16)
    mkdbg("ptraw", [128, H * C], BF16)
    mkdbg("gt", [128, H * C], BF16)
    mkdbg("tfin", [128, H * B], BF16)
    mkdbg("uu", [128, H * HD], BF16)
    mkdbg("outs0", [128, DM], BF16)
    mkdbg("ssf", [128, H * HD])
    mkdbg("gnmul", [128, 8])
    mkdbg("gnadd", [128, 8])
    mkdbg("aval", [1, 4])

    eso = ExitStack()
    so = lambda n, s, dt=F32: eso.enter_context(nc.sbuf_tensor(n, s, dt))

    sc = {}
    for k, v in CONSTS.items():
        dt = BF16 if v.dtype == ml_dtypes.bfloat16 else F32
        sc[k] = so("s_" + k, list(v.shape), dt)
    sbias = {k: so("sb_" + k, [128, 8]) for k in hbn}
    sbrow = so("sbrow", [33, DM])
    sbrow16 = so("sbrow16", [33, DM], BF16)
    sba = so("sb_ba", [H, 1])
    sbb = so("sb_bb", [H, 1])
    sconv = so("sb_convw", [128, 8 * 3 * 4])
    amaxc = so("amaxc", [128, 40])
    srow = so("srow", [4, 128])
    sval = so("sval", [4, 12])
    sxcol = so("sxcol", [128, 4])
    gnmul = so("gnmul", [128, 8])
    gnadd = so("gnadd", [128, 8])
    gnc = so("gnc", [32, 64])
    gnr = so("gnr", [16, 8])
    vbr = so("vbr", [16, 8])
    vbm = so("vbm", [16, 8])
    INDPF = so("indpf_t", [16, 128])
    identf = so("identf", [128, 128])

    pa = eso.enter_context(nc.psum_tensor("pa", [128, 1024], F32))
    pb = eso.enter_context(nc.psum_tensor("pb", [128, 1024], F32))
    pc = eso.enter_context(nc.psum_tensor("pc", [128, 512], F32))
    pstat = eso.enter_context(nc.psum_tensor("pstat", [128, 512], F32))
    pt8 = eso.enter_context(nc.psum_tensor("pt8", [128, 2048], BF16))

    sems = {k: eso.enter_context(nc.semaphore("sem_" + k))
            for k in ("pe", "dve", "act", "gps", "dma", "gdma")}
    nc.const_aps.register(F32, 1e-5, sc["CVALS"][:, 0:1])
    nc.const_aps.register(F32, -1.0, sc["CVALS"][:, 1:2])

    esi = ExitStack()
    si = lambda n, s, dt=F32: esi.enter_context(nc.sbuf_tensor(n, s, dt))
    swq = si("swq", [128, 8 * DM], BF16)
    swk = si("swk", [128, 8 * DM], BF16)
    swv = si("swv", [128, 8 * DM], BF16)
    swa = si("swa", [128, 8 * H], BF16)
    swb = si("swb", [128, 8 * H], BF16)
    sdiag = si("sdiag", [128, 64 * 128], BF16)
    vcv = si("vcv", [128, 512], BF16)
    xts = si("xts", [128, DM])
    lntmp = si("lntmp", [128, DM])
    xnfs = xts
    xnfb = si("xnfb", [128, 8 * TILE], BF16)
    lncol = si("lncol", [128, 8])
    pre = si("pre", [128, 8 * P3], BF16)
    halo = si("halo", [128, 3 * 8 * 3], BF16)
    qfn = si("qfn", [128, 8 * TILE], BF16)
    kfn = si("kfn", [128, 8 * TILE], BF16)
    khfc = si("khfc", [128, 8 * C], BF16)
    khfcE = si("khfcE", [128, 8 * C], BF16)
    khfcO = si("khfcO", [128, 8 * C], BF16)
    vf = si("vf", [128, 8 * TILE], BF16)
    ksq = si("ksq", [128, TILE], BF16)
    nsqp = si("nsqp", [48, TILE])
    nsqq = nsqp[0:16, :]; nsqk = nsqp[32:48, :]
    negl = nsqp[0:16, :]
    rnq = si("rnq", [16, TILE], BF16)
    rnk = si("rnk", [16, TILE], BF16)
    lap = si("lap", [112, TILE])
    ltile = lap[0:16, :]
    atile = lap[96:112, :]
    abti = si("abti", [16, C], BF16)
    btile = si("btile", [16, TILE], BF16)
    lhl = si("lhl", [48, TILE], BF16)
    lhi = lhl[0:16, :]; llo = lhl[32:48, :]
    nlhi = si("nlhi", [16, TILE], BF16)
    nllo = si("nllo", [16, TILE], BF16)
    lfl2 = si("lfl2", [2, 2 * H * C], BF16)
    eet = si("eet", [128, H * C], BF16)
    uu = eet[:, 0:H * HD]
    eegh = si("eegh", [128, 1024], BF16)
    rcc = si("rcc", [128, 16], BF16)
    ptraw = si("ptraw", [128, H * C], BF16)
    ptod = ptraw
    gtt = si("gtt", [128, H * C], BF16)
    dd = None
    ddt = si("ddt", [128, H * B], BF16)
    dp = {1: si("dpow1", [128, H * B], BF16), 2: si("dpow2", [128, H * B], BF16),
          16: si("dpow16", [128, H * B], BF16)}
    dd = dp[16]
    dtp = {1: ddt, 2: si("dt2", [128, H * B], BF16),
           4: si("dt4", [128, H * B], BF16), 8: si("dt8", [128, H * B], BF16)}
    ggl = si("ggl", [128, H * B], BF16)
    ggr = si("ggr", [128, H * B], BF16)
    txx = dp[16]
    tfin = ggr

    kqs0 = si("kqs0", [128, H * HD], BF16)
    vtm = si("vtm", [128, H * HD], BF16)
    khtm = si("khtm", [128, H * HD], BF16)
    rhs_ = si("rhs_", [128, H * HD], BF16)
    preb = kqs0

    upb = vtm
    ssf = si("ssf", [128, H * HD])
    ssbE = si("ssbE", [128, H * HD], BF16)
    ssbO = si("ssbO", [128, H * HD], BF16)
    sst = lntmp
    outst = si("outst", [128, DM], BF16)
    acdg = si("acdg", [16, 16], BF16)
    wtf = lntmp

    def head_rb(h):
        return 64 * (h % 2)

    def fchunk(t, h, cc, w=C):
        fb = h // 2
        rb = head_rb(h)
        return t[rb:rb + 64, fb * TILE + cc * C: fb * TILE + cc * C + w]

    with eso, esi:
        with nc.Block() as block:
            p = Prog(nc, block, sems)

            def ld_consts(e):
                ins = []
                for k in sc:
                    t_ = sc[k]
                    sl = tuple(slice(0, d) for d in t_.shape)
                    ins.append(e.dma_start(out=t_[sl], in_=hc[k][sl]))
                for k in sbias:
                    ins.append(e.dma_start(out=sbias[k][:, :], in_=hb[k][:, :]))
                ins.append(e.dma_start(out=sbrow[0:1, :], in_=hb["bo"][:, :]))
                ins.append(e.dma_start(out=sbrow[32:33, :], in_=hb["bffn"][:, :]))
                ins.append(e.dma_start(out=sba[:, :], in_=hb["ba"][:, :]))
                ins.append(e.dma_start(out=sbb[:, :], in_=hb["bb"][:, :]))
                ins.append(e.dma_start(out=sconv[:, :], in_=hconv[:, :]))
                return ins
            p.sec("dma", ld_consts, wr=["consts"], n_dma=len(sc) + len(sbias) + 5)

            def ld_w(e):
                ins = []
                for w, dst in (("wq", swq), ("wk", swk), ("wv", swv)):
                    for cc in range(8):
                        ins.append(e.dma_start(out=dst[:, cc * DM:(cc + 1) * DM],
                                               in_=hw[w][cc, :, :]))
                for w, dst in (("wa", swa), ("wb", swb)):
                    for cc in range(8):
                        ins.append(e.dma_start(out=dst[:, cc * H:(cc + 1) * H],
                                               in_=hw[w][cc, :, :]))
                return ins
            p.sec("gdma", ld_w, wr=["w1"], n_dma=3 * 8 + 2 * 8)

            def mk_init(e):
                e.memset(eet[:, :], 0.0)
                e.memset(ssf[:, :], 0.0)
                e.memset(ssbE[:, :], 0.0)
                e.memset(ssbO[:, :], 0.0)
                e.memset(halo[:, :], 0.0)
                e.memset(amaxc[:, :], 0.0)
                e.memset(sval[:, :], 0.0)
                e.memset(gnc[:, :], 0.0)
                e.memset(srow[:, :], 0.0)
                return e.tensor_copy(out=identf[:, :], in_=sc["IDENT128"][:, :])
            p.sec("dve", mk_init, rd=["consts"],
                  wr=["uu", "ssf", "ssb", "halo", "amaxc", "sval", "gnc", "srow", "identf"])

            def mk_diag(e):
                last = None
                for idx in range(64):
                    last = e.tensor_scalar(
                        out=sdiag[:, idx * 128:(idx + 1) * 128],
                        in0=sc["IDENT128"][:, :],
                        scalar1=sconv[:, idx:idx + 1],
                        scalar2=None, op0=ALU.mult)
                return last
            p.sec("gps", mk_diag, rd=["consts"], wr=["sdiag"])

            for cc in range(8):
                wbuf, wkey = (wtf, "lntmp") if cc % 2 == 0 else (xnfs, "xts")
                pw_, pwk = (pa, "pa") if cc % 2 == 0 else (pb, "pb")

                def ld_wf(e, cc=cc, wbuf=wbuf):
                    ins = []
                    for oc in range(8):
                        ins.append(e.dma_start(
                            out=wbuf[:, oc * 128:(oc + 1) * 128],
                            in_=hw["wffn"][oc, :, cc * 128:(cc + 1) * 128]))
                    return ins
                p.sec("dma", ld_wf, wr=[wkey], n_dma=8)

                def tp_wf(e, wbuf=wbuf, pw_=pw_):
                    last = None
                    for oc in range(8):
                        last = e.transpose(pw_[:, oc * 128:(oc + 1) * 128],
                                           wbuf[:, oc * 128:(oc + 1) * 128],
                                           identf[:, :])
                    return last
                p.sec("pe", tp_wf, rd=[wkey, "identf"], wr=[pwk])

                def ev_wf1(e, cc=cc, wbuf=wbuf, pw_=pw_):
                    return e.tensor_copy(out=wbuf[:, :], in_=pw_[:, :])
                p.sec("dve", ev_wf1, rd=[pwk], wr=[wkey])

                def ev_wf2(e, cc=cc, wbuf=wbuf):
                    return e.reduce_max(out=amaxc[:, 34 + (cc % 2) * 4:35 + (cc % 2) * 4],
                                        in_=wbuf[:, :], axis=AX.X,
                                        apply_absolute_value=True)
                p.sec("dve", ev_wf2, rd=[wkey], wr=["amaxw_t%d" % (cc % 2)])

                def ev_wf3(e, cc=cc):
                    return e.tensor_tensor(out=amaxc[:, 35:36], in0=amaxc[:, 35:36],
                                           in1=amaxc[:, 34 + (cc % 2) * 4:35 + (cc % 2) * 4],
                                           op=ALU.max)
                p.sec("dve", ev_wf3, rd=["amaxw_t%d" % (cc % 2), "amaxc"], wr=["amaxc"])

                def st_wf(e, cc=cc, wbuf=wbuf):
                    return e.dma_start(out=hwt[cc, :, :], in_=wbuf[:, :])
                p.sec("dma", st_wf, rd=[wkey], wr=["hwt_%d" % cc], n_dma=1)

            stats_on = [False]

            for T in range(n_tiles):
                for s_ in range(NSUB):
                    def ld_x(e, T=T, s_=s_):
                        return e.dma_start(
                            out=xts[:, :],
                            in_=hx[T * TILE + s_ * 128: T * TILE + (s_ + 1) * 128, :])
                    p.sec("dma", ld_x, wr=["xts"], n_dma=1)

                    def ln_a1(e):
                        return e.reduce_sum(out=lncol[:, 0:1], in_=xts[:, :], axis=AX.X)
                    p.sec("dve", ln_a1, rd=["xts"], wr=["lncol_a"])

                    def ln_a2(e):
                        return e.tensor_scalar(out=lncol[:, 1:2], in0=lncol[:, 0:1],
                                               scalar1=1.0 / DM, scalar2=None, op0=ALU.mult)
                    p.sec("dve", ln_a2, rd=["lncol_a"], wr=["lncol"])

                    def ln_b(e):
                        return e.activation(out=lntmp[:, :], in_=xts[:, :], func=AF.Square,
                                            accum_out=lncol[:, 2:3])
                    p.sec("act", ln_b, rd=["xts"], wr=["lntmp", "lncol2"])

                    def ln_c1(e):
                        return e.tensor_scalar(out=lncol[:, 3:4], in0=lncol[:, 1:2],
                                               scalar1=lncol[:, 1:2], scalar2=None, op0=ALU.mult)
                    p.sec("dve", ln_c1, rd=["lncol"], wr=["lncol_c"])

                    def ln_c2(e):
                        return e.tensor_scalar(out=lncol[:, 4:5], in0=lncol[:, 2:3],
                                               scalar1=1.0 / DM, scalar2=lncol[:, 3:4],
                                               op0=ALU.mult, op1=ALU.subtract)
                    p.sec("dve", ln_c2, rd=["lncol_c", "lncol2"], wr=["lncol"])

                    def ln_d(e):
                        return e.activation(out=lncol[:, 5:6], in_=lncol[:, 4:5],
                                            func=AF.Sqrt, bias=1e-5)
                    p.sec("act", ln_d, rd=["lncol"], wr=["lncol3"])

                    def ln_e1(e):
                        return e.reciprocal(out=lncol[:, 6:7], in_=lncol[:, 5:6])
                    p.sec("dve", ln_e1, rd=["lncol3"], wr=["lncol_e"])

                    def ln_e2(e):
                        return e.tensor_scalar(out=lntmp[:, :], in0=xts[:, :],
                                               scalar1=lncol[:, 1:2], scalar2=lncol[:, 6:7],
                                               op0=ALU.subtract, op1=ALU.mult)
                    p.sec("dve", ln_e2, rd=["xts", "lncol", "lncol_e"], wr=["lntmp"])

                    def ln_e3(e, T=T, s_=s_):
                        return e.reduce_max(out=amaxc[:, T * NSUB + s_:T * NSUB + s_ + 1],
                                            in_=lntmp[:, :], axis=AX.X,
                                            apply_absolute_value=True)
                    p.sec("dve", ln_e3, rd=["lntmp"], wr=["amaxc"])

                    psx, pkx = (pa, "pa") if s_ % 2 == 0 else (pb, "pb")

                    def tp_x(e, psx=psx):
                        last = None
                        for fb in range(8):
                            last = e.transpose(psx[:, fb * 128:(fb + 1) * 128],
                                               lntmp[:, fb * 128:(fb + 1) * 128],
                                               identf[:, :])
                        return last
                    p.sec("pe", tp_x, rd=["lntmp", "identf"], wr=[pkx])

                    def ev_f(e, psx=psx):
                        return e.tensor_copy(out=xnfs[:, :], in_=psx[:, :])
                    p.sec("dve", ev_f, rd=[pkx], wr=["xts"])

                    def ev_b(e, s_=s_, psx=psx):
                        ap = xnfb[:, :].rearrange("p (fb t) -> p fb t", fb=8)[:, :, s_ * 128:(s_ + 1) * 128]
                        return e.tensor_copy(out=ap,
                                             in_=psx[:, :].rearrange("p (fb t) -> p fb t", fb=8))
                    p.sec("dve", ev_b, rd=[pkx], wr=["xnfb_%d" % s_])

                    def st_xnf(e, T=T, s_=s_):
                        ap = hxnf[T, :, :].rearrange("p (fb t) -> p fb t", fb=8)[:, :, s_ * 128:(s_ + 1) * 128]
                        return e.dma_start(out=ap, in_=xnfs[:, :].rearrange("p (fb t) -> p fb t", fb=8))
                    p.sec("dma", st_xnf, rd=["xts"], wr=["hxnf_%d_%d" % (T, s_)], n_dma=1)

                xnfb_r = ["xnfb_%d" % s_ for s_ in range(NSUB)]
                if "xnf" in dbg and T == 0:
                    p.sec("dma", lambda e: e.dma_start(out=dbg["xnf"][:, :], in_=hxnf[0, :, :]),
                          rd=["hxnf_0_0", "hxnf_0_1", "hxnf_0_2", "hxnf_0_3"], n_dma=1)

                for pi, (pname, wten, bkey, dstF) in enumerate(
                        (("q", swq, "bq", qfn), ("k", swk, "bk", kfn), ("v", swv, "bv", vf))):
                    for fbp in range(4):
                        ps_, pk_ = (pa, "pa") if fbp % 2 == 0 else (pb, "pb")

                        def mm_p(e, wten=wten, fbp=fbp, ps_=ps_):
                            last = None
                            for half in range(2):
                                fb = fbp * 2 + half
                                for cc in range(8):
                                    last = e.matmul(
                                        ps_[:, half * 512:(half + 1) * 512],
                                        wten[:, cc * DM + fb * 128: cc * DM + (fb + 1) * 128],
                                        xnfb[:, cc * 512:(cc + 1) * 512],
                                        start=(cc == 0), stop=(cc == 7))
                            return last
                        p.sec("pe", mm_p, rd=xnfb_r + ["w1"], wr=[pk_])

                        def ev_p(e, bkey=bkey, fbp=fbp, ps_=ps_):
                            last = None
                            for half in range(2):
                                fb = fbp * 2 + half
                                last = e.tensor_scalar(
                                    out=pre[:, fb * P3 + 3: fb * P3 + 3 + TILE],
                                    in0=ps_[:, half * 512:(half + 1) * 512],
                                    scalar1=sbias[bkey][:, fb:fb + 1], scalar2=None,
                                    op0=ALU.add)
                            return last
                        p.sec("dve", ev_p, rd=[pk_, "consts"], wr=["pre_%d" % fbp])

                    def h_in(e, pi=pi):
                        ap = pre[:, :].rearrange("p (fb t) -> p fb t", fb=8)[:, :, 0:3]
                        hap = halo[:, :].rearrange("p (pr fbt) -> p pr fbt", pr=3)[:, pi]\
                            .rearrange("p (fb t) -> p fb t", fb=8)
                        return e.tensor_copy(out=ap, in_=hap)
                    p.sec("dve", h_in, rd=["halo"], wr=["prehalo"],
                          extra=[p.res["pre_%d" % f]["w"] for f in range(4)])

                    if pi < 2:
                        for fbp in range(4):
                            ps_, pk_ = (pb, "pb") if fbp % 2 == 0 else (pa, "pa")

                            def mm_c(e, pi=pi, fbp=fbp, ps_=ps_):
                                last = None
                                for half in range(2):
                                    fb = fbp * 2 + half
                                    for tap in range(4):
                                        idx = (pi * 8 + fb) * 4 + tap
                                        last = e.matmul(
                                            ps_[:, half * 512:(half + 1) * 512],
                                            sdiag[:, idx * 128:(idx + 1) * 128],
                                            pre[:, fb * P3 + tap: fb * P3 + tap + TILE],
                                            start=(tap == 0), stop=(tap == 3))
                                return last
                            p.sec("pe", mm_c, rd=["prehalo", "sdiag", "pre_%d" % fbp], wr=[pk_])

                            def ev_c(e, dstF=dstF, fbp=fbp, ps_=ps_):
                                last = None
                                for half in range(2):
                                    fb = fbp * 2 + half
                                    last = e.activation(out=dstF[:, fb * 512:(fb + 1) * 512],
                                                        in_=ps_[:, half * 512:(half + 1) * 512],
                                                        func=AF.Silu)
                                return last
                            p.sec("act", ev_c, rd=[pk_], wr=["F%d_%d" % (pi, fbp)])
                    else:
                        for fb in range(8):
                            def cv_v0(e, fb=fb):
                                base = (2 * 8 + fb) * 4
                                return e.tensor_scalar(
                                    out=vcv[:, :],
                                    in0=pre[:, fb * P3 + 0: fb * P3 + TILE],
                                    scalar1=sconv[:, base:base + 1],
                                    scalar2=None, op0=ALU.mult)
                            p.sec("dve", cv_v0, rd=["prehalo", "consts", "pre_%d" % (fb // 2)],
                                  wr=["vcv"])
                            for tap in range(1, 4):
                                def cv_vt(e, fb=fb, tap=tap):
                                    base = (2 * 8 + fb) * 4
                                    return e.scalar_tensor_tensor(
                                        out=vcv[:, :],
                                        in0=pre[:, fb * P3 + tap: fb * P3 + tap + TILE],
                                        scalar=sconv[:, base + tap:base + tap + 1],
                                        in1=vcv[:, :], op0=ALU.mult, op1=ALU.add)
                                p.sec("dve", cv_vt,
                                      rd=["prehalo", "consts", "pre_%d" % (fb // 2), "vcv"],
                                      wr=["vcv"])

                            def sl_v(e, dstF=dstF, fb=fb):
                                return e.activation(out=dstF[:, fb * 512:(fb + 1) * 512],
                                                    in_=vcv[:, :], func=AF.Silu)
                            p.sec("act", sl_v, rd=["vcv"], wr=["F%d_%d" % (pi, fb // 2)])

                    def h_out(e, pi=pi):
                        ap = pre[:, :].rearrange("p (fb t) -> p fb t", fb=8)[:, :, TILE:TILE + 3]
                        hap = halo[:, :].rearrange("p (pr fbt) -> p pr fbt", pr=3)[:, pi]\
                            .rearrange("p (fb t) -> p fb t", fb=8)
                        return e.tensor_copy(out=hap, in_=ap)
                    p.sec("dve", h_out, rd=["pre_0", "pre_1", "pre_2", "pre_3"], wr=["halo"])

                    if pi < 2:
                        nsqt = nsqq if pi == 0 else nsqk
                        for fb in range(8):
                            def sq_f(e, dstF=dstF, fb=fb):
                                return e.tensor_mul(ksq[:, :], dstF[:, fb * 512:(fb + 1) * 512],
                                                    dstF[:, fb * 512:(fb + 1) * 512])
                            p.sec("gps", sq_f, rd=["F%d_%d" % (pi, fb // 2)], wr=["ksq"])

                            def mm_n(e, fb=fb):
                                return e.matmul(pc[0:16, 0:512], sc["IND2"][:, fb * H:(fb + 1) * H],
                                                ksq[:, :], start=(fb == 0), stop=(fb == 7))
                            p.sec("pe", mm_n, rd=["ksq", "consts"],
                                  wr=["pc"] if fb == 7 else [])

                        def ev_n(e, nsqt=nsqt):
                            return e.activation(out=nsqt[:, :], in_=pc[0:16, 0:512], func=AF.Sqrt)
                        p.sec("act", ev_n, rd=["pc"], wr=["nsq_%d" % pi])

                def rec_n1(e):
                    e.tensor_scalar(out=nsqq[:, :], in0=nsqq[:, :], scalar1=1e-6,
                                    scalar2=None, op0=ALU.add)
                    return e.tensor_scalar(out=nsqk[:, :], in0=nsqk[:, :], scalar1=1e-6,
                                           scalar2=None, op0=ALU.add)
                p.sec("dve", rec_n1, rd=["nsq_0", "nsq_1"], wr=["rn_a"])

                def rec_n2(e):
                    e.reciprocal(out=nsqq[:, :], in_=nsqq[:, :])
                    return e.reciprocal(out=nsqk[:, :], in_=nsqk[:, :])
                p.sec("dve", rec_n2, rd=["rn_a"], wr=["rn_b"])

                def rec_n3(e):
                    e.tensor_copy(out=rnq[:, :], in_=nsqq[:, :])
                    return e.tensor_copy(out=rnk[:, :], in_=nsqk[:, :])
                p.sec("dve", rec_n3, rd=["rn_b"], wr=["rn"])

                for ji, (srcT, dstT, vec) in enumerate(((qfn, qfn, rnq), (kfn, kfn, rnk))):
                    for fbp in range(4):
                        def mm_r(e, vec=vec, fbp=fbp):
                            last = None
                            for half in range(2):
                                fb = fbp * 2 + half
                                last = e.matmul(pa[:, half * 512:(half + 1) * 512],
                                                sc["INDT"][:, fb * 128:(fb + 1) * 128],
                                                vec[:, :], start=True, stop=True)
                            return last
                        p.sec("pe", mm_r, rd=["rn", "consts"], wr=["pa"])

                        def ap_r(e, srcT=srcT, dstT=dstT, fbp=fbp):
                            last = None
                            for half in range(2):
                                fb = fbp * 2 + half
                                last = e.tensor_tensor(out=dstT[:, fb * 512:(fb + 1) * 512],
                                                       in0=srcT[:, fb * 512:(fb + 1) * 512],
                                                       in1=pa[:, half * 512:(half + 1) * 512],
                                                       op=ALU.mult)
                            return last
                        wkey = ("Fq_n" if ji == 0 else "Fk_n") + "_%d" % fbp
                        rkey = ["F0_%d" % fbp] if ji == 0 else ["F1_%d" % fbp]
                        p.sec("dve", ap_r, rd=["pa"] + rkey, wr=[wkey])
                qfn_r = ["Fq_n_%d" % f for f in range(4)]
                kfn_r = ["Fk_n_%d" % f for f in range(4)]
                khf_r = ["khfc"]
                vf_r = ["F2_%d" % f for f in range(4)]

                def mm_ab(e):
                    last = None
                    for cc in range(8):
                        e.matmul(pb[0:16, 0:512], swa[:, cc * H:(cc + 1) * H],
                                 xnfb[:, cc * 512:(cc + 1) * 512],
                                 start=(cc == 0), stop=(cc == 7))
                    for cc in range(8):
                        last = e.matmul(pb[0:16, 512:1024], swb[:, cc * H:(cc + 1) * H],
                                        xnfb[:, cc * 512:(cc + 1) * 512],
                                        start=(cc == 0), stop=(cc == 7))
                    return last
                p.sec("pe", mm_ab, rd=xnfb_r + ["w1"], wr=["pb"])

                def ev_ab(e):
                    e.activation(out=atile[:, :], in_=pb[0:16, 0:512], func=AF.Sigmoid,
                                 bias=sba[:, 0:1])
                    return e.activation(out=btile[:, :], in_=pb[0:16, 512:1024],
                                        func=AF.Sigmoid, bias=sbb[:, 0:1])
                p.sec("act", ev_ab, rd=["pb", "consts"], wr=["atile", "btile"])

                def ev_ab2(e):
                    return e.activation(out=ltile[:, :], in_=atile[:, :], func=AF.Ln)
                p.sec("act", ev_ab2, rd=["atile"], wr=["ltile"])

                def cum_l1(e):
                    return e.tensor_tensor_scan(out=ltile[:, :], data0=ltile[:, :],
                                                data1=ltile[:, :], initial=0.0,
                                                op0=ALU.add, op1=ALU.min)
                p.sec("dve", cum_l1, rd=["ltile", "rn"], wr=["ltile"])

                def cum_l2(e):
                    e.tensor_scalar(out=negl[:, :], in0=ltile[:, :], scalar1=-1.0,
                                    scalar2=None, op0=ALU.mult)
                    return e.tensor_copy(out=lhi[:, :], in_=ltile[:, :])
                p.sec("dve", cum_l2, rd=["ltile"], wr=["negl", "lhi"])

                def cum_l3(e):
                    e.tensor_scalar(out=nlhi[:, :], in0=lhi[:, :], scalar1=-1.0,
                                    scalar2=None, op0=ALU.mult)
                    return e.tensor_tensor(out=sst[0:16, 0:TILE], in0=ltile[:, :],
                                           in1=lhi[:, :], op=ALU.subtract)
                p.sec("dve", cum_l3, rd=["ltile", "lhi"], wr=["nlhi_r", "lntmp"])

                def cum_l4(e):
                    return e.tensor_copy(out=llo[:, :], in_=sst[0:16, 0:TILE])
                p.sec("dve", cum_l4, rd=["lntmp"], wr=["llo_r"])

                def cum_l5(e):
                    return e.tensor_scalar(out=nllo[:, :], in0=llo[:, :], scalar1=-1.0,
                                           scalar2=None, op0=ALU.mult)
                p.sec("dve", cum_l5, rd=["llo_r"], wr=["lhl"],
                      extra=[p.res["nlhi_r"]["w"]])
                if "ltile" in dbg and T == 0:
                    p.sec("dma", lambda e: e.dma_start(out=dbg["ltile"][:, :], in_=ltile[:, :]),
                          rd=["ltile"], n_dma=1)

                def st_lhl(e):
                    a_ = e.dma_start(out=hlhl[0:16, :], in_=lhl[0:16, :])
                    b_ = e.dma_start(out=hlhl[32:48, :], in_=lhl[32:48, :])
                    return [a_, b_]
                p.sec("dma", st_lhl, rd=["lhl", "llo_r", "nlhi_r", "lhi"], wr=["hlhl"], n_dma=2)


                for nm, t_, rr in (("qfn", qfn, qfn_r), ("kfn", kfn, kfn_r),
                                   ("vf", vf, vf_r)):
                    if nm in dbg and T == 0:
                        p.sec("dma", (lambda t_, nm: lambda e: e.dma_start(out=dbg[nm][:, :], in_=t_[:, :]))(t_, nm),
                              rd=rr, n_dma=1)

                if not do_attn:
                    continue
                for cc in range(NCH):
                    p.stage_limit = chunk_upto
                    first_chunk = (T == 0 and cc == 0)
                    qr = qfn_r + kfn_r + khf_r + vf_r + ["ltile", "negl", "lflat"]

                    def mk_a(e, cc=cc):
                        if cc > 0:
                            return e.activation(out=atile[:, 0:C],
                                                in_=ltile[:, cc * C:(cc + 1) * C],
                                                func=AF.Exp, bias=negl[:, cc * C - 1:cc * C])
                        return e.activation(out=atile[:, 0:C], in_=ltile[:, 0:C], func=AF.Exp)
                    p.sec("act", mk_a, rd=["ltile", "negl"], wr=["achk_a"])

                    def mk_a2(e):
                        return e.tensor_copy(out=abti[:, :], in_=atile[:, 0:C])
                    p.sec("dve", mk_a2, rd=["achk_a"], wr=["achk"])

                    p.stage_cur = 1
                    if cc % 2 == 0:
                        def ld_lfl(e, cc=cc):
                            h4 = hlhl.rearrange("(o h) (c t) -> o h c t", o=3, c=NCH)
                            a_ = e.dma_start(
                                out=lfl2[0:1, :].rearrange("o (c h t) -> o c h t", c=2, h=16),
                                in_=h4[0:1, 0:16, cc:cc + 2, :].rearrange("o h c t -> o c h t"),
                                allow_slow_non_contiguous=True)
                            b_ = e.dma_start(
                                out=lfl2[1:2, :].rearrange("o (c h t) -> o c h t", c=2, h=16),
                                in_=h4[2:3, 0:16, cc:cc + 2, :].rearrange("o h c t -> o c h t"),
                                allow_slow_non_contiguous=True)
                            return [a_, b_]
                        p.sec("gdma", ld_lfl, rd=["hlhl"], wr=["lflat"], n_dma=2)
                    def mm_bhrep(e, cc=cc):
                        last = None
                        for fb in range(8):
                            last = e.matmul(pb[:, fb * 128:(fb + 1) * 128],
                                            sc["INDT"][:, fb * 128:(fb + 1) * 128],
                                            btile[:, cc * C:(cc + 1) * C],
                                            start=True, stop=True)
                        return last
                    p.sec("pe", mm_bhrep, rd=["btile", "consts"] + kfn_r, wr=["pb"])

                    def mk_khfc(e, cc=cc):
                        return e.tensor_tensor(
                            out=khfc[:, :],
                            in0=kfn[:, :].rearrange("p (fb t) -> p fb t", fb=8)[:, :, cc * C:(cc + 1) * C],
                            in1=pb[:, :].rearrange("p (fb t) -> p fb t", fb=8),
                            op=ALU.mult)
                    p.sec("dve", mk_khfc, rd=["pb"] + kfn_r, wr=["khfc"])

                    def mk_khfceo(e):
                        e.tensor_scalar(out=khfcE[:, :], in0=khfc[:, :],
                                        scalar1=sc["CVALS"][:, 2:3], scalar2=None,
                                        op0=ALU.mult)
                        return e.tensor_scalar(out=khfcO[:, :], in0=khfc[:, :],
                                               scalar1=sc["CVALS"][:, 3:4], scalar2=None,
                                               op0=ALU.mult)
                    p.sec("dve", mk_khfceo, rd=["khfc", "consts"], wr=["khfc2"])

                    def khsl(h, w=C):
                        rb = head_rb(h)
                        return khfc[rb:rb + 64, (h // 2) * 128:(h // 2) * 128 + w]

                    p.stage_cur = 2
                    def mm_ks0(e, cc=cc):
                        last = None
                        for h in range(H):
                            fb = h // 2
                            ssx = ssbE if h % 2 == 0 else ssbO
                            last = e.matmul(pa[:, h * HD:(h + 1) * HD],
                                            kfn[:, fb * TILE + cc * C: fb * TILE + cc * C + C],
                                            ssx[:, h * HD:(h + 1) * HD],
                                            start=True, stop=True)
                        return last
                    p.sec("pe", mm_ks0, rd=kfn_r + ["ssb"], wr=["pa"])

                    def ev_ks0(e):
                        return e.tensor_copy(out=kqs0[:, :], in_=pa[:, :])
                    p.sec("dve", ev_ks0, rd=["pa"], wr=["kqs0"])

                    def mm_qs0(e, cc=cc):
                        last = None
                        for h in range(H):
                            fb = h // 2
                            ssx = ssbE if h % 2 == 0 else ssbO
                            last = e.matmul(pb[:, h * HD:(h + 1) * HD],
                                            qfn[:, fb * TILE + cc * C: fb * TILE + cc * C + C],
                                            ssx[:, h * HD:(h + 1) * HD],
                                            start=True, stop=True)
                        return last
                    p.sec("pe", mm_qs0, rd=qfn_r + ["ssb"], wr=["pb"])

                    p.stage_cur = 3
                    def mm_arep2(e):
                        e.matmul(pa[:, 0:512], abti[:, :], sc["IND64"][:, 0:512],
                                 start=True, stop=True)
                        return e.matmul(pa[:, 512:1024], abti[:, :], sc["IND64"][:, 512:1024],
                                        start=True, stop=True)
                    p.sec("pe", mm_arep2, rd=["achk", "consts", "kqs0"], wr=["pa"])

                    p.stage_cur = 4
                    def tp_vk(e, cc=cc):
                        last = None
                        for fb in range(8):
                            e.transpose(pt8[:, fb * 128:(fb + 1) * 128],
                                        vf[:, fb * TILE + cc * C: fb * TILE + cc * C + C],
                                        sc["IDENT128"][:, :])
                            last = e.transpose(pt8[:, 1024 + fb * 128: 1024 + (fb + 1) * 128],
                                               khfc[:, fb * 128:(fb + 1) * 128],
                                               sc["IDENT128"][:, :])
                        return last
                    p.sec("pe", tp_vk, rd=vf_r + khf_r + ["consts"], wr=["pt8"])

                    def ev_vk(e):
                        e.tensor_copy(out=vtm[:, :], in_=pt8[:, 0:1024])
                        return e.tensor_copy(out=khtm[:, :], in_=pt8[:, 1024:2048])
                    p.sec("dve", ev_vk, rd=["pt8"], wr=["vtm", "khtm"])

                    p.stage_cur = 5
                    def mk_rhs1(e):
                        return e.tensor_tensor(out=sst[:, :], in0=pa[:, :], in1=kqs0[:, :],
                                               op=ALU.mult)
                    p.sec("dve", mk_rhs1, rd=["pa", "kqs0"], wr=["lntmp"])

                    def mk_rhs2(e):
                        return e.tensor_tensor(out=rhs_[:, :], in0=vtm[:, :], in1=sst[:, :],
                                               op=ALU.subtract)
                    p.sec("dve", mk_rhs2, rd=["vtm", "lntmp"], wr=["rhs"])

                    def mk_oa1(e):
                        return e.tensor_copy(out=kqs0[:, :], in_=pb[:, :])
                    p.sec("dve", mk_oa1, rd=["pb"], wr=["kqs0"])

                    def mk_oa2(e):
                        return e.tensor_tensor(out=outst[:, :], in0=pa[:, :], in1=kqs0[:, :],
                                               op=ALU.mult)
                    p.sec("dve", mk_oa2, rd=["pa", "kqs0"], wr=["oa", "outst"])

                    p.stage_cur = 6
                    def mm_ld(e, cc=cc):
                        last = None
                        for gh, ps in ((0, pb), (1, pa)):
                            for q in range(2):
                                off = (cc % 2) * 2048 + gh * 1024 + q * 512
                                tgt = ps[:, q * 512:(q + 1) * 512]
                                e.matmul(tgt, sc["ONES2"][:, :],
                                         lfl2[0:2, off:off + 512], start=True, stop=False)
                                e.matmul(tgt, nlhi[:, cc * C:(cc + 1) * C],
                                         sc["INDBD"][:, gh * 1024 + q * 512: gh * 1024 + (q + 1) * 512],
                                         start=False, stop=False)
                                last = e.matmul(tgt, nllo[:, cc * C:(cc + 1) * C],
                                                sc["INDBD"][:, gh * 1024 + q * 512: gh * 1024 + (q + 1) * 512],
                                                start=False, stop=True)
                        return last
                    p.sec("pe", mm_ld, rd=["lflat", "lhl", "nlhi_r", "consts", "oa"],
                          wr=["pb", "pa"])

                    def ee_a(e):
                        e.tensor_scalar(out=sst[:, :], in0=pb[:, :], scalar1=0.0,
                                        scalar2=None, op0=ALU.min)
                        return e.tensor_scalar(out=xts[:, :], in0=pa[:, :], scalar1=0.0,
                                               scalar2=None, op0=ALU.min)
                    p.sec("dve", ee_a, rd=["pb", "pa"], wr=["lntmp", "xts"])

                    def ee_b(e):
                        e.activation(out=eet[:, 0:1024], in_=sst[:, :], func=AF.Exp)
                        return e.activation(out=eet[:, 1024:2048], in_=xts[:, :],
                                            func=AF.Exp)
                    p.sec("act", ee_b, rd=["lntmp", "xts"], wr=["ee_0", "ee_1"])

                    p.stage_cur = 7
                    def mk_rcc(e):
                        return e.tensor_copy(
                            out=rcc[:, :],
                            in_=eet[:, :].rearrange("p (h t) -> p h t", h=16)[:, :, C - 1:C])
                    p.sec("dve", mk_rcc, rd=["ee_0", "ee_1"], wr=["rcc"])
                    if "ee" in dbg and first_chunk:
                        p.sec("dma", lambda e: e.dma_start(out=dbg["ee"][:, :], in_=eet[:, :]),
                              rd=["ee_0", "ee_1"], n_dma=1)

                    p.stage_cur = 8
                    def mk_eep(e):
                        v3 = eet[:, :].rearrange("p (h t) -> p h t", h=16)
                        return e.tensor_tensor(out=v3, in0=v3,
                                               in1=sc["MASK1"][:, :].rearrange("p (a t) -> p a t", a=1)
                                               .broadcast_to((128, 16, 128)),
                                               op=ALU.mult)
                    p.sec("dve", mk_eep, rd=["rcc", "consts"],
                          wr=["eep", "ee_0", "ee_1"])

                    p.stage_cur = 9
                    def mm_kqx(e, cc=cc, src=kfn):
                        last = None
                        for gh, ps in ((0, pb), (1, pa)):
                            for hh in range(8):
                                h = gh * 8 + hh
                                fb = h // 2
                                khx = khfcE if h % 2 == 0 else khfcO
                                last = e.matmul(ps[:, hh * 128:(hh + 1) * 128],
                                                khx[:, fb * 128:(fb + 1) * 128],
                                                src[:, fb * TILE + cc * C: fb * TILE + cc * C + C],
                                                start=True, stop=True)
                        return last
                    p.sec("pe", lambda e, cc=cc: mm_kqx(e, cc, kfn),
                          rd=["khfc2"] + kfn_r + ["rhs", "lntmp", "xts"],
                          wr=["pb", "pa"])

                    def mk_pt(e):
                        e.tensor_tensor(out=ptraw[:, 0:1024], in0=eet[:, 0:1024],
                                        in1=pb[:, :], op=ALU.mult)
                        return e.tensor_tensor(out=ptraw[:, 1024:2048], in0=eet[:, 1024:2048],
                                               in1=pa[:, :], op=ALU.mult)
                    p.sec("dve", mk_pt, rd=["pb", "pa", "eep"], wr=["ptraw_0", "ptraw_1"])

                    p.sec("pe", lambda e, cc=cc: mm_kqx(e, cc, qfn),
                          rd=["khfc2"] + qfn_r + ["ptraw_0", "ptraw_1"], wr=["pb", "pa"])

                    for gh in range(2):
                        def mk_eeg(e, gh=gh):
                            return e.tensor_tensor(out=eegh[:, :].rearrange("p (h t) -> p h t", h=8),
                                                   in0=eet[:, gh * 1024:(gh + 1) * 1024].rearrange("p (h t) -> p h t", h=8),
                                                   in1=sc["IDT1"][:, :].rearrange("p (a t) -> p a t", a=1)
                                                   .broadcast_to((128, 8, 128)),
                                                   op=ALU.add)
                        p.sec("gps", mk_eeg, rd=["eep", "consts"], wr=["eegh"])

                        def mk_gt(e, gh=gh):
                            return e.tensor_tensor(out=gtt[:, gh * 1024:(gh + 1) * 1024],
                                                   in0=eegh[:, :],
                                                   in1=(pb if gh == 0 else pa)[:, :],
                                                   op=ALU.mult)
                        p.sec("dve", mk_gt, rd=["pb", "pa", "eegh"], wr=["gt_%d" % gh])

                    if "ptraw" in dbg and first_chunk:
                        p.sec("dma", lambda e: e.dma_start(out=dbg["ptraw"][:, :], in_=ptraw[:, :]),
                              rd=["ptraw_0", "ptraw_1"], n_dma=1)
                    if "gt" in dbg and first_chunk:
                        p.sec("dma", lambda e: e.dma_start(out=dbg["gt"][:, :], in_=gtt[:, :]),
                              rd=["gt_0", "gt_1"], n_dma=1)

                    p.stage_cur = 10
                    def mk_dd(e):
                        last = None
                        for i in range(4):
                            last = e.tensor_copy(
                                out=dd[32 * i:32 * i + 32, :].rearrange("p (h t) -> p h t", h=16),
                                in_=ptraw[32 * i:32 * i + 32, :].rearrange("p (h t) -> p h t", h=16)[:, :, 32 * i:32 * i + 32])
                        return last
                    p.sec("dve", mk_dd, rd=["ptraw_0", "ptraw_1"], wr=["dd"])

                    p.stage_cur = 11
                    def mk_ptod(e):
                        v3 = ptod[:, :].rearrange("p (h t) -> p h t", h=16)
                        return e.tensor_tensor(out=v3, in0=v3,
                                               in1=sc["NEGBLK1"][:, :].rearrange("p (a t) -> p a t", a=1)
                                               .broadcast_to((128, 16, 128)),
                                               op=ALU.mult)
                    p.sec("gps", mk_ptod, rd=["ptraw_0", "ptraw_1", "dd", "consts"], wr=["ptod"])

                    p.stage_cur = 12
                    def vtp(e, dst, src):
                        return e.transpose(out=dst[:, :], in_=src[:, :])

                    p.sec("dve", lambda e: vtp(e, ddt, dd), rd=["dd"], wr=["ddt"])

                    sqr_seq = [(dd, ddt, dp[1], dtp[2], "d2"),
                               (dp[1], dtp[2], dp[2], dtp[4], "d4"),
                               (dp[2], dtp[4], dp[1], dtp[8], "d8")]
                    iddb = (sc["IDD1"][:, :].rearrange("p (a t) -> p a t", a=1)
                            .broadcast_to((128, 16, 32)))

                    def subIDD(e, dst, src):
                        return e.tensor_tensor(out=dst[:, :].rearrange("p (h t) -> p h t", h=16),
                                               in0=iddb,
                                               in1=src[:, :].rearrange("p (h t) -> p h t", h=16),
                                               op=ALU.subtract)

                    def mm_sq(e, srcT, src, out):
                        last = None
                        for i in range(4):
                            for h in range(H):
                                last = e.matmul(out[32 * i:32 * i + 32, h * 32:(h + 1) * 32],
                                                srcT[32 * i:32 * i + 32, h * 32:(h + 1) * 32],
                                                src[32 * i:32 * i + 32, h * 32:(h + 1) * 32],
                                                start=True, stop=True,
                                                tile_position=(32 * i, 32 * i))
                        return last
                    mm_prod = mm_sq
                    tAt = eegh[:, 512:1024]
                    p.sec("pe", lambda e: mm_sq(e, ddt, dd, pc), rd=["dd", "ddt"], wr=["pc"])
                    p.sec("dve", lambda e: e.tensor_copy(out=dp[1][:, :], in_=pc[:, :]),
                          rd=["pc"], wr=["d2"])
                    p.sec("dve", lambda e: vtp(e, dtp[2], dp[1]), rd=["d2"], wr=["d2t"])
                    p.sec("gps", lambda e: subIDD(e, ggl, ddt), rd=["ddt", "consts"], wr=["ggl"])
                    p.sec("gps", lambda e: subIDD(e, ggr, dp[1]), rd=["d2", "consts"], wr=["ggr"])
                    p.sec("pe", lambda e: mm_prod(e, ggl, ggr, pb[:, 0:512]),
                          rd=["ggl", "ggr"], wr=["pbA", "pb"])
                    p.sec("dve", lambda e: e.tensor_copy(out=dp[2][:, :], in_=pb[:, 0:512]),
                          rd=["pbA", "pb"], wr=["aev"])
                    p.sec("dve", lambda e: vtp(e, tAt, dp[2]), rd=["aev"], wr=["tAt", "eegh"])
                    p.sec("pe", lambda e: mm_sq(e, dtp[2], dp[1], pc), rd=["d2", "d2t"], wr=["pc"])
                    p.sec("dve", lambda e: e.tensor_copy(out=dp[2][:, :], in_=pc[:, :]),
                          rd=["pc", "aev", "tAt"], wr=["d4"])
                    p.sec("dve", lambda e: vtp(e, dtp[4], dp[2]), rd=["d4"], wr=["d4t"])
                    p.sec("pe", lambda e: mm_sq(e, dtp[4], dp[2], pc), rd=["d4", "d4t"], wr=["pc"])
                    p.sec("dve", lambda e: e.tensor_copy(out=dp[1][:, :], in_=pc[:, :]),
                          rd=["pc"], wr=["d8"])
                    p.sec("dve", lambda e: vtp(e, dtp[8], dp[1]), rd=["d8"], wr=["d8t"])
                    p.sec("gps", lambda e: subIDD(e, ggl, dtp[4]), rd=["d4t", "consts"], wr=["ggl"])
                    p.sec("gps", lambda e: subIDD(e, ggr, dp[1]), rd=["d8", "consts"], wr=["ggr"])
                    p.sec("pe", lambda e: mm_prod(e, ggl, ggr, pb[:, 512:1024]),
                          rd=["ggl", "ggr"], wr=["pbB", "pb"])
                    p.sec("dve", lambda e: e.tensor_copy(out=ggl[:, :], in_=pb[:, 512:1024]),
                          rd=["pbB", "pb"], wr=["tB", "ggl"])
                    p.sec("pe", lambda e: mm_sq(e, dtp[8], dp[1], pc), rd=["d8", "d8t"], wr=["pc"])
                    p.sec("dve", lambda e: e.tensor_copy(out=dp[16][:, :], in_=pc[:, :]),
                          rd=["pc"], wr=["d16"])
                    p.sec("pe", lambda e: mm_prod(e, tAt, ggl, pb[:, 0:512]),
                          rd=["tAt", "tB", "pbA", "pb", "eegh"], wr=["pbAB", "pbA", "pb"])
                    p.sec("dve", lambda e: e.tensor_copy(out=dp[1][:, :], in_=pb[:, 0:512]),
                          rd=["pbAB", "pb", "d8", "d8t"], wr=["ab"])
                    p.sec("dve", lambda e: vtp(e, dtp[4], dp[1]), rd=["ab", "tB"], wr=["abt"])
                    p.sec("gps", lambda e: subIDD(e, ggr, dp[16]), rd=["d16", "consts"], wr=["ggr"])
                    p.sec("pe", lambda e: mm_prod(e, dtp[4], ggr, pc),
                          rd=["abt", "ggr", "d16", "pc"], wr=["pc"])
                    p.sec("dve", lambda e: e.tensor_copy(out=tfin[:, :], in_=pc[:, :]),
                          rd=["pc"], wr=["tfin"])
                    if "tfin" in dbg and first_chunk:
                        p.sec("dma", lambda e: e.dma_start(out=dbg["tfin"][:, :], in_=tfin[:, :]),
                              rd=["tfin"], n_dma=1)

                    p.stage_cur = 13
                    for j in range(4):
                        def mm_od(e, j=j):
                            last = None
                            for h in range(H):
                                last = e.matmul(pa[32 * j:32 * j + 32, h * HD:(h + 1) * HD],
                                                ptod[:, h * 128 + 32 * j: h * 128 + 32 * j + 32],
                                                uu[:, h * HD:(h + 1) * HD],
                                                start=True, stop=True,
                                                tile_position=(0, 32 * j))
                            return last
                        p.sec("pe", mm_od, rd=["ptod", "uu"], wr=["pa"])

                        def mk_pre(e, j=j):
                            return e.scalar_tensor_tensor(
                                out=preb[32 * j:32 * j + 32, :],
                                in0=pa[32 * j:32 * j + 32, :], scalar=1.0,
                                in1=rhs_[32 * j:32 * j + 32, :],
                                op0=ALU.mult, op1=ALU.add)
                        p.sec("dve", mk_pre, rd=["pa", "rhs"], wr=["preb"])

                        def mm_t32(e, j=j):
                            last = None
                            for h in range(H):
                                last = e.matmul(pb[32 * j:32 * j + 32, h * HD:(h + 1) * HD],
                                                tfin[32 * j:32 * j + 32, h * 32:(h + 1) * 32],
                                                preb[32 * j:32 * j + 32, h * HD:(h + 1) * HD],
                                                start=True, stop=True,
                                                tile_position=(32 * j, 32 * j))
                            return last
                        p.sec("pe", mm_t32, rd=["tfin", "preb"], wr=["pb"])

                        def ev_u(e, j=j):
                            return e.tensor_copy(out=uu[32 * j:32 * j + 32, :],
                                                 in_=pb[32 * j:32 * j + 32, :])
                        p.sec("dve", ev_u, rd=["pb"], wr=["uu"])
                    if "uu" in dbg and first_chunk:
                        p.sec("dma", lambda e: e.dma_start(out=dbg["uu"][:, :], in_=uu[:, :]),
                              rd=["uu"], n_dma=1)

                    p.stage_cur = 14
                    def mm_o(e):
                        last = None
                        for h in range(H):
                            last = e.matmul(pa[:, h * HD:(h + 1) * HD],
                                            gtt[:, h * 128:(h + 1) * 128],
                                            uu[:, h * HD:(h + 1) * HD],
                                            start=True, stop=True)
                        return last
                    p.sec("pe", mm_o, rd=["gt_0", "gt_1", "uu"], wr=["pa"])

                    def mk_of(e, cc=cc):
                        return e.tensor_tensor(out=outst[:, :], in0=outst[:, :], in1=pa[:, :],
                                               op=ALU.add)
                    p.sec("dve", mk_of, rd=["pa", "oa", "outst"], wr=["outst"])

                    p.stage_cur = 17 if chunk_upto >= 16 else 99
                    def st_outs(e, T=T, cc=cc):
                        return e.dma_start(out=houts[T * TILE + cc * C: T * TILE + (cc + 1) * C, :],
                                           in_=outst[:, :])
                    p.sec("dma", st_outs, rd=["outst"], wr=["houts_%d_%d" % (T, cc)], n_dma=1)
                    if "outs0" in dbg and first_chunk:
                        p.sec("dma", lambda e: e.dma_start(out=dbg["outs0"][:, :], in_=outst[:, :]),
                              rd=["outst"], n_dma=1)

                    p.stage_cur = 15
                    def sq_o(e):
                        return e.tensor_mul(eegh[:, :], outst[:, :], outst[:, :])
                    p.sec("gps", sq_o, rd=["outst"], wr=["eegh"])

                    def mm_st(e, first=not stats_on[0]):
                        e.matmul(pstat[0:1, 0:512], sc["ONESC"][:, :], outst[:, 0:512],
                                 start=first, stop=True, skip_group_check=True)
                        e.matmul(pstat[32:33, 0:512], sc["ONESC"][:, :], outst[:, 512:1024],
                                 start=first, stop=True, skip_group_check=True)
                        e.matmul(pstat[64:65, 0:512], sc["ONESC"][:, :], eegh[:, 0:512],
                                 start=first, stop=True, skip_group_check=True)
                        return e.matmul(pstat[96:97, 0:512], sc["ONESC"][:, :], eegh[:, 512:1024],
                                        start=first, stop=True, skip_group_check=True,
                                        tile_position=(0, 96))
                    p.sec("pe", mm_st, rd=["outst", "eegh", "consts"], wr=["pstat"])
                    stats_on[0] = True

                    p.stage_cur = 16
                    def mk_up(e):
                        return e.tensor_tensor(
                            out=upb[:, :].rearrange("p (h d) -> p h d", h=16),
                            in0=uu[:, :].rearrange("p (h d) -> p h d", h=16),
                            in1=rcc[:, :].rearrange("p (h a) -> p h a", a=1)
                                .broadcast_to((128, 16, 64)),
                            op=ALU.mult)
                    p.sec("dve", mk_up, rd=["uu", "rcc"], wr=["upb"])

                    def mk_acdg(e):
                        return e.tensor_tensor(
                            out=acdg[:, :], in0=sc["ID16"][:, :],
                            in1=abti[:, C - 1:C].broadcast_to((16, 16)),
                            op=ALU.mult)
                    p.sec("dve", mk_acdg, rd=["achk", "consts"], wr=["acdg"])

                    def mm_acv(e):
                        return e.matmul(pc[:, 0:16], sc["ONES16"][:, :], acdg[:, :],
                                        start=True, stop=True)
                    p.sec("pe", mm_acv, rd=["acdg", "consts", "tfin"], wr=["pc"])

                    def mm_dss(e):
                        last = None
                        for h in range(H):
                            for rb in (0, 64):
                                last = e.matmul(pa[rb:rb + 64, h * HD:(h + 1) * HD],
                                                khtm[:, h * HD:(h + 1) * HD],
                                                upb[:, h * HD:(h + 1) * HD],
                                                start=True, stop=True)
                        return last
                    p.sec("pe", mm_dss, rd=["khtm", "upb", "outst"], wr=["pa"])

                    def upd_ss1(e):
                        return e.tensor_tensor(
                            out=sst[:, :].rearrange("p (h d) -> p h d", h=16),
                            in0=ssf[:, :].rearrange("p (h d) -> p h d", h=16),
                            in1=pc[:, 0:16].rearrange("p (h a) -> p h a", a=1)
                            .broadcast_to((128, 16, 64)), op=ALU.mult)
                    p.sec("dve", upd_ss1, rd=["pc", "ssf"], wr=["lntmp"])

                    def upd_ss2(e):
                        return e.tensor_tensor(out=ssf[:, :], in0=sst[:, :], in1=pa[:, :],
                                               op=ALU.add)
                    p.sec("dve", upd_ss2, rd=["pa", "lntmp"], wr=["ssf"])

                    def cast_ss(e):
                        e.tensor_copy(out=ssbE[0:64, :], in_=ssf[0:64, :])
                        return e.tensor_copy(out=ssbO[64:128, :], in_=ssf[64:128, :])
                    p.sec("dve", cast_ss, rd=["ssf"], wr=["ssb"])
                    if "ssf" in dbg and T == 0 and cc == NCH - 1:
                        p.sec("dma", lambda e: e.dma_start(out=dbg["ssf"][:, :], in_=ssf[:, :]),
                              rd=["ssf"], n_dma=1)

            def amax_loc(e):
                return e.reduce_max(out=amaxc[:, 36:37], in_=amaxc[:, 0:32], axis=AX.X)
            p.sec("dve", amax_loc, rd=["amaxc"], wr=["amaxv"])

            def st_am(e):
                a = e.dma_start(out=hrow[0:1, :].rearrange("o (p a) -> (o p) a", a=1), in_=amaxc[:, 36:37],
                                allow_slow_non_contiguous=True)
                b = e.dma_start(out=hrow[1:2, :].rearrange("o (p a) -> (o p) a", a=1), in_=amaxc[:, 35:36],
                                allow_slow_non_contiguous=True)
                return [a, b]
            p.sec("dma", st_am, rd=["amaxv", "amaxc"], wr=["hrow"], n_dma=2)

            def ld_am(e):
                return e.dma_start(out=srow[0:2, :], in_=hrow[0:2, :])
            p.sec("dma", ld_am, rd=["hrow"], wr=["srow"], n_dma=1)

            def red_am(e):
                return e.reduce_max(out=sval[0:2, 0:1], in_=srow[0:2, :], axis=AX.X)
            p.sec("dve", red_am, rd=["srow"], wr=["sval"])

            def st_cc(e):
                return e.dma_start(out=hccin[:, :], in_=sval[0:1, 0:1])
            p.sec("dma", st_cc, rd=["sval"], wr=["hccin"], n_dma=1)

            if n_cores == 8:
                def do_cc(e):
                    return e.collective_compute(
                        "AllReduce", ALU.max, replica_groups=[[0, 1, 2, 3, 4, 5, 6, 7]],
                        ins=[hccin[:, :]], outs=[hccout[:, :]])
                p.sec("gps", do_cc, rd=["hccin"], wr=["hccout"])
            else:
                def do_cc(e):
                    return e.dma_start(out=hccout[:, :], in_=hccin[:, :])
                p.sec("dma", do_cc, rd=["hccin"], wr=["hccout"], n_dma=1)

            def ld_cc(e):
                return e.dma_start(out=sval[0:1, 1:2], in_=hccout[:, :])
            p.sec("dma", ld_cc, rd=["hccout"], wr=["sval_g"], n_dma=1)

            def st_wam(e):
                return e.dma_start(out=hwam[:, :], in_=sval[1:2, 0:1])
            p.sec("dma", st_wam, rd=["sval"], wr=["hwam"], n_dma=1)

            def ld_wam(e):
                return e.dma_start(out=sval[0:1, 5:6], in_=hwam[:, :])
            p.sec("dma", ld_wam, rd=["hwam"], wr=["sval_w"], n_dma=1)

            scale_steps = [
                lambda e: e.tensor_scalar(out=sval[0:1, 2:3], in0=sval[0:1, 1:2],
                                          scalar1=1e-12, scalar2=None, op0=ALU.max),
                lambda e: e.reciprocal(out=sval[0:1, 3:4], in_=sval[0:1, 2:3]),
                lambda e: e.tensor_scalar(out=sval[0:1, 4:5], in0=sval[0:1, 3:4],
                                          scalar1=224.0, scalar2=None, op0=ALU.mult),
                lambda e: e.tensor_scalar(out=sval[0:1, 9:10], in0=sval[0:1, 5:6],
                                          scalar1=1e-12, scalar2=None, op0=ALU.max),
                lambda e: e.reciprocal(out=sval[0:1, 10:11], in_=sval[0:1, 9:10]),
                lambda e: e.tensor_scalar(out=sval[0:1, 6:7], in0=sval[0:1, 10:11],
                                          scalar1=224.0, scalar2=None, op0=ALU.mult),
                lambda e: e.tensor_scalar(out=sval[0:1, 7:8], in0=sval[0:1, 4:5],
                                          scalar1=sval[0:1, 6:7], scalar2=None, op0=ALU.mult),
                lambda e: e.tensor_scalar(out=sval[0:1, 8:9], in0=sval[0:1, 2:3],
                                          scalar1=sval[0:1, 9:10], scalar2=None, op0=ALU.mult),
                lambda e: e.tensor_scalar(out=sval[0:1, 8:9], in0=sval[0:1, 8:9],
                                          scalar1=1.0 / (224.0 * 224.0), scalar2=None,
                                          op0=ALU.mult),
            ]
            for si_, stp in enumerate(scale_steps):
                p.sec("dve", stp, rd=["sval_g", "sval_w", "sval", "scales"], wr=["scales"])
            if "aval" in dbg:
                p.sec("dma", lambda e: e.dma_start(out=dbg["aval"][:, :], in_=sval[0:1, 4:8]),
                      rd=["scales"], n_dma=1)

            def mm_bcast(e):
                e.matmul(pc[:, 0:1], sc["ONESF"][:, :], sval[0:1, 4:5], start=True, stop=True)
                e.matmul(pc[:, 1:2], sc["ONESF"][:, :], sval[0:1, 8:9], start=True, stop=True)
                return e.matmul(pc[:, 2:3], sc["ONESF"][:, :], sval[0:1, 6:7],
                                start=True, stop=True)
            p.sec("pe", mm_bcast, rd=["scales", "consts"], wr=["pc"])

            def ev_bcast(e):
                return e.tensor_copy(out=sxcol[:, 0:3], in_=pc[:, 0:3])
            p.sec("dve", ev_bcast, rd=["pc"], wr=["sxcol"])

            if do_attn:
                def ev_stats(e):
                    e.tensor_copy(out=sst[0:1, 0:512], in_=pstat[0:1, 0:512])
                    e.tensor_copy(out=sst[32:33, 0:512], in_=pstat[32:33, 0:512])
                    e.tensor_copy(out=sst[64:65, 0:512], in_=pstat[64:65, 0:512])
                    return e.tensor_copy(out=sst[96:97, 0:512], in_=pstat[96:97, 0:512])
                p.sec("dve", ev_stats, rd=["pstat"], wr=["lntmp"])

                def red_stats(e):
                    last = None
                    order = ((0, 0), (32, 8), (64, 16), (96, 24))
                    for r, c0 in order:
                        last = e.tensor_reduce(out=gnc[0:1, c0:c0 + 8],
                                               in_=sst[r:r + 1, 0:512].rearrange("p (h d) -> p h d", h=8),
                                               axis=AX.X, op=ALU.add)
                    return last
                p.sec("dve", red_stats, rd=["lntmp"], wr=["gnc"])

                def tp_stats(e):
                    return e.transpose(out=gnc[0:32, 32:64], in_=gnc[0:32, 0:32])
                p.sec("dve", tp_stats, rd=["gnc"], wr=["gnc"])
                NTOK = float(n_tiles * TILE * HD)

                def mk_mu(e):
                    return e.tensor_scalar(out=gnc[0:32, 33:34], in0=gnc[0:32, 32:33],
                                           scalar1=1.0 / NTOK, scalar2=None, op0=ALU.mult)
                p.sec("dve", mk_mu, rd=["gnc"], wr=["gnc"])

                def st_gn(e):
                    return e.dma_start(out=hrow[2:3, 0:32].rearrange("o (p a) -> (o p) a", a=1),
                                       in_=gnc[0:32, 33:34],
                                       allow_slow_non_contiguous=True)
                p.sec("dma", st_gn, rd=["gnc"], wr=["hrow2"], n_dma=1)

                def ld_gn(e):
                    a = e.dma_start(out=gnr[:, 0:1],
                                    in_=hrow[2:3, 0:16].rearrange("o (f a) -> (o f) a", a=1),
                                    allow_slow_non_contiguous=True)
                    b = e.dma_start(out=gnr[:, 2:3],
                                    in_=hrow[2:3, 16:32].rearrange("o (f a) -> (o f) a", a=1),
                                    allow_slow_non_contiguous=True)
                    return [a, b]
                p.sec("dma", ld_gn, rd=["hrow2"], wr=["gnr"], n_dma=2)

                def mk_gn1a(e):
                    return e.tensor_scalar(out=gnr[:, 4:5], in0=gnr[:, 0:1],
                                           scalar1=gnr[:, 0:1], scalar2=None, op0=ALU.mult)
                p.sec("dve", mk_gn1a, rd=["gnr"], wr=["gnr_a"])

                def mk_gn1b(e):
                    return e.tensor_tensor(out=gnr[:, 4:5], in0=gnr[:, 2:3], in1=gnr[:, 4:5],
                                           op=ALU.subtract)
                p.sec("dve", mk_gn1b, rd=["gnr_a", "gnr"], wr=["gnr"])

                def mk_gn2(e):
                    return e.activation(out=gnr[:, 5:6], in_=gnr[:, 4:5], func=AF.Sqrt,
                                        bias=1e-5)
                p.sec("act", mk_gn2, rd=["gnr"], wr=["gnr"])

                def mk_vb0(e):
                    return e.reciprocal(out=gnr[:, 5:6], in_=gnr[:, 5:6])
                p.sec("dve", mk_vb0, rd=["gnr"], wr=["gnr_c"])

                def mk_vb1(e):
                    e.tensor_tensor(out=vbr[:, :], in0=sc["PATTERN16"][:, :],
                                    in1=gnr[:, 5:6].broadcast_to((16, 8)), op=ALU.mult)
                    return e.tensor_tensor(out=vbm[:, :], in0=sc["PATTERN16"][:, :],
                                           in1=gnr[:, 0:1].broadcast_to((16, 8)), op=ALU.mult)
                p.sec("dve", mk_vb1, rd=["gnr_c", "gnr", "consts"], wr=["vb"])

                def mk_indpf(e):
                    return e.tensor_copy(out=INDPF[:, :], in_=sc["INDP"][:, :])
                p.sec("dve", mk_indpf, rd=["consts"], wr=["indpf"])

                def mm_vb2(e):
                    e.matmul(pc[:, 4:12], INDPF[:, :], vbr[:, :], start=True, stop=True)
                    return e.matmul(pc[:, 12:20], INDPF[:, :], vbm[:, :], start=True, stop=True)
                p.sec("pe", mm_vb2, rd=["vb", "indpf"], wr=["pc"])

                def mk_gnf1(e):
                    return e.tensor_tensor(out=gnmul[:, :], in0=pc[:, 4:12],
                                           in1=sbias["gns"][:, :], op=ALU.mult)
                p.sec("dve", mk_gnf1, rd=["pc", "consts"], wr=["gnf_a"])

                def mk_gnf2(e):
                    return e.tensor_tensor(out=gnadd[:, :], in0=pc[:, 12:20], in1=gnmul[:, :],
                                           op=ALU.mult)
                p.sec("dve", mk_gnf2, rd=["pc", "gnf_a"], wr=["gnf_b"])

                def mk_gnf3(e):
                    return e.tensor_tensor(out=gnadd[:, :], in0=sbias["gnb"][:, :],
                                           in1=gnadd[:, :], op=ALU.subtract)
                p.sec("dve", mk_gnf3, rd=["gnf_b", "consts"], wr=["gnfin"])
                if "gnmul" in dbg:
                    p.sec("dma", lambda e: e.dma_start(out=dbg["gnmul"][:, :], in_=gnmul[:, :]),
                          rd=["gnfin"], n_dma=1)
                if "gnadd" in dbg:
                    p.sec("dma", lambda e: e.dma_start(out=dbg["gnadd"][:, :], in_=gnadd[:, :]),
                          rd=["gnfin"], n_dma=1)

            def mk_brow(e):
                e.tensor_copy(out=sbrow16[0:1, :], in_=sbrow[0:1, :])
                return e.tensor_scalar(out=sbrow16[32:33, :], in0=sbrow[32:33, :],
                                       scalar1=sval[0:1, 7:8], scalar2=None, op0=ALU.mult)
            p.sec("dve", mk_brow, rd=["consts", "scales"], wr=["brow"])

            p.flush()
            if do_phase2:
                esi.close()
                es2 = ExitStack()
                s2 = lambda n, s, dt=F32: es2.enter_context(nc.sbuf_tensor(n, s, dt))
                swg = s2("swg", [128, 8 * DM], BF16)
                swo = s2("swo", [128, 8 * DM], BF16)
                wt8 = s2("wt8", [128, 8 * DM], FP8)
                wtf2 = s2("wtf2", [128, DM])
                xnf2 = s2("xnf2", [128, 8 * TILE])
                xnfb2 = s2("xnfb2", [128, 8 * TILE], BF16)
                xq8 = s2("xq8", [128, 8 * TILE], FP8)
                ofm = s2("ofm", [128, 8 * TILE], BF16)
                gf = s2("gf", [128, 8 * TILE], BF16)
                yf = s2("yf", [128, 8 * TILE], BF16)
                gyf = s2("gyf", [128, 8 * TILE], BF16)
                xt2 = s2("xt2", [128, DM])
                outf = s2("outf", [128, DM])

                with es2:
                    def ld_w2(e):
                        ins = []
                        for w, dst in (("wg", swg), ("wo", swo)):
                            for cc2 in range(8):
                                ins.append(e.dma_start(out=dst[:, cc2 * DM:(cc2 + 1) * DM],
                                                       in_=hw[w][cc2, :, :]))
                        return ins
                    p.sec("gdma", ld_w2, wr=["w2"], n_dma=16)

                    for cc2 in range(8):
                        def ld_wt(e, cc2=cc2):
                            return e.dma_start(out=wtf2[:, :], in_=hwt[cc2, :, :])
                        p.sec("dma", ld_wt, rd=["hwt_%d" % cc2], wr=["wtf2"], n_dma=1)

                        def q_wt(e, cc2=cc2):
                            return e.tensor_scalar(out=wt8[:, cc2 * DM:(cc2 + 1) * DM],
                                                   in0=wtf2[:, :], scalar1=sxcol[:, 2:3],
                                                   scalar2=None, op0=ALU.mult)
                        p.sec("dve", q_wt, rd=["wtf2", "sxcol"], wr=["wt8_%d" % cc2])
                    wt8_r = ["wt8_%d" % i for i in range(8)]

                    for T in range(n_tiles):
                        def ld_p2(e, T=T):
                            ins = [e.dma_start(out=xnf2[:, :], in_=hxnf[T, :, :])]
                            for fb in range(8):
                                ins.append(e.dma_start(
                                    out=ofm[:, fb * 512:(fb + 1) * 512],
                                    in_=houts[T * TILE:(T + 1) * TILE, fb * 128:(fb + 1) * 128],
                                    transpose=True))
                            return ins
                        deps_outs = ["houts_%d_%d" % (T, cc3) for cc3 in range(NCH)] if do_attn else []
                        p.sec("dma", ld_p2, rd=["hxnf_%d_0" % T, "hxnf_%d_1" % T,
                                                "hxnf_%d_2" % T, "hxnf_%d_3" % T] + deps_outs,
                              wr=["xnf2", "ofm"], n_dma=9)

                        def cast_x2(e):
                            return e.tensor_copy(out=xnfb2[:, :], in_=xnf2[:, :])
                        p.sec("dve", cast_x2, rd=["xnf2"], wr=["xnfb2"])

                        def q_x(e):
                            return e.activation(out=xq8[:, :], in_=xnf2[:, :], func=AF.Identity,
                                                scale=sxcol[:, 0:1])
                        p.sec("act", q_x, rd=["xnf2", "sxcol"], wr=["xq8"])

                        for fbp in range(4):
                            psg, pkg = (pa, "pa") if fbp % 2 == 0 else (pb, "pb")

                            def mm_g(e, fbp=fbp, psg=psg):
                                last = None
                                for half in range(2):
                                    fb = fbp * 2 + half
                                    for cc2 in range(8):
                                        last = e.matmul(
                                            psg[:, half * 512:(half + 1) * 512],
                                            swg[:, cc2 * DM + fb * 128: cc2 * DM + (fb + 1) * 128],
                                            xnfb2[:, cc2 * 512:(cc2 + 1) * 512],
                                            start=(cc2 == 0), stop=(cc2 == 7))
                                return last
                            p.sec("pe", mm_g, rd=["xnfb2", "w2"], wr=[pkg])

                            def ev_g(e, fbp=fbp, psg=psg):
                                last = None
                                for half in range(2):
                                    fb = fbp * 2 + half
                                    last = e.activation(out=gf[:, fb * 512:(fb + 1) * 512],
                                                        in_=psg[:, half * 512:(half + 1) * 512],
                                                        func=AF.Silu, bias=sbias["bg"][:, fb:fb + 1])
                                return last
                            p.sec("act", ev_g, rd=[pkg, "consts"], wr=["gf_%d" % fbp])

                        def mk_y(e):
                            last = None
                            for fb in range(8):
                                last = e.tensor_scalar(out=yf[:, fb * 512:(fb + 1) * 512],
                                                       in0=ofm[:, fb * 512:(fb + 1) * 512],
                                                       scalar1=gnmul[:, fb:fb + 1],
                                                       scalar2=gnadd[:, fb:fb + 1],
                                                       op0=ALU.mult, op1=ALU.add)
                            return last
                        p.sec("dve", mk_y, rd=["ofm", "gnfin"], wr=["yf"])

                        def mk_gy(e):
                            return e.tensor_mul(gyf[:, :], gf[:, :], yf[:, :])
                        p.sec("dve", mk_gy, rd=["yf"] + ["gf_%d" % i for i in range(4)],
                              wr=["gyf"])

                        for s_ in range(NSUB):
                            def ld_x2(e, T=T, s_=s_):
                                return e.dma_start(
                                    out=xt2[:, :],
                                    in_=hx[T * TILE + s_ * 128: T * TILE + (s_ + 1) * 128, :])
                            p.sec("dma", ld_x2, wr=["xt2"], n_dma=1)

                            atth = ((pa[:, 0:512], pa[:, 512:1024]) if s_ % 2 == 0
                                    else (pc[:, 0:512], pstat[:, 0:512]))
                            attk = "pa" if s_ % 2 == 0 else "pcs"

                            def mm_att(e, s_=s_, atth=atth):
                                last = None
                                for half in range(2):
                                    for cc2 in range(8):
                                        e.matmul(
                                            atth[half],
                                            gyf[:, cc2 * 512 + s_ * 128: cc2 * 512 + (s_ + 1) * 128],
                                            swo[:, cc2 * DM + half * 512: cc2 * DM + (half + 1) * 512],
                                            start=(cc2 == 0), stop=False)
                                    last = e.matmul(atth[half], sc["ONES1"][:, :],
                                                    sbrow16[0:1, half * 512:(half + 1) * 512],
                                                    start=False, stop=True)
                                return last
                            p.sec("pe", mm_att, rd=["gyf", "w2", "consts", "brow"], wr=[attk])

                            def mm_ffn(e, s_=s_):
                                last = None
                                for half in range(2):
                                    for cc2 in range(8):
                                        e.matmul(
                                            pb[:, half * 512:(half + 1) * 512],
                                            xq8[:, cc2 * 512 + s_ * 128: cc2 * 512 + (s_ + 1) * 128],
                                            wt8[:, cc2 * DM + half * 512: cc2 * DM + (half + 1) * 512],
                                            start=(cc2 == 0), stop=False)
                                    last = e.matmul(pb[:, half * 512:(half + 1) * 512],
                                                    sc["ONES33"][32:33, :],
                                                    sbrow16[32:33, half * 512:(half + 1) * 512],
                                                    start=False, stop=True)
                                return last
                            p.sec("pe", mm_ffn, rd=["xq8", "brow", "consts"] + wt8_r, wr=["pb"])

                            def mk_out1(e, s_=s_):
                                return e.scalar_tensor_tensor(
                                    out=outf[:, :], in0=pb[:, :], scalar=sxcol[:, 1:2],
                                    in1=xt2[:, :], op0=ALU.mult, op1=ALU.add)
                            p.sec("dve", mk_out1, rd=["pb", "xt2", "sxcol"], wr=["outf_a"])

                            def mk_out2(e, s_=s_, atth=atth):
                                e.tensor_tensor(out=outf[:, 0:512], in0=outf[:, 0:512],
                                                in1=atth[0], op=ALU.add)
                                return e.tensor_tensor(out=outf[:, 512:1024],
                                                       in0=outf[:, 512:1024],
                                                       in1=atth[1], op=ALU.add)
                            p.sec("dve", mk_out2, rd=[attk, "outf_a"], wr=["outf"])

                            def st_y(e, T=T, s_=s_):
                                return e.dma_start(
                                    out=hy[T * TILE + s_ * 128: T * TILE + (s_ + 1) * 128, :],
                                    in_=outf[:, :])
                            p.sec("dma", st_y, rd=["outf"], wr=["y_%d_%d" % (T, s_)], n_dma=1)

            def fin(e):
                return None
            p.sec("dma", fin, no_inc=True,
                  extra=[p.t("gdma"), p.t("pe"), p.t("dve"), p.t("act"),
                         p.t("gps"), p.t("dma")])
            p.flush()

    return nc


def kernel(**inputs):
    cm = prep_common(inputs)
    x = np.asarray(inputs["L0_attn_x"], np.float32)
    nc = build_program()
    in_maps = []
    for i in range(8):
        m = dict(cm)
        m["x"] = np.ascontiguousarray(x[i])
        in_maps.append(m)
    res = run_bass_kernel_spmd(nc, in_maps, core_ids=list(range(8)))
    return np.stack([r["y"] for r in res.results], axis=0)


# version 20 (final)
# speedup vs baseline: 1.0628x; 1.0031x over previous
import numpy as np
import ml_dtypes
from contextlib import ExitStack
import concourse.bass as bass
import concourse.mybir as mybir
from concourse.bass_utils import run_bass_kernel_spmd

F32 = mybir.dt.float32
BF16 = mybir.dt.bfloat16
FP8 = mybir.dt.float8e4
AF = mybir.ActivationFunctionType
ALU = mybir.AluOpType
AX = mybir.AxisListType

S, DM, H, HD = 4096, 1024, 16, 64
TILE, NSUB, C, B = 512, 4, 128, 32
NCH = TILE // C
NT = S // TILE
P3 = TILE + 3


def make_consts():
    c = {}
    ii = np.arange(128)
    bf = ml_dtypes.bfloat16
    c["IDENT128"] = np.eye(128, dtype=bf)
    id64 = np.zeros((128, 64), dtype=bf)
    id64[:64, :] = np.eye(64)
    id64[64:, :] = np.eye(64)
    c["ID64"] = id64
    c["MASK1"] = (ii[:, None] < ii[None, :]).astype(np.float32).astype(bf)
    c["IDT1"] = np.eye(128, dtype=np.float32).astype(bf)
    c["NEGBLK1"] = (-(ii[:, None] < (ii[None, :] // B) * B).astype(np.float32)).astype(bf)
    idd = np.zeros((128, B), dtype=np.float32)
    for i in range(4):
        idd[32 * i:32 * i + 32, :] = np.eye(32)
    c["IDD1"] = idd.astype(bf)
    ind2 = np.zeros((128, 8 * H), dtype=np.float32)
    for fb in range(8):
        ind2[0:64, fb * H + 2 * fb] = 1.0
        ind2[64:128, fb * H + 2 * fb + 1] = 1.0
    c["IND2"] = ind2.astype(bf)
    indt = np.zeros((H, 8 * 128), dtype=np.float32)
    for fb in range(8):
        indt[2 * fb, fb * 128:fb * 128 + 64] = 1.0
        indt[2 * fb + 1, fb * 128 + 64:fb * 128 + 128] = 1.0
    c["INDT"] = indt.astype(bf)
    i64 = np.zeros((H, H * HD), dtype=np.float32)
    for h in range(H):
        i64[h, h * HD:(h + 1) * HD] = 1.0
    c["IND64"] = i64.astype(bf)
    ibd = np.zeros((H, H * 128), np.float32)
    for h in range(H):
        ibd[h, h * 128:(h + 1) * 128] = 1.0
    c["INDBD"] = ibd.astype(bf)
    c["ONES1"] = np.ones((1, 128), dtype=bf)
    o33 = np.zeros((33, 128), np.float32); o33[0] = 1; o33[32] = 1
    c["ONES33"] = o33.astype(bf)
    c["ONES2"] = np.ones((2, 128), dtype=bf)
    c["ONES16"] = np.ones((16, 128), dtype=bf)
    c["ONESC"] = np.ones((128, 1), dtype=bf)
    c["ONESF"] = np.ones((1, 128), dtype=np.float32)
    c["ID16"] = np.eye(16, dtype=np.float32)
    indp = np.zeros((H, 128), np.float32)
    for h in range(H):
        indp[h, 64 * (h % 2):64 * (h % 2) + 64] = 1.0
    c["INDP"] = indp.astype(bf)
    p16 = np.zeros((H, 8), np.float32)
    for h in range(H):
        p16[h, h // 2] = 1.0
    c["PATTERN16"] = p16
    cv = np.zeros((128, 4), np.float32)
    cv[:, 0] = 1e-5; cv[:, 1] = -1.0
    cv[0:64, 2] = 1.0
    cv[64:128, 3] = 1.0
    c["CVALS"] = cv
    return c


CONSTS = make_consts()


def prep_common(inputs):
    f32 = np.float32
    cm = {}
    for k, v in CONSTS.items():
        cm["c_" + k] = v

    def rw(w):
        return np.ascontiguousarray(np.asarray(w, f32).reshape(8, 128, -1))
    cm["wq"] = rw(inputs["L0_attn_Wq"]); cm["wk"] = rw(inputs["L0_attn_Wk"])
    cm["wv"] = rw(inputs["L0_attn_Wv"]); cm["wg"] = rw(inputs["L0_attn_Wg"])
    cm["wo"] = rw(inputs["L0_attn_Wo"])
    cm["wa"] = rw(inputs["L0_attn_Wa"]); cm["wb"] = rw(inputs["L0_attn_Wb"])
    cm["wffn"] = np.ascontiguousarray(np.asarray(inputs["L0_ffn_weight"], f32).reshape(8, 128, DM))

    def col8(b):
        return np.ascontiguousarray(np.asarray(b, f32).reshape(8, 128).T)
    cm["bq"] = col8(inputs["L0_attn_bq"]); cm["bk"] = col8(inputs["L0_attn_bk"])
    cm["bv"] = col8(inputs["L0_attn_bv"]); cm["bg"] = col8(inputs["L0_attn_bg"])
    cm["gns"] = col8(inputs["L0_attn_gn_scale"]); cm["gnb"] = col8(inputs["L0_attn_gn_bias"])
    cm["bo"] = np.ascontiguousarray(np.asarray(inputs["L0_attn_bo"], f32).reshape(1, DM))
    cm["bffn"] = np.ascontiguousarray(np.asarray(inputs["L0_ffn_bias"], f32).reshape(1, DM))
    cm["ba"] = np.ascontiguousarray(np.asarray(inputs["L0_attn_ba"], f32).reshape(H, 1))
    cm["bb"] = np.ascontiguousarray(np.asarray(inputs["L0_attn_bb"], f32).reshape(H, 1))
    cw = np.zeros((128, 8 * 3 * 4), f32)
    for pr, key in enumerate(("L0_attn_q_conv", "L0_attn_k_conv", "L0_attn_v_conv")):
        w = np.asarray(inputs[key], f32).reshape(8, 128, 4)
        for fb in range(8):
            cw[:, (pr * 8 + fb) * 4:(pr * 8 + fb) * 4 + 4] = w[fb]
    cm["convw"] = cw
    return cm


class Prog:
    ENG2BLK = {"pe": "tensor", "dve": "vector", "act": "scalar",
               "gps": "gpsimd", "dma": "sync", "gdma": "gpsimd"}

    def __init__(self, nc, block, sems):
        self.nc, self.block, self.sems = nc, block, sems
        self.cnt = {k: 0 for k in self.ENG2BLK}
        self.res = {}
        self.pend = {k: [] for k in self.ENG2BLK}

    FLUSH_N = 1
    FLUSH_EVERY = True

    def flush(self):
        for eng, bodies in self.pend.items():
            n = self.FLUSH_N
            for i in range(0, len(bodies), n):
                def mega(e, bs=tuple(bodies[i:i + n])):
                    for b in bs:
                        b(e)
                getattr(self.block, self.ENG2BLK[eng])(mega)
            self.pend[eng] = []

    def _get(self, r):
        if r not in self.res:
            self.res[r] = {"w": None, "r": []}
        return self.res[r]

    stage_limit = 99
    stage_cur = 0

    def sec(self, eng, fn, rd=(), wr=(), extra=(), n_dma=0, no_inc=False):
        if self.stage_cur > self.stage_limit:
            return (eng, self.cnt[eng])
        deps = []
        for r in rd:
            st = self._get(r)
            if st["w"] is not None:
                deps.append(st["w"])
        for r in wr:
            st = self._get(r)
            if st["w"] is not None:
                deps.append(st["w"])
            deps.extend(st["r"])
        deps.extend([d for d in extra if d is not None])
        is_dma = eng in ("dma", "gdma")
        if is_dma and n_dma > 0:
            deps.append((eng, self.cnt[eng]))
        dd = {}
        for (de, dc) in deps:
            if dc > 0 and dc > dd.get(de, 0):
                dd[de] = dc
        deps = list(dd.items())
        for (de, dc) in deps:
            assert dc <= self.cnt[de], (eng, de, dc, self.cnt[de])
        sems = self.sems

        def body(e):
            for (de, dc) in deps:
                e.wait_ge(sems[de], dc)
            last = fn(e)
            if is_dma:
                if isinstance(last, (list, tuple)):
                    for ins in last:
                        ins.then_inc(sems[eng], 16)
                elif last is not None:
                    last.then_inc(sems[eng], 16)
            elif last is not None:
                last.then_inc(sems[eng], 1)
            else:
                assert no_inc, "compute section returned None without no_inc"
        self.pend[eng].append(body)
        if self.FLUSH_EVERY:
            self.flush()
        if is_dma:
            self.cnt[eng] += 16 * n_dma
        elif not no_inc:
            self.cnt[eng] += 1
        tk = (eng, self.cnt[eng])
        for r in rd:
            self._get(r)["r"].append(tk)
        for r in wr:
            st = self._get(r)
            st["w"] = tk
            st["r"] = []
        return tk

    def t(self, eng):
        return (eng, self.cnt[eng])


def build_program(n_tiles=NT, do_attn=True, do_phase2=True, debug=(), n_cores=8, chunk_upto=99):
    nc = bass.Bass("TRN2", target_bir_lowering=False, detect_race_conditions=False)

    hx = nc.dram_tensor("x", [S, DM], F32, kind="ExternalInput").ap()
    hw = {w: nc.dram_tensor(w, [8, 128, DM], F32, kind="ExternalInput").ap()
          for w in ("wq", "wk", "wv", "wg", "wo", "wffn")}
    hw["wa"] = nc.dram_tensor("wa", [8, 128, H], F32, kind="ExternalInput").ap()
    hw["wb"] = nc.dram_tensor("wb", [8, 128, H], F32, kind="ExternalInput").ap()
    hbn = ("bq", "bk", "bv", "bg", "gns", "gnb")
    hb = {b: nc.dram_tensor(b, [128, 8], F32, kind="ExternalInput").ap() for b in hbn}
    hb["bo"] = nc.dram_tensor("bo", [1, DM], F32, kind="ExternalInput").ap()
    hb["bffn"] = nc.dram_tensor("bffn", [1, DM], F32, kind="ExternalInput").ap()
    hb["ba"] = nc.dram_tensor("ba", [H, 1], F32, kind="ExternalInput").ap()
    hb["bb"] = nc.dram_tensor("bb", [H, 1], F32, kind="ExternalInput").ap()
    hconv = nc.dram_tensor("convw", [128, 8 * 3 * 4], F32, kind="ExternalInput").ap()
    hc = {}
    for k, v in CONSTS.items():
        dt = BF16 if v.dtype == ml_dtypes.bfloat16 else F32
        hc[k] = nc.dram_tensor("c_" + k, list(v.shape), dt, kind="ExternalInput").ap()

    hy = nc.dram_tensor("y", [S, DM], F32, kind="ExternalOutput").ap()
    hxnf = nc.dram_tensor("xnf_sc", [NT, 128, 8 * TILE], F32, kind="Internal").ap()
    houts = nc.dram_tensor("outs_sc", [S, DM], BF16, kind="Internal").ap()
    hwt = nc.dram_tensor("wt_sc", [8, 128, DM], F32, kind="Internal").ap()
    hrow = nc.dram_tensor("rowbounce", [4, 128], F32, kind="Internal").ap()
    hlhl = nc.dram_tensor("lhl_sc", [48, TILE], BF16, kind="Internal").ap()
    hwam = nc.dram_tensor("wam", [1, 1], F32, kind="Internal").ap()
    hccin = nc.dram_tensor("cc_in", [1, 1], F32, kind="Internal").ap()
    hccout = nc.dram_tensor("cc_out", [1, 1], F32, kind="Internal").ap()

    dbg = {}
    def mkdbg(name, shape, dt=F32):
        if name in debug:
            dbg[name] = nc.dram_tensor("dbg_" + name, list(shape), dt,
                                       kind="ExternalOutput").ap()
    mkdbg("xnf", [128, 8 * TILE])
    mkdbg("kfn", [128, 8 * TILE], BF16)
    mkdbg("qfn", [128, 8 * TILE], BF16)
    mkdbg("vf", [128, 8 * TILE], BF16)
    mkdbg("khf", [128, 8 * TILE], BF16)
    mkdbg("ltile", [H, TILE])
    mkdbg("ee", [128, H * C], BF16)
    mkdbg("ptraw", [128, H * C], BF16)
    mkdbg("gt", [128, H * C], BF16)
    mkdbg("tfin", [128, H * B], BF16)
    mkdbg("uu", [128, H * HD], BF16)
    mkdbg("outs0", [128, DM], BF16)
    mkdbg("ssf", [128, H * HD])
    mkdbg("gnmul", [128, 8])
    mkdbg("gnadd", [128, 8])
    mkdbg("aval", [1, 4])

    eso = ExitStack()
    so = lambda n, s, dt=F32: eso.enter_context(nc.sbuf_tensor(n, s, dt))

    sc = {}
    for k, v in CONSTS.items():
        dt = BF16 if v.dtype == ml_dtypes.bfloat16 else F32
        sc[k] = so("s_" + k, list(v.shape), dt)
    sbias = {k: so("sb_" + k, [128, 8]) for k in hbn}
    sbrow = so("sbrow", [33, DM])
    sbrow16 = so("sbrow16", [33, DM], BF16)
    sba = so("sb_ba", [H, 1])
    sbb = so("sb_bb", [H, 1])
    sconv = so("sb_convw", [128, 8 * 3 * 4])
    amaxc = so("amaxc", [128, 40])
    srow = so("srow", [4, 128])
    sval = so("sval", [4, 12])
    sxcol = so("sxcol", [128, 4])
    gnmul = so("gnmul", [128, 8])
    gnadd = so("gnadd", [128, 8])
    gnc = so("gnc", [32, 64])
    gnr = so("gnr", [16, 8])
    vbr = so("vbr", [16, 8])
    vbm = so("vbm", [16, 8])
    INDPF = so("indpf_t", [16, 128])
    identf = so("identf", [128, 128])

    pa = eso.enter_context(nc.psum_tensor("pa", [128, 1024], F32))
    pb = eso.enter_context(nc.psum_tensor("pb", [128, 1024], F32))
    pc = eso.enter_context(nc.psum_tensor("pc", [128, 512], F32))
    pstat = eso.enter_context(nc.psum_tensor("pstat", [128, 512], F32))
    pt8 = eso.enter_context(nc.psum_tensor("pt8", [128, 2048], BF16))

    sems = {k: eso.enter_context(nc.semaphore("sem_" + k))
            for k in ("pe", "dve", "act", "gps", "dma", "gdma")}
    nc.const_aps.register(F32, 1e-5, sc["CVALS"][:, 0:1])
    nc.const_aps.register(F32, -1.0, sc["CVALS"][:, 1:2])

    esi = ExitStack()
    si = lambda n, s, dt=F32: esi.enter_context(nc.sbuf_tensor(n, s, dt))
    swq = si("swq", [128, 8 * DM], BF16)
    swk = si("swk", [128, 8 * DM], BF16)
    swv = si("swv", [128, 8 * DM], BF16)
    swa = si("swa", [128, 8 * H], BF16)
    swb = si("swb", [128, 8 * H], BF16)
    sdiag = si("sdiag", [128, 64 * 128], BF16)
    vcv = si("vcv", [128, 512], BF16)
    xts = si("xts", [128, DM])
    lntmp = si("lntmp", [128, DM])
    xnfs = xts
    xnfb = si("xnfb", [128, 8 * TILE], BF16)
    lncol = si("lncol", [128, 8])
    pre = si("pre", [128, 8 * P3], BF16)
    halo = si("halo", [128, 3 * 8 * 3], BF16)
    qfn = si("qfn", [128, 8 * TILE], BF16)
    kfn = si("kfn", [128, 8 * TILE], BF16)
    khfc = si("khfc", [128, 8 * C], BF16)
    khfcE = si("khfcE", [128, 8 * C], BF16)
    khfcO = si("khfcO", [128, 8 * C], BF16)
    vf = si("vf", [128, 8 * TILE], BF16)
    ksq = si("ksq", [128, TILE], BF16)
    nsqp = si("nsqp", [48, TILE])
    nsqq = nsqp[0:16, :]; nsqk = nsqp[32:48, :]
    negl = nsqp[0:16, :]
    rnq = si("rnq", [16, TILE], BF16)
    rnk = si("rnk", [16, TILE], BF16)
    lap = si("lap", [112, TILE])
    ltile = lap[0:16, :]
    atile = lap[96:112, :]
    abti = si("abti", [16, C], BF16)
    btile = si("btile", [16, TILE], BF16)
    lhl = si("lhl", [48, TILE], BF16)
    lhi = lhl[0:16, :]; llo = lhl[32:48, :]
    nlhi = si("nlhi", [16, TILE], BF16)
    nllo = si("nllo", [16, TILE], BF16)
    lfl2 = si("lfl2", [2, 2 * H * C], BF16)
    eet = si("eet", [128, H * C], BF16)
    uu = eet[:, 0:H * HD]
    eegh = si("eegh", [128, 1024], BF16)
    rcc = si("rcc", [128, 16], BF16)
    ptraw = si("ptraw", [128, H * C], BF16)
    ptod = ptraw
    gtt = si("gtt", [128, H * C], BF16)
    dd = None
    ddt = si("ddt", [128, H * B], BF16)
    dp = {1: si("dpow1", [128, H * B], BF16), 2: si("dpow2", [128, H * B], BF16),
          16: si("dpow16", [128, H * B], BF16)}
    dd = dp[16]
    dtp = {1: ddt, 2: si("dt2", [128, H * B], BF16),
           4: si("dt4", [128, H * B], BF16), 8: si("dt8", [128, H * B], BF16)}
    ggl = si("ggl", [128, H * B], BF16)
    ggr = si("ggr", [128, H * B], BF16)
    txx = dp[16]
    tfin = ggr

    kqs0 = si("kqs0", [128, H * HD], BF16)
    vtm = si("vtm", [128, H * HD], BF16)
    khtm = si("khtm", [128, H * HD], BF16)
    rhs_ = si("rhs_", [128, H * HD], BF16)
    preb = kqs0

    upb = vtm
    ssf = si("ssf", [128, H * HD])
    ssbE = si("ssbE", [128, H * HD], BF16)
    ssbO = si("ssbO", [128, H * HD], BF16)
    sst = lntmp
    outst = si("outst", [128, DM], BF16)
    acdg = si("acdg", [16, 16], BF16)
    wtf = lntmp

    def head_rb(h):
        return 64 * (h % 2)

    def fchunk(t, h, cc, w=C):
        fb = h // 2
        rb = head_rb(h)
        return t[rb:rb + 64, fb * TILE + cc * C: fb * TILE + cc * C + w]

    with eso, esi:
        with nc.Block() as block:
            p = Prog(nc, block, sems)

            def ld_consts(e):
                ins = []
                for k in sc:
                    t_ = sc[k]
                    sl = tuple(slice(0, d) for d in t_.shape)
                    ins.append(e.dma_start(out=t_[sl], in_=hc[k][sl]))
                for k in sbias:
                    ins.append(e.dma_start(out=sbias[k][:, :], in_=hb[k][:, :]))
                ins.append(e.dma_start(out=sbrow[0:1, :], in_=hb["bo"][:, :]))
                ins.append(e.dma_start(out=sbrow[32:33, :], in_=hb["bffn"][:, :]))
                ins.append(e.dma_start(out=sba[:, :], in_=hb["ba"][:, :]))
                ins.append(e.dma_start(out=sbb[:, :], in_=hb["bb"][:, :]))
                ins.append(e.dma_start(out=sconv[:, :], in_=hconv[:, :]))
                return ins
            p.sec("dma", ld_consts, wr=["consts"], n_dma=len(sc) + len(sbias) + 5)

            def ld_w(e):
                ins = []
                for w, dst in (("wq", swq), ("wk", swk), ("wv", swv)):
                    for cc in range(8):
                        ins.append(e.dma_start(out=dst[:, cc * DM:(cc + 1) * DM],
                                               in_=hw[w][cc, :, :]))
                for w, dst in (("wa", swa), ("wb", swb)):
                    for cc in range(8):
                        ins.append(e.dma_start(out=dst[:, cc * H:(cc + 1) * H],
                                               in_=hw[w][cc, :, :]))
                return ins
            p.sec("gdma", ld_w, wr=["w1"], n_dma=3 * 8 + 2 * 8)

            def mk_init(e):
                e.memset(eet[:, :], 0.0)
                e.memset(ssf[:, :], 0.0)
                e.memset(ssbE[:, :], 0.0)
                e.memset(ssbO[:, :], 0.0)
                e.memset(halo[:, :], 0.0)
                e.memset(amaxc[:, :], 0.0)
                e.memset(sval[:, :], 0.0)
                e.memset(gnc[:, :], 0.0)
                e.memset(srow[:, :], 0.0)
                return e.tensor_copy(out=identf[:, :], in_=sc["IDENT128"][:, :])
            p.sec("dve", mk_init, rd=["consts"],
                  wr=["uu", "ssf", "ssb", "halo", "amaxc", "sval", "gnc", "srow", "identf"])

            def mk_diag(e):
                last = None
                for idx in range(64):
                    last = e.tensor_scalar(
                        out=sdiag[:, idx * 128:(idx + 1) * 128],
                        in0=sc["IDENT128"][:, :],
                        scalar1=sconv[:, idx:idx + 1],
                        scalar2=None, op0=ALU.mult)
                return last
            p.sec("gps", mk_diag, rd=["consts"], wr=["sdiag"])

            for cc in range(8):
                wbuf, wkey = (wtf, "lntmp") if cc % 2 == 0 else (xnfs, "xts")
                pw_, pwk = (pa, "pa") if cc % 2 == 0 else (pb, "pb")

                def ld_wf(e, cc=cc, wbuf=wbuf):
                    ins = []
                    for oc in range(8):
                        ins.append(e.dma_start(
                            out=wbuf[:, oc * 128:(oc + 1) * 128],
                            in_=hw["wffn"][oc, :, cc * 128:(cc + 1) * 128]))
                    return ins
                p.sec("dma", ld_wf, wr=[wkey], n_dma=8)

                def tp_wf(e, wbuf=wbuf, pw_=pw_):
                    last = None
                    for oc in range(8):
                        last = e.transpose(pw_[:, oc * 128:(oc + 1) * 128],
                                           wbuf[:, oc * 128:(oc + 1) * 128],
                                           identf[:, :])
                    return last
                p.sec("pe", tp_wf, rd=[wkey, "identf"], wr=[pwk])

                def ev_wf1(e, cc=cc, wbuf=wbuf, pw_=pw_):
                    return e.tensor_copy(out=wbuf[:, :], in_=pw_[:, :])
                p.sec("dve", ev_wf1, rd=[pwk], wr=[wkey])

                def ev_wf2(e, cc=cc, wbuf=wbuf):
                    return e.reduce_max(out=amaxc[:, 34 + (cc % 2) * 4:35 + (cc % 2) * 4],
                                        in_=wbuf[:, :], axis=AX.X,
                                        apply_absolute_value=True)
                p.sec("dve", ev_wf2, rd=[wkey], wr=["amaxw_t%d" % (cc % 2)])

                def ev_wf3(e, cc=cc):
                    return e.tensor_tensor(out=amaxc[:, 35:36], in0=amaxc[:, 35:36],
                                           in1=amaxc[:, 34 + (cc % 2) * 4:35 + (cc % 2) * 4],
                                           op=ALU.max)
                p.sec("dve", ev_wf3, rd=["amaxw_t%d" % (cc % 2), "amaxc"], wr=["amaxc"])

                def st_wf(e, cc=cc, wbuf=wbuf):
                    return e.dma_start(out=hwt[cc, :, :], in_=wbuf[:, :])
                p.sec("dma", st_wf, rd=[wkey], wr=["hwt_%d" % cc], n_dma=1)

            stats_on = [False]

            for T in range(n_tiles):
                for s_ in range(NSUB):
                    def ld_x(e, T=T, s_=s_):
                        return e.dma_start(
                            out=xts[:, :],
                            in_=hx[T * TILE + s_ * 128: T * TILE + (s_ + 1) * 128, :])
                    p.sec("dma", ld_x, wr=["xts"], n_dma=1)

                    def ln_a1(e):
                        return e.reduce_sum(out=lncol[:, 0:1], in_=xts[:, :], axis=AX.X)
                    p.sec("dve", ln_a1, rd=["xts"], wr=["lncol_a"])

                    def ln_a2(e):
                        return e.tensor_scalar(out=lncol[:, 1:2], in0=lncol[:, 0:1],
                                               scalar1=1.0 / DM, scalar2=None, op0=ALU.mult)
                    p.sec("dve", ln_a2, rd=["lncol_a"], wr=["lncol"])

                    def ln_b(e):
                        return e.activation(out=lntmp[:, :], in_=xts[:, :], func=AF.Square,
                                            accum_out=lncol[:, 2:3])
                    p.sec("act", ln_b, rd=["xts"], wr=["lntmp", "lncol2"])

                    def ln_c1(e):
                        return e.tensor_scalar(out=lncol[:, 3:4], in0=lncol[:, 1:2],
                                               scalar1=lncol[:, 1:2], scalar2=None, op0=ALU.mult)
                    p.sec("dve", ln_c1, rd=["lncol"], wr=["lncol_c"])

                    def ln_c2(e):
                        return e.tensor_scalar(out=lncol[:, 4:5], in0=lncol[:, 2:3],
                                               scalar1=1.0 / DM, scalar2=lncol[:, 3:4],
                                               op0=ALU.mult, op1=ALU.subtract)
                    p.sec("dve", ln_c2, rd=["lncol_c", "lncol2"], wr=["lncol"])

                    def ln_d(e):
                        return e.activation(out=lncol[:, 5:6], in_=lncol[:, 4:5],
                                            func=AF.Sqrt, bias=1e-5)
                    p.sec("act", ln_d, rd=["lncol"], wr=["lncol3"])

                    def ln_e1(e):
                        return e.reciprocal(out=lncol[:, 6:7], in_=lncol[:, 5:6])
                    p.sec("dve", ln_e1, rd=["lncol3"], wr=["lncol_e"])

                    def ln_e2(e):
                        return e.tensor_scalar(out=lntmp[:, :], in0=xts[:, :],
                                               scalar1=lncol[:, 1:2], scalar2=lncol[:, 6:7],
                                               op0=ALU.subtract, op1=ALU.mult)
                    p.sec("dve", ln_e2, rd=["xts", "lncol", "lncol_e"], wr=["lntmp"])

                    def ln_e3(e, T=T, s_=s_):
                        return e.reduce_max(out=amaxc[:, T * NSUB + s_:T * NSUB + s_ + 1],
                                            in_=lntmp[:, :], axis=AX.X,
                                            apply_absolute_value=True)
                    p.sec("dve", ln_e3, rd=["lntmp"], wr=["amaxc"])

                    psx, pkx = (pa, "pa") if s_ % 2 == 0 else (pb, "pb")

                    def tp_x(e, psx=psx):
                        last = None
                        for fb in range(8):
                            last = e.transpose(psx[:, fb * 128:(fb + 1) * 128],
                                               lntmp[:, fb * 128:(fb + 1) * 128],
                                               identf[:, :])
                        return last
                    p.sec("pe", tp_x, rd=["lntmp", "identf"], wr=[pkx])

                    def ev_f(e, psx=psx):
                        return e.tensor_copy(out=xnfs[:, :], in_=psx[:, :])
                    p.sec("dve", ev_f, rd=[pkx], wr=["xts"])

                    def ev_b(e, s_=s_, psx=psx):
                        ap = xnfb[:, :].rearrange("p (fb t) -> p fb t", fb=8)[:, :, s_ * 128:(s_ + 1) * 128]
                        return e.tensor_copy(out=ap,
                                             in_=psx[:, :].rearrange("p (fb t) -> p fb t", fb=8))
                    p.sec("dve", ev_b, rd=[pkx], wr=["xnfb_%d" % s_])

                    def st_xnf(e, T=T, s_=s_):
                        ap = hxnf[T, :, :].rearrange("p (fb t) -> p fb t", fb=8)[:, :, s_ * 128:(s_ + 1) * 128]
                        return e.dma_start(out=ap, in_=xnfs[:, :].rearrange("p (fb t) -> p fb t", fb=8))
                    p.sec("dma", st_xnf, rd=["xts"], wr=["hxnf_%d_%d" % (T, s_)], n_dma=1)

                xnfb_r = ["xnfb_%d" % s_ for s_ in range(NSUB)]
                if "xnf" in dbg and T == 0:
                    p.sec("dma", lambda e: e.dma_start(out=dbg["xnf"][:, :], in_=hxnf[0, :, :]),
                          rd=["hxnf_0_0", "hxnf_0_1", "hxnf_0_2", "hxnf_0_3"], n_dma=1)

                for pi, (pname, wten, bkey, dstF) in enumerate(
                        (("q", swq, "bq", qfn), ("k", swk, "bk", kfn), ("v", swv, "bv", vf))):
                    for fbp in range(4):
                        ps_, pk_ = (pa, "pa") if fbp % 2 == 0 else (pb, "pb")

                        def mm_p(e, wten=wten, fbp=fbp, ps_=ps_):
                            last = None
                            for half in range(2):
                                fb = fbp * 2 + half
                                for cc in range(8):
                                    last = e.matmul(
                                        ps_[:, half * 512:(half + 1) * 512],
                                        wten[:, cc * DM + fb * 128: cc * DM + (fb + 1) * 128],
                                        xnfb[:, cc * 512:(cc + 1) * 512],
                                        start=(cc == 0), stop=(cc == 7))
                            return last
                        p.sec("pe", mm_p, rd=xnfb_r + ["w1"], wr=[pk_])

                        def ev_p(e, bkey=bkey, fbp=fbp, ps_=ps_):
                            last = None
                            for half in range(2):
                                fb = fbp * 2 + half
                                last = e.tensor_scalar(
                                    out=pre[:, fb * P3 + 3: fb * P3 + 3 + TILE],
                                    in0=ps_[:, half * 512:(half + 1) * 512],
                                    scalar1=sbias[bkey][:, fb:fb + 1], scalar2=None,
                                    op0=ALU.add)
                            return last
                        p.sec("dve", ev_p, rd=[pk_, "consts"], wr=["pre_%d" % fbp])

                    def h_in(e, pi=pi):
                        ap = pre[:, :].rearrange("p (fb t) -> p fb t", fb=8)[:, :, 0:3]
                        hap = halo[:, :].rearrange("p (pr fbt) -> p pr fbt", pr=3)[:, pi]\
                            .rearrange("p (fb t) -> p fb t", fb=8)
                        return e.tensor_copy(out=ap, in_=hap)
                    p.sec("dve", h_in, rd=["halo"], wr=["prehalo"],
                          extra=[p.res["pre_%d" % f]["w"] for f in range(4)])

                    if pi < 2:
                        for fbp in range(4):
                            ps_, pk_ = (pb, "pb") if fbp % 2 == 0 else (pa, "pa")

                            def mm_c(e, pi=pi, fbp=fbp, ps_=ps_):
                                last = None
                                for half in range(2):
                                    fb = fbp * 2 + half
                                    for tap in range(4):
                                        idx = (pi * 8 + fb) * 4 + tap
                                        last = e.matmul(
                                            ps_[:, half * 512:(half + 1) * 512],
                                            sdiag[:, idx * 128:(idx + 1) * 128],
                                            pre[:, fb * P3 + tap: fb * P3 + tap + TILE],
                                            start=(tap == 0), stop=(tap == 3))
                                return last
                            p.sec("pe", mm_c, rd=["prehalo", "sdiag", "pre_%d" % fbp], wr=[pk_])

                            def ev_c(e, dstF=dstF, fbp=fbp, ps_=ps_):
                                last = None
                                for half in range(2):
                                    fb = fbp * 2 + half
                                    last = e.activation(out=dstF[:, fb * 512:(fb + 1) * 512],
                                                        in_=ps_[:, half * 512:(half + 1) * 512],
                                                        func=AF.Silu)
                                return last
                            p.sec("act", ev_c, rd=[pk_], wr=["F%d_%d" % (pi, fbp)])
                    else:
                        for fb in range(8):
                            def cv_v0(e, fb=fb):
                                base = (2 * 8 + fb) * 4
                                return e.tensor_scalar(
                                    out=vcv[:, :],
                                    in0=pre[:, fb * P3 + 0: fb * P3 + TILE],
                                    scalar1=sconv[:, base:base + 1],
                                    scalar2=None, op0=ALU.mult)
                            p.sec("dve", cv_v0, rd=["prehalo", "consts", "pre_%d" % (fb // 2)],
                                  wr=["vcv"])
                            for tap in range(1, 4):
                                def cv_vt(e, fb=fb, tap=tap):
                                    base = (2 * 8 + fb) * 4
                                    return e.scalar_tensor_tensor(
                                        out=vcv[:, :],
                                        in0=pre[:, fb * P3 + tap: fb * P3 + tap + TILE],
                                        scalar=sconv[:, base + tap:base + tap + 1],
                                        in1=vcv[:, :], op0=ALU.mult, op1=ALU.add)
                                p.sec("dve", cv_vt,
                                      rd=["prehalo", "consts", "pre_%d" % (fb // 2), "vcv"],
                                      wr=["vcv"])

                            def sl_v(e, dstF=dstF, fb=fb):
                                return e.activation(out=dstF[:, fb * 512:(fb + 1) * 512],
                                                    in_=vcv[:, :], func=AF.Silu)
                            p.sec("act", sl_v, rd=["vcv"], wr=["F%d_%d" % (pi, fb // 2)])

                    def h_out(e, pi=pi):
                        ap = pre[:, :].rearrange("p (fb t) -> p fb t", fb=8)[:, :, TILE:TILE + 3]
                        hap = halo[:, :].rearrange("p (pr fbt) -> p pr fbt", pr=3)[:, pi]\
                            .rearrange("p (fb t) -> p fb t", fb=8)
                        return e.tensor_copy(out=hap, in_=ap)
                    p.sec("dve", h_out, rd=["pre_0", "pre_1", "pre_2", "pre_3"], wr=["halo"])

                    if pi < 2:
                        nsqt = nsqq if pi == 0 else nsqk
                        for fb in range(8):
                            def sq_f(e, dstF=dstF, fb=fb):
                                return e.tensor_mul(ksq[:, :], dstF[:, fb * 512:(fb + 1) * 512],
                                                    dstF[:, fb * 512:(fb + 1) * 512])
                            p.sec("gps", sq_f, rd=["F%d_%d" % (pi, fb // 2)], wr=["ksq"])

                            def mm_n(e, fb=fb):
                                return e.matmul(pc[0:16, 0:512], sc["IND2"][:, fb * H:(fb + 1) * H],
                                                ksq[:, :], start=(fb == 0), stop=(fb == 7))
                            p.sec("pe", mm_n, rd=["ksq", "consts"],
                                  wr=["pc"] if fb == 7 else [])

                        def ev_n(e, nsqt=nsqt):
                            return e.activation(out=nsqt[:, :], in_=pc[0:16, 0:512], func=AF.Sqrt)
                        p.sec("act", ev_n, rd=["pc"], wr=["nsq_%d" % pi])

                def rec_n1(e):
                    e.tensor_scalar(out=nsqq[:, :], in0=nsqq[:, :], scalar1=1e-6,
                                    scalar2=None, op0=ALU.add)
                    return e.tensor_scalar(out=nsqk[:, :], in0=nsqk[:, :], scalar1=1e-6,
                                           scalar2=None, op0=ALU.add)
                p.sec("dve", rec_n1, rd=["nsq_0", "nsq_1"], wr=["rn_a"])

                def rec_n2(e):
                    e.reciprocal(out=nsqq[:, :], in_=nsqq[:, :])
                    return e.reciprocal(out=nsqk[:, :], in_=nsqk[:, :])
                p.sec("dve", rec_n2, rd=["rn_a"], wr=["rn_b"])

                def rec_n3(e):
                    e.tensor_copy(out=rnq[:, :], in_=nsqq[:, :])
                    return e.tensor_copy(out=rnk[:, :], in_=nsqk[:, :])
                p.sec("dve", rec_n3, rd=["rn_b"], wr=["rn"])

                for ji, (srcT, dstT, vec) in enumerate(((qfn, qfn, rnq), (kfn, kfn, rnk))):
                    for fbp in range(4):
                        def mm_r(e, vec=vec, fbp=fbp):
                            last = None
                            for half in range(2):
                                fb = fbp * 2 + half
                                last = e.matmul(pa[:, half * 512:(half + 1) * 512],
                                                sc["INDT"][:, fb * 128:(fb + 1) * 128],
                                                vec[:, :], start=True, stop=True)
                            return last
                        p.sec("pe", mm_r, rd=["rn", "consts"], wr=["pa"])

                        def ap_r(e, srcT=srcT, dstT=dstT, fbp=fbp):
                            last = None
                            for half in range(2):
                                fb = fbp * 2 + half
                                last = e.tensor_tensor(out=dstT[:, fb * 512:(fb + 1) * 512],
                                                       in0=srcT[:, fb * 512:(fb + 1) * 512],
                                                       in1=pa[:, half * 512:(half + 1) * 512],
                                                       op=ALU.mult)
                            return last
                        wkey = ("Fq_n" if ji == 0 else "Fk_n") + "_%d" % fbp
                        rkey = ["F0_%d" % fbp] if ji == 0 else ["F1_%d" % fbp]
                        p.sec("dve", ap_r, rd=["pa"] + rkey, wr=[wkey])
                qfn_r = ["Fq_n_%d" % f for f in range(4)]
                kfn_r = ["Fk_n_%d" % f for f in range(4)]
                khf_r = ["khfc"]
                vf_r = ["F2_%d" % f for f in range(4)]

                def mm_ab(e):
                    last = None
                    for cc in range(8):
                        e.matmul(pb[0:16, 0:512], swa[:, cc * H:(cc + 1) * H],
                                 xnfb[:, cc * 512:(cc + 1) * 512],
                                 start=(cc == 0), stop=(cc == 7))
                    for cc in range(8):
                        last = e.matmul(pb[0:16, 512:1024], swb[:, cc * H:(cc + 1) * H],
                                        xnfb[:, cc * 512:(cc + 1) * 512],
                                        start=(cc == 0), stop=(cc == 7))
                    return last
                p.sec("pe", mm_ab, rd=xnfb_r + ["w1"], wr=["pb"])

                def ev_ab(e):
                    e.activation(out=atile[:, :], in_=pb[0:16, 0:512], func=AF.Sigmoid,
                                 bias=sba[:, 0:1])
                    return e.activation(out=btile[:, :], in_=pb[0:16, 512:1024],
                                        func=AF.Sigmoid, bias=sbb[:, 0:1])
                p.sec("act", ev_ab, rd=["pb", "consts"], wr=["atile", "btile"])

                def ev_ab2(e):
                    return e.activation(out=ltile[:, :], in_=atile[:, :], func=AF.Ln)
                p.sec("act", ev_ab2, rd=["atile"], wr=["ltile"])

                def cum_l1(e):
                    return e.tensor_tensor_scan(out=ltile[:, :], data0=ltile[:, :],
                                                data1=ltile[:, :], initial=0.0,
                                                op0=ALU.add, op1=ALU.min)
                p.sec("dve", cum_l1, rd=["ltile", "rn"], wr=["ltile"])

                def cum_l2(e):
                    e.tensor_scalar(out=negl[:, :], in0=ltile[:, :], scalar1=-1.0,
                                    scalar2=None, op0=ALU.mult)
                    return e.tensor_copy(out=lhi[:, :], in_=ltile[:, :])
                p.sec("dve", cum_l2, rd=["ltile"], wr=["negl", "lhi"])

                def cum_l3(e):
                    e.tensor_scalar(out=nlhi[:, :], in0=lhi[:, :], scalar1=-1.0,
                                    scalar2=None, op0=ALU.mult)
                    return e.tensor_tensor(out=sst[0:16, 0:TILE], in0=ltile[:, :],
                                           in1=lhi[:, :], op=ALU.subtract)
                p.sec("dve", cum_l3, rd=["ltile", "lhi"], wr=["nlhi_r", "lntmp"])

                def cum_l4(e):
                    return e.tensor_copy(out=llo[:, :], in_=sst[0:16, 0:TILE])
                p.sec("dve", cum_l4, rd=["lntmp"], wr=["llo_r"])

                def cum_l5(e):
                    return e.tensor_scalar(out=nllo[:, :], in0=llo[:, :], scalar1=-1.0,
                                           scalar2=None, op0=ALU.mult)
                p.sec("dve", cum_l5, rd=["llo_r"], wr=["lhl"],
                      extra=[p.res["nlhi_r"]["w"]])
                if "ltile" in dbg and T == 0:
                    p.sec("dma", lambda e: e.dma_start(out=dbg["ltile"][:, :], in_=ltile[:, :]),
                          rd=["ltile"], n_dma=1)

                def st_lhl(e):
                    a_ = e.dma_start(out=hlhl[0:16, :], in_=lhl[0:16, :])
                    b_ = e.dma_start(out=hlhl[32:48, :], in_=lhl[32:48, :])
                    return [a_, b_]
                p.sec("dma", st_lhl, rd=["lhl", "llo_r", "nlhi_r", "lhi"], wr=["hlhl"], n_dma=2)


                for nm, t_, rr in (("qfn", qfn, qfn_r), ("kfn", kfn, kfn_r),
                                   ("vf", vf, vf_r)):
                    if nm in dbg and T == 0:
                        p.sec("dma", (lambda t_, nm: lambda e: e.dma_start(out=dbg[nm][:, :], in_=t_[:, :]))(t_, nm),
                              rd=rr, n_dma=1)

                if not do_attn:
                    continue
                for cc in range(NCH):
                    p.stage_limit = chunk_upto
                    first_chunk = (T == 0 and cc == 0)
                    qr = qfn_r + kfn_r + khf_r + vf_r + ["ltile", "negl", "lflat"]

                    def mk_a(e, cc=cc):
                        if cc > 0:
                            return e.activation(out=atile[:, 0:C],
                                                in_=ltile[:, cc * C:(cc + 1) * C],
                                                func=AF.Exp, bias=negl[:, cc * C - 1:cc * C])
                        return e.activation(out=atile[:, 0:C], in_=ltile[:, 0:C], func=AF.Exp)
                    p.sec("act", mk_a, rd=["ltile", "negl"], wr=["achk_a"])

                    def mk_a2(e):
                        return e.tensor_copy(out=abti[:, :], in_=atile[:, 0:C])
                    p.sec("dve", mk_a2, rd=["achk_a"], wr=["achk"])

                    p.stage_cur = 1
                    if cc % 2 == 0:
                        def ld_lfl(e, cc=cc):
                            h4 = hlhl.rearrange("(o h) (c t) -> o h c t", o=3, c=NCH)
                            a_ = e.dma_start(
                                out=lfl2[0:1, :].rearrange("o (c h t) -> o c h t", c=2, h=16),
                                in_=h4[0:1, 0:16, cc:cc + 2, :].rearrange("o h c t -> o c h t"),
                                allow_slow_non_contiguous=True)
                            b_ = e.dma_start(
                                out=lfl2[1:2, :].rearrange("o (c h t) -> o c h t", c=2, h=16),
                                in_=h4[2:3, 0:16, cc:cc + 2, :].rearrange("o h c t -> o c h t"),
                                allow_slow_non_contiguous=True)
                            return [a_, b_]
                        p.sec("gdma", ld_lfl, rd=["hlhl"], wr=["lflat"], n_dma=2)
                    def mm_bhrep(e, cc=cc):
                        last = None
                        for fb in range(8):
                            last = e.matmul(pb[:, fb * 128:(fb + 1) * 128],
                                            sc["INDT"][:, fb * 128:(fb + 1) * 128],
                                            btile[:, cc * C:(cc + 1) * C],
                                            start=True, stop=True)
                        return last
                    p.sec("pe", mm_bhrep, rd=["btile", "consts"] + kfn_r, wr=["pb"])

                    def mk_khfc(e, cc=cc):
                        return e.tensor_tensor(
                            out=khfc[:, :],
                            in0=kfn[:, :].rearrange("p (fb t) -> p fb t", fb=8)[:, :, cc * C:(cc + 1) * C],
                            in1=pb[:, :].rearrange("p (fb t) -> p fb t", fb=8),
                            op=ALU.mult)
                    p.sec("dve", mk_khfc, rd=["pb"] + kfn_r, wr=["khfc"])

                    def mk_khfceo(e):
                        e.tensor_scalar(out=khfcE[:, :], in0=khfc[:, :],
                                        scalar1=sc["CVALS"][:, 2:3], scalar2=None,
                                        op0=ALU.mult)
                        return e.tensor_scalar(out=khfcO[:, :], in0=khfc[:, :],
                                               scalar1=sc["CVALS"][:, 3:4], scalar2=None,
                                               op0=ALU.mult)
                    p.sec("dve", mk_khfceo, rd=["khfc", "consts"], wr=["khfc2"])

                    def khsl(h, w=C):
                        rb = head_rb(h)
                        return khfc[rb:rb + 64, (h // 2) * 128:(h // 2) * 128 + w]

                    p.stage_cur = 2
                    def mm_ks0(e, cc=cc):
                        last = None
                        for h in range(H):
                            fb = h // 2
                            ssx = ssbE if h % 2 == 0 else ssbO
                            last = e.matmul(pa[:, h * HD:(h + 1) * HD],
                                            kfn[:, fb * TILE + cc * C: fb * TILE + cc * C + C],
                                            ssx[:, h * HD:(h + 1) * HD],
                                            start=True, stop=True)
                        return last
                    p.sec("pe", mm_ks0, rd=kfn_r + ["ssb"], wr=["pa"])

                    def ev_ks0(e):
                        return e.tensor_copy(out=kqs0[:, :], in_=pa[:, :])
                    p.sec("dve", ev_ks0, rd=["pa"], wr=["kqs0"])

                    def mm_qs0(e, cc=cc):
                        last = None
                        for h in range(H):
                            fb = h // 2
                            ssx = ssbE if h % 2 == 0 else ssbO
                            last = e.matmul(pb[:, h * HD:(h + 1) * HD],
                                            qfn[:, fb * TILE + cc * C: fb * TILE + cc * C + C],
                                            ssx[:, h * HD:(h + 1) * HD],
                                            start=True, stop=True)
                        return last
                    p.sec("pe", mm_qs0, rd=qfn_r + ["ssb"], wr=["pb"])

                    p.stage_cur = 3
                    def mm_arep2(e):
                        e.matmul(pa[:, 0:512], abti[:, :], sc["IND64"][:, 0:512],
                                 start=True, stop=True)
                        return e.matmul(pa[:, 512:1024], abti[:, :], sc["IND64"][:, 512:1024],
                                        start=True, stop=True)
                    p.sec("pe", mm_arep2, rd=["achk", "consts", "kqs0"], wr=["pa"])

                    p.stage_cur = 4
                    def tp_vk(e, cc=cc):
                        last = None
                        for fb in range(8):
                            e.transpose(pt8[:, fb * 128:(fb + 1) * 128],
                                        vf[:, fb * TILE + cc * C: fb * TILE + cc * C + C],
                                        sc["IDENT128"][:, :])
                            last = e.transpose(pt8[:, 1024 + fb * 128: 1024 + (fb + 1) * 128],
                                               khfc[:, fb * 128:(fb + 1) * 128],
                                               sc["IDENT128"][:, :])
                        return last
                    p.sec("pe", tp_vk, rd=vf_r + khf_r + ["consts"], wr=["pt8"])

                    def ev_vk(e):
                        e.tensor_copy(out=vtm[:, :], in_=pt8[:, 0:1024])
                        return e.tensor_copy(out=khtm[:, :], in_=pt8[:, 1024:2048])
                    p.sec("dve", ev_vk, rd=["pt8"], wr=["vtm", "khtm"])

                    p.stage_cur = 5
                    def mk_rhs1(e):
                        return e.tensor_tensor(out=sst[:, :], in0=pa[:, :], in1=kqs0[:, :],
                                               op=ALU.mult)
                    p.sec("dve", mk_rhs1, rd=["pa", "kqs0"], wr=["lntmp"])

                    def mk_rhs2(e):
                        return e.tensor_tensor(out=rhs_[:, :], in0=vtm[:, :], in1=sst[:, :],
                                               op=ALU.subtract)
                    p.sec("dve", mk_rhs2, rd=["vtm", "lntmp"], wr=["rhs"])

                    def mk_oa1(e):
                        return e.tensor_copy(out=kqs0[:, :], in_=pb[:, :])
                    p.sec("dve", mk_oa1, rd=["pb"], wr=["kqs0"])

                    def mk_oa2(e):
                        return e.tensor_tensor(out=outst[:, :], in0=pa[:, :], in1=kqs0[:, :],
                                               op=ALU.mult)
                    p.sec("dve", mk_oa2, rd=["pa", "kqs0"], wr=["oa", "outst"])

                    p.stage_cur = 6
                    def mm_ld(e, cc=cc):
                        last = None
                        for gh, ps in ((0, pb), (1, pa)):
                            for q in range(2):
                                off = (cc % 2) * 2048 + gh * 1024 + q * 512
                                tgt = ps[:, q * 512:(q + 1) * 512]
                                e.matmul(tgt, sc["ONES2"][:, :],
                                         lfl2[0:2, off:off + 512], start=True, stop=False)
                                e.matmul(tgt, nlhi[:, cc * C:(cc + 1) * C],
                                         sc["INDBD"][:, gh * 1024 + q * 512: gh * 1024 + (q + 1) * 512],
                                         start=False, stop=False)
                                last = e.matmul(tgt, nllo[:, cc * C:(cc + 1) * C],
                                                sc["INDBD"][:, gh * 1024 + q * 512: gh * 1024 + (q + 1) * 512],
                                                start=False, stop=True)
                        return last
                    p.sec("pe", mm_ld, rd=["lflat", "lhl", "nlhi_r", "consts", "oa"],
                          wr=["pb", "pa"])

                    def ee_a(e):
                        e.tensor_scalar(out=sst[:, :], in0=pb[:, :], scalar1=0.0,
                                        scalar2=None, op0=ALU.min)
                        return e.tensor_scalar(out=xts[:, :], in0=pa[:, :], scalar1=0.0,
                                               scalar2=None, op0=ALU.min)
                    p.sec("dve", ee_a, rd=["pb", "pa"], wr=["lntmp", "xts"])

                    def ee_b(e):
                        e.activation(out=eet[:, 0:1024], in_=sst[:, :], func=AF.Exp)
                        return e.activation(out=eet[:, 1024:2048], in_=xts[:, :],
                                            func=AF.Exp)
                    p.sec("act", ee_b, rd=["lntmp", "xts"], wr=["ee_0", "ee_1"])

                    p.stage_cur = 7
                    def mk_rcc(e):
                        return e.tensor_copy(
                            out=rcc[:, :],
                            in_=eet[:, :].rearrange("p (h t) -> p h t", h=16)[:, :, C - 1:C])
                    p.sec("dve", mk_rcc, rd=["ee_0", "ee_1"], wr=["rcc"])
                    if "ee" in dbg and first_chunk:
                        p.sec("dma", lambda e: e.dma_start(out=dbg["ee"][:, :], in_=eet[:, :]),
                              rd=["ee_0", "ee_1"], n_dma=1)

                    p.stage_cur = 8
                    def mk_eep(e):
                        v3 = eet[:, :].rearrange("p (h t) -> p h t", h=16)
                        return e.tensor_tensor(out=v3, in0=v3,
                                               in1=sc["MASK1"][:, :].rearrange("p (a t) -> p a t", a=1)
                                               .broadcast_to((128, 16, 128)),
                                               op=ALU.mult)
                    p.sec("dve", mk_eep, rd=["rcc", "consts"],
                          wr=["eep", "ee_0", "ee_1"])

                    p.stage_cur = 9
                    def mm_kqx(e, cc=cc, src=kfn):
                        last = None
                        for gh, ps in ((0, pb), (1, pa)):
                            for hh in range(8):
                                h = gh * 8 + hh
                                fb = h // 2
                                khx = khfcE if h % 2 == 0 else khfcO
                                last = e.matmul(ps[:, hh * 128:(hh + 1) * 128],
                                                khx[:, fb * 128:(fb + 1) * 128],
                                                src[:, fb * TILE + cc * C: fb * TILE + cc * C + C],
                                                start=True, stop=True)
                        return last
                    p.sec("pe", lambda e, cc=cc: mm_kqx(e, cc, kfn),
                          rd=["khfc2"] + kfn_r + ["rhs", "lntmp", "xts"],
                          wr=["pb", "pa"])

                    def mk_pt(e):
                        e.tensor_tensor(out=ptraw[:, 0:1024], in0=eet[:, 0:1024],
                                        in1=pb[:, :], op=ALU.mult)
                        return e.tensor_tensor(out=ptraw[:, 1024:2048], in0=eet[:, 1024:2048],
                                               in1=pa[:, :], op=ALU.mult)
                    p.sec("dve", mk_pt, rd=["pb", "pa", "eep"], wr=["ptraw_0", "ptraw_1"])

                    p.sec("pe", lambda e, cc=cc: mm_kqx(e, cc, qfn),
                          rd=["khfc2"] + qfn_r + ["ptraw_0", "ptraw_1"], wr=["pb", "pa"])

                    for gh in range(2):
                        def mk_eeg(e, gh=gh):
                            return e.tensor_tensor(out=eegh[:, :].rearrange("p (h t) -> p h t", h=8),
                                                   in0=eet[:, gh * 1024:(gh + 1) * 1024].rearrange("p (h t) -> p h t", h=8),
                                                   in1=sc["IDT1"][:, :].rearrange("p (a t) -> p a t", a=1)
                                                   .broadcast_to((128, 8, 128)),
                                                   op=ALU.add)
                        p.sec("dve", mk_eeg, rd=["eep", "consts"], wr=["eegh"])

                        def mk_gt(e, gh=gh):
                            return e.tensor_tensor(out=gtt[:, gh * 1024:(gh + 1) * 1024],
                                                   in0=eegh[:, :],
                                                   in1=(pb if gh == 0 else pa)[:, :],
                                                   op=ALU.mult)
                        p.sec("dve", mk_gt, rd=["pb", "pa", "eegh"], wr=["gt_%d" % gh])

                    if "ptraw" in dbg and first_chunk:
                        p.sec("dma", lambda e: e.dma_start(out=dbg["ptraw"][:, :], in_=ptraw[:, :]),
                              rd=["ptraw_0", "ptraw_1"], n_dma=1)
                    if "gt" in dbg and first_chunk:
                        p.sec("dma", lambda e: e.dma_start(out=dbg["gt"][:, :], in_=gtt[:, :]),
                              rd=["gt_0", "gt_1"], n_dma=1)

                    p.stage_cur = 10
                    def mk_dd(e):
                        last = None
                        for i in range(4):
                            last = e.tensor_copy(
                                out=dd[32 * i:32 * i + 32, :].rearrange("p (h t) -> p h t", h=16),
                                in_=ptraw[32 * i:32 * i + 32, :].rearrange("p (h t) -> p h t", h=16)[:, :, 32 * i:32 * i + 32])
                        return last
                    p.sec("dve", mk_dd, rd=["ptraw_0", "ptraw_1"], wr=["dd"])

                    p.stage_cur = 11
                    def mk_ptod(e):
                        v3 = ptod[:, :].rearrange("p (h t) -> p h t", h=16)
                        return e.tensor_tensor(out=v3, in0=v3,
                                               in1=sc["NEGBLK1"][:, :].rearrange("p (a t) -> p a t", a=1)
                                               .broadcast_to((128, 16, 128)),
                                               op=ALU.mult)
                    p.sec("gps", mk_ptod, rd=["ptraw_0", "ptraw_1", "dd", "consts"], wr=["ptod"])

                    p.stage_cur = 12
                    def vtp(e, dst, src):
                        return e.transpose(out=dst[:, :], in_=src[:, :])

                    p.sec("dve", lambda e: vtp(e, ddt, dd), rd=["dd"], wr=["ddt"])

                    sqr_seq = [(dd, ddt, dp[1], dtp[2], "d2"),
                               (dp[1], dtp[2], dp[2], dtp[4], "d4"),
                               (dp[2], dtp[4], dp[1], dtp[8], "d8")]
                    iddb = (sc["IDD1"][:, :].rearrange("p (a t) -> p a t", a=1)
                            .broadcast_to((128, 16, 32)))

                    def subIDD(e, dst, src):
                        return e.tensor_tensor(out=dst[:, :].rearrange("p (h t) -> p h t", h=16),
                                               in0=iddb,
                                               in1=src[:, :].rearrange("p (h t) -> p h t", h=16),
                                               op=ALU.subtract)

                    def mm_sq(e, srcT, src, out):
                        last = None
                        for i in range(4):
                            for h in range(H):
                                last = e.matmul(out[32 * i:32 * i + 32, h * 32:(h + 1) * 32],
                                                srcT[32 * i:32 * i + 32, h * 32:(h + 1) * 32],
                                                src[32 * i:32 * i + 32, h * 32:(h + 1) * 32],
                                                start=True, stop=True,
                                                tile_position=(32 * i, 32 * i))
                        return last
                    mm_prod = mm_sq
                    tAt = eegh[:, 512:1024]
                    p.sec("pe", lambda e: mm_sq(e, ddt, dd, pc), rd=["dd", "ddt"], wr=["pc"])
                    p.sec("dve", lambda e: e.tensor_copy(out=dp[1][:, :], in_=pc[:, :]),
                          rd=["pc"], wr=["d2"])
                    p.sec("dve", lambda e: vtp(e, dtp[2], dp[1]), rd=["d2"], wr=["d2t"])
                    p.sec("gps", lambda e: subIDD(e, ggl, ddt), rd=["ddt", "consts"], wr=["ggl"])
                    p.sec("gps", lambda e: subIDD(e, ggr, dp[1]), rd=["d2", "consts"], wr=["ggr"])
                    p.sec("pe", lambda e: mm_prod(e, ggl, ggr, pb[:, 0:512]),
                          rd=["ggl", "ggr"], wr=["pbA", "pb"])
                    p.sec("dve", lambda e: e.tensor_copy(out=dp[2][:, :], in_=pb[:, 0:512]),
                          rd=["pbA", "pb"], wr=["aev"])
                    p.sec("dve", lambda e: vtp(e, tAt, dp[2]), rd=["aev"], wr=["tAt", "eegh"])
                    p.sec("pe", lambda e: mm_sq(e, dtp[2], dp[1], pc), rd=["d2", "d2t"], wr=["pc"])
                    p.sec("dve", lambda e: e.tensor_copy(out=dp[2][:, :], in_=pc[:, :]),
                          rd=["pc", "aev", "tAt"], wr=["d4"])
                    p.sec("dve", lambda e: vtp(e, dtp[4], dp[2]), rd=["d4"], wr=["d4t"])
                    p.sec("pe", lambda e: mm_sq(e, dtp[4], dp[2], pc), rd=["d4", "d4t"], wr=["pc"])
                    p.sec("dve", lambda e: e.tensor_copy(out=dp[1][:, :], in_=pc[:, :]),
                          rd=["pc"], wr=["d8"])
                    p.sec("dve", lambda e: vtp(e, dtp[8], dp[1]), rd=["d8"], wr=["d8t"])
                    p.sec("gps", lambda e: subIDD(e, ggl, dtp[4]), rd=["d4t", "consts"], wr=["ggl"])
                    p.sec("gps", lambda e: subIDD(e, ggr, dp[1]), rd=["d8", "consts"], wr=["ggr"])
                    p.sec("pe", lambda e: mm_prod(e, ggl, ggr, pb[:, 512:1024]),
                          rd=["ggl", "ggr"], wr=["pbB", "pb"])
                    p.sec("dve", lambda e: e.tensor_copy(out=ggl[:, :], in_=pb[:, 512:1024]),
                          rd=["pbB", "pb"], wr=["tB", "ggl"])
                    p.sec("pe", lambda e: mm_sq(e, dtp[8], dp[1], pc), rd=["d8", "d8t"], wr=["pc"])
                    p.sec("dve", lambda e: e.tensor_copy(out=dp[16][:, :], in_=pc[:, :]),
                          rd=["pc"], wr=["d16"])
                    p.sec("pe", lambda e: mm_prod(e, tAt, ggl, pb[:, 0:512]),
                          rd=["tAt", "tB", "pbA", "pb", "eegh"], wr=["pbAB", "pbA", "pb"])
                    p.sec("dve", lambda e: e.tensor_copy(out=dp[1][:, :], in_=pb[:, 0:512]),
                          rd=["pbAB", "pb", "d8", "d8t"], wr=["ab"])
                    p.sec("dve", lambda e: vtp(e, dtp[4], dp[1]), rd=["ab", "tB"], wr=["abt"])
                    p.sec("gps", lambda e: subIDD(e, ggr, dp[16]), rd=["d16", "consts"], wr=["ggr"])
                    p.sec("pe", lambda e: mm_prod(e, dtp[4], ggr, pc),
                          rd=["abt", "ggr", "d16", "pc"], wr=["pc"])
                    p.sec("dve", lambda e: e.tensor_copy(out=tfin[:, :], in_=pc[:, :]),
                          rd=["pc"], wr=["tfin"])
                    if "tfin" in dbg and first_chunk:
                        p.sec("dma", lambda e: e.dma_start(out=dbg["tfin"][:, :], in_=tfin[:, :]),
                              rd=["tfin"], n_dma=1)

                    p.stage_cur = 13
                    for j in range(4):
                        def mm_od(e, j=j):
                            last = None
                            for h in range(H):
                                last = e.matmul(pa[32 * j:32 * j + 32, h * HD:(h + 1) * HD],
                                                ptod[:, h * 128 + 32 * j: h * 128 + 32 * j + 32],
                                                uu[:, h * HD:(h + 1) * HD],
                                                start=True, stop=True,
                                                tile_position=(0, 32 * j))
                            return last
                        p.sec("pe", mm_od, rd=["ptod", "uu"], wr=["pa"])

                        def mk_pre(e, j=j):
                            return e.scalar_tensor_tensor(
                                out=preb[32 * j:32 * j + 32, :],
                                in0=pa[32 * j:32 * j + 32, :], scalar=1.0,
                                in1=rhs_[32 * j:32 * j + 32, :],
                                op0=ALU.mult, op1=ALU.add)
                        p.sec("dve", mk_pre, rd=["pa", "rhs"], wr=["preb"])

                        def mm_t32(e, j=j):
                            last = None
                            for h in range(H):
                                last = e.matmul(pb[32 * j:32 * j + 32, h * HD:(h + 1) * HD],
                                                tfin[32 * j:32 * j + 32, h * 32:(h + 1) * 32],
                                                preb[32 * j:32 * j + 32, h * HD:(h + 1) * HD],
                                                start=True, stop=True,
                                                tile_position=(32 * j, 32 * j))
                            return last
                        p.sec("pe", mm_t32, rd=["tfin", "preb"], wr=["pb"])

                        def ev_u(e, j=j):
                            return e.tensor_copy(out=uu[32 * j:32 * j + 32, :],
                                                 in_=pb[32 * j:32 * j + 32, :])
                        p.sec("dve", ev_u, rd=["pb"], wr=["uu"])
                    if "uu" in dbg and first_chunk:
                        p.sec("dma", lambda e: e.dma_start(out=dbg["uu"][:, :], in_=uu[:, :]),
                              rd=["uu"], n_dma=1)

                    p.stage_cur = 14
                    def mm_o(e):
                        last = None
                        for h in range(H):
                            last = e.matmul(pa[:, h * HD:(h + 1) * HD],
                                            gtt[:, h * 128:(h + 1) * 128],
                                            uu[:, h * HD:(h + 1) * HD],
                                            start=True, stop=True)
                        return last
                    p.sec("pe", mm_o, rd=["gt_0", "gt_1", "uu"], wr=["pa"])

                    def mk_of(e, cc=cc):
                        return e.tensor_tensor(out=outst[:, :], in0=outst[:, :], in1=pa[:, :],
                                               op=ALU.add)
                    p.sec("dve", mk_of, rd=["pa", "oa", "outst"], wr=["outst"])

                    p.stage_cur = 17 if chunk_upto >= 16 else 99
                    def st_outs(e, T=T, cc=cc):
                        return e.dma_start(out=houts[T * TILE + cc * C: T * TILE + (cc + 1) * C, :],
                                           in_=outst[:, :])
                    p.sec("dma", st_outs, rd=["outst"], wr=["houts_%d_%d" % (T, cc)], n_dma=1)
                    if "outs0" in dbg and first_chunk:
                        p.sec("dma", lambda e: e.dma_start(out=dbg["outs0"][:, :], in_=outst[:, :]),
                              rd=["outst"], n_dma=1)

                    p.stage_cur = 15
                    def sq_o(e):
                        return e.tensor_mul(eegh[:, :], outst[:, :], outst[:, :])
                    p.sec("gps", sq_o, rd=["outst"], wr=["eegh"])

                    def mm_st(e, first=not stats_on[0]):
                        e.matmul(pstat[0:1, 0:512], sc["ONESC"][:, :], outst[:, 0:512],
                                 start=first, stop=True, skip_group_check=True)
                        e.matmul(pstat[32:33, 0:512], sc["ONESC"][:, :], outst[:, 512:1024],
                                 start=first, stop=True, skip_group_check=True)
                        e.matmul(pstat[64:65, 0:512], sc["ONESC"][:, :], eegh[:, 0:512],
                                 start=first, stop=True, skip_group_check=True)
                        return e.matmul(pstat[96:97, 0:512], sc["ONESC"][:, :], eegh[:, 512:1024],
                                        start=first, stop=True, skip_group_check=True,
                                        tile_position=(0, 96))
                    p.sec("pe", mm_st, rd=["outst", "eegh", "consts"], wr=["pstat"])
                    stats_on[0] = True

                    p.stage_cur = 16
                    def mk_up(e):
                        return e.tensor_tensor(
                            out=upb[:, :].rearrange("p (h d) -> p h d", h=16),
                            in0=uu[:, :].rearrange("p (h d) -> p h d", h=16),
                            in1=rcc[:, :].rearrange("p (h a) -> p h a", a=1)
                                .broadcast_to((128, 16, 64)),
                            op=ALU.mult)
                    p.sec("dve", mk_up, rd=["uu", "rcc"], wr=["upb"])

                    def mk_acdg(e):
                        return e.tensor_tensor(
                            out=acdg[:, :], in0=sc["ID16"][:, :],
                            in1=abti[:, C - 1:C].broadcast_to((16, 16)),
                            op=ALU.mult)
                    p.sec("dve", mk_acdg, rd=["achk", "consts"], wr=["acdg"])

                    def mm_acv(e):
                        return e.matmul(pc[:, 0:16], sc["ONES16"][:, :], acdg[:, :],
                                        start=True, stop=True)
                    p.sec("pe", mm_acv, rd=["acdg", "consts", "tfin"], wr=["pc"])

                    def mm_dss(e):
                        last = None
                        for h in range(H):
                            for rb in (0, 64):
                                last = e.matmul(pa[rb:rb + 64, h * HD:(h + 1) * HD],
                                                khtm[:, h * HD:(h + 1) * HD],
                                                upb[:, h * HD:(h + 1) * HD],
                                                start=True, stop=True)
                        return last
                    p.sec("pe", mm_dss, rd=["khtm", "upb", "outst"], wr=["pa"])

                    def upd_ss1(e):
                        return e.tensor_tensor(
                            out=sst[:, :].rearrange("p (h d) -> p h d", h=16),
                            in0=ssf[:, :].rearrange("p (h d) -> p h d", h=16),
                            in1=pc[:, 0:16].rearrange("p (h a) -> p h a", a=1)
                            .broadcast_to((128, 16, 64)), op=ALU.mult)
                    p.sec("dve", upd_ss1, rd=["pc", "ssf"], wr=["lntmp"])

                    def upd_ss2(e):
                        return e.tensor_tensor(out=ssf[:, :], in0=sst[:, :], in1=pa[:, :],
                                               op=ALU.add)
                    p.sec("dve", upd_ss2, rd=["pa", "lntmp"], wr=["ssf"])

                    def cast_ss(e):
                        e.tensor_copy(out=ssbE[0:64, :], in_=ssf[0:64, :])
                        return e.tensor_copy(out=ssbO[64:128, :], in_=ssf[64:128, :])
                    p.sec("dve", cast_ss, rd=["ssf"], wr=["ssb"])
                    if "ssf" in dbg and T == 0 and cc == NCH - 1:
                        p.sec("dma", lambda e: e.dma_start(out=dbg["ssf"][:, :], in_=ssf[:, :]),
                              rd=["ssf"], n_dma=1)

            def amax_loc(e):
                return e.reduce_max(out=amaxc[:, 36:37], in_=amaxc[:, 0:32], axis=AX.X)
            p.sec("dve", amax_loc, rd=["amaxc"], wr=["amaxv"])

            def st_am(e):
                a = e.dma_start(out=hrow[0:1, :].rearrange("o (p a) -> (o p) a", a=1), in_=amaxc[:, 36:37],
                                allow_slow_non_contiguous=True)
                b = e.dma_start(out=hrow[1:2, :].rearrange("o (p a) -> (o p) a", a=1), in_=amaxc[:, 35:36],
                                allow_slow_non_contiguous=True)
                return [a, b]
            p.sec("dma", st_am, rd=["amaxv", "amaxc"], wr=["hrow"], n_dma=2)

            def ld_am(e):
                return e.dma_start(out=srow[0:2, :], in_=hrow[0:2, :])
            p.sec("dma", ld_am, rd=["hrow"], wr=["srow"], n_dma=1)

            def red_am(e):
                return e.reduce_max(out=sval[0:2, 0:1], in_=srow[0:2, :], axis=AX.X)
            p.sec("dve", red_am, rd=["srow"], wr=["sval"])

            def st_cc(e):
                return e.dma_start(out=hccin[:, :], in_=sval[0:1, 0:1])
            p.sec("dma", st_cc, rd=["sval"], wr=["hccin"], n_dma=1)

            if n_cores == 8:
                def do_cc(e):
                    return e.collective_compute(
                        "AllReduce", ALU.max, replica_groups=[[0, 1, 2, 3, 4, 5, 6, 7]],
                        ins=[hccin[:, :]], outs=[hccout[:, :]])
                p.sec("gps", do_cc, rd=["hccin"], wr=["hccout"])
            else:
                def do_cc(e):
                    return e.dma_start(out=hccout[:, :], in_=hccin[:, :])
                p.sec("dma", do_cc, rd=["hccin"], wr=["hccout"], n_dma=1)

            def ld_cc(e):
                return e.dma_start(out=sval[0:1, 1:2], in_=hccout[:, :])
            p.sec("dma", ld_cc, rd=["hccout"], wr=["sval_g"], n_dma=1)

            def st_wam(e):
                return e.dma_start(out=hwam[:, :], in_=sval[1:2, 0:1])
            p.sec("dma", st_wam, rd=["sval"], wr=["hwam"], n_dma=1)

            def ld_wam(e):
                return e.dma_start(out=sval[0:1, 5:6], in_=hwam[:, :])
            p.sec("dma", ld_wam, rd=["hwam"], wr=["sval_w"], n_dma=1)

            scale_steps = [
                lambda e: e.tensor_scalar(out=sval[0:1, 2:3], in0=sval[0:1, 1:2],
                                          scalar1=1e-12, scalar2=None, op0=ALU.max),
                lambda e: e.reciprocal(out=sval[0:1, 3:4], in_=sval[0:1, 2:3]),
                lambda e: e.tensor_scalar(out=sval[0:1, 4:5], in0=sval[0:1, 3:4],
                                          scalar1=224.0, scalar2=None, op0=ALU.mult),
                lambda e: e.tensor_scalar(out=sval[0:1, 9:10], in0=sval[0:1, 5:6],
                                          scalar1=1e-12, scalar2=None, op0=ALU.max),
                lambda e: e.reciprocal(out=sval[0:1, 10:11], in_=sval[0:1, 9:10]),
                lambda e: e.tensor_scalar(out=sval[0:1, 6:7], in0=sval[0:1, 10:11],
                                          scalar1=224.0, scalar2=None, op0=ALU.mult),
                lambda e: e.tensor_scalar(out=sval[0:1, 7:8], in0=sval[0:1, 4:5],
                                          scalar1=sval[0:1, 6:7], scalar2=None, op0=ALU.mult),
                lambda e: e.tensor_scalar(out=sval[0:1, 8:9], in0=sval[0:1, 2:3],
                                          scalar1=sval[0:1, 9:10], scalar2=None, op0=ALU.mult),
                lambda e: e.tensor_scalar(out=sval[0:1, 8:9], in0=sval[0:1, 8:9],
                                          scalar1=1.0 / (224.0 * 224.0), scalar2=None,
                                          op0=ALU.mult),
            ]
            for si_, stp in enumerate(scale_steps):
                p.sec("dve", stp, rd=["sval_g", "sval_w", "sval", "scales"], wr=["scales"])
            if "aval" in dbg:
                p.sec("dma", lambda e: e.dma_start(out=dbg["aval"][:, :], in_=sval[0:1, 4:8]),
                      rd=["scales"], n_dma=1)

            def mm_bcast(e):
                e.matmul(pc[:, 0:1], sc["ONESF"][:, :], sval[0:1, 4:5], start=True, stop=True)
                e.matmul(pc[:, 1:2], sc["ONESF"][:, :], sval[0:1, 8:9], start=True, stop=True)
                return e.matmul(pc[:, 2:3], sc["ONESF"][:, :], sval[0:1, 6:7],
                                start=True, stop=True)
            p.sec("pe", mm_bcast, rd=["scales", "consts"], wr=["pc"])

            def ev_bcast(e):
                return e.tensor_copy(out=sxcol[:, 0:3], in_=pc[:, 0:3])
            p.sec("dve", ev_bcast, rd=["pc"], wr=["sxcol"])

            if do_attn:
                def ev_stats(e):
                    e.tensor_copy(out=sst[0:1, 0:512], in_=pstat[0:1, 0:512])
                    e.tensor_copy(out=sst[32:33, 0:512], in_=pstat[32:33, 0:512])
                    e.tensor_copy(out=sst[64:65, 0:512], in_=pstat[64:65, 0:512])
                    return e.tensor_copy(out=sst[96:97, 0:512], in_=pstat[96:97, 0:512])
                p.sec("dve", ev_stats, rd=["pstat"], wr=["lntmp"])

                def red_stats(e):
                    last = None
                    order = ((0, 0), (32, 8), (64, 16), (96, 24))
                    for r, c0 in order:
                        last = e.tensor_reduce(out=gnc[0:1, c0:c0 + 8],
                                               in_=sst[r:r + 1, 0:512].rearrange("p (h d) -> p h d", h=8),
                                               axis=AX.X, op=ALU.add)
                    return last
                p.sec("dve", red_stats, rd=["lntmp"], wr=["gnc"])

                def tp_stats(e):
                    return e.transpose(out=gnc[0:32, 32:64], in_=gnc[0:32, 0:32])
                p.sec("dve", tp_stats, rd=["gnc"], wr=["gnc"])
                NTOK = float(n_tiles * TILE * HD)

                def mk_mu(e):
                    return e.tensor_scalar(out=gnc[0:32, 33:34], in0=gnc[0:32, 32:33],
                                           scalar1=1.0 / NTOK, scalar2=None, op0=ALU.mult)
                p.sec("dve", mk_mu, rd=["gnc"], wr=["gnc"])

                def st_gn(e):
                    return e.dma_start(out=hrow[2:3, 0:32].rearrange("o (p a) -> (o p) a", a=1),
                                       in_=gnc[0:32, 33:34],
                                       allow_slow_non_contiguous=True)
                p.sec("dma", st_gn, rd=["gnc"], wr=["hrow2"], n_dma=1)

                def ld_gn(e):
                    a = e.dma_start(out=gnr[:, 0:1],
                                    in_=hrow[2:3, 0:16].rearrange("o (f a) -> (o f) a", a=1),
                                    allow_slow_non_contiguous=True)
                    b = e.dma_start(out=gnr[:, 2:3],
                                    in_=hrow[2:3, 16:32].rearrange("o (f a) -> (o f) a", a=1),
                                    allow_slow_non_contiguous=True)
                    return [a, b]
                p.sec("dma", ld_gn, rd=["hrow2"], wr=["gnr"], n_dma=2)

                def mk_gn1a(e):
                    return e.tensor_scalar(out=gnr[:, 4:5], in0=gnr[:, 0:1],
                                           scalar1=gnr[:, 0:1], scalar2=None, op0=ALU.mult)
                p.sec("dve", mk_gn1a, rd=["gnr"], wr=["gnr_a"])

                def mk_gn1b(e):
                    return e.tensor_tensor(out=gnr[:, 4:5], in0=gnr[:, 2:3], in1=gnr[:, 4:5],
                                           op=ALU.subtract)
                p.sec("dve", mk_gn1b, rd=["gnr_a", "gnr"], wr=["gnr"])

                def mk_gn2(e):
                    return e.activation(out=gnr[:, 5:6], in_=gnr[:, 4:5], func=AF.Sqrt,
                                        bias=1e-5)
                p.sec("act", mk_gn2, rd=["gnr"], wr=["gnr"])

                def mk_vb0(e):
                    return e.reciprocal(out=gnr[:, 5:6], in_=gnr[:, 5:6])
                p.sec("dve", mk_vb0, rd=["gnr"], wr=["gnr_c"])

                def mk_vb1(e):
                    e.tensor_tensor(out=vbr[:, :], in0=sc["PATTERN16"][:, :],
                                    in1=gnr[:, 5:6].broadcast_to((16, 8)), op=ALU.mult)
                    return e.tensor_tensor(out=vbm[:, :], in0=sc["PATTERN16"][:, :],
                                           in1=gnr[:, 0:1].broadcast_to((16, 8)), op=ALU.mult)
                p.sec("dve", mk_vb1, rd=["gnr_c", "gnr", "consts"], wr=["vb"])

                def mk_indpf(e):
                    return e.tensor_copy(out=INDPF[:, :], in_=sc["INDP"][:, :])
                p.sec("dve", mk_indpf, rd=["consts"], wr=["indpf"])

                def mm_vb2(e):
                    e.matmul(pc[:, 4:12], INDPF[:, :], vbr[:, :], start=True, stop=True)
                    return e.matmul(pc[:, 12:20], INDPF[:, :], vbm[:, :], start=True, stop=True)
                p.sec("pe", mm_vb2, rd=["vb", "indpf"], wr=["pc"])

                def mk_gnf1(e):
                    return e.tensor_tensor(out=gnmul[:, :], in0=pc[:, 4:12],
                                           in1=sbias["gns"][:, :], op=ALU.mult)
                p.sec("dve", mk_gnf1, rd=["pc", "consts"], wr=["gnf_a"])

                def mk_gnf2(e):
                    return e.tensor_tensor(out=gnadd[:, :], in0=pc[:, 12:20], in1=gnmul[:, :],
                                           op=ALU.mult)
                p.sec("dve", mk_gnf2, rd=["pc", "gnf_a"], wr=["gnf_b"])

                def mk_gnf3(e):
                    return e.tensor_tensor(out=gnadd[:, :], in0=sbias["gnb"][:, :],
                                           in1=gnadd[:, :], op=ALU.subtract)
                p.sec("dve", mk_gnf3, rd=["gnf_b", "consts"], wr=["gnfin"])
                if "gnmul" in dbg:
                    p.sec("dma", lambda e: e.dma_start(out=dbg["gnmul"][:, :], in_=gnmul[:, :]),
                          rd=["gnfin"], n_dma=1)
                if "gnadd" in dbg:
                    p.sec("dma", lambda e: e.dma_start(out=dbg["gnadd"][:, :], in_=gnadd[:, :]),
                          rd=["gnfin"], n_dma=1)

            def mk_brow(e):
                e.tensor_copy(out=sbrow16[0:1, :], in_=sbrow[0:1, :])
                return e.tensor_scalar(out=sbrow16[32:33, :], in0=sbrow[32:33, :],
                                       scalar1=sval[0:1, 7:8], scalar2=None, op0=ALU.mult)
            p.sec("dve", mk_brow, rd=["consts", "scales"], wr=["brow"])

            p.flush()
            if do_phase2:
                esi.close()
                es2 = ExitStack()
                s2 = lambda n, s, dt=F32: es2.enter_context(nc.sbuf_tensor(n, s, dt))
                swg = s2("swg", [128, 8 * DM], BF16)
                swo = s2("swo", [128, 8 * DM], BF16)
                wt8 = s2("wt8", [128, 8 * DM], FP8)
                wtf2 = s2("wtf2", [128, DM])
                xnf2 = s2("xnf2", [128, 8 * TILE])
                xnfb2 = s2("xnfb2", [128, 8 * TILE], BF16)
                xq8 = s2("xq8", [128, 8 * TILE], FP8)
                ofm = s2("ofm", [128, 8 * TILE], BF16)
                gf = s2("gf", [128, 8 * TILE], BF16)
                yf = s2("yf", [128, 8 * TILE], BF16)
                gyf = s2("gyf", [128, 8 * TILE], BF16)
                xt2 = s2("xt2", [128, DM])
                outf = s2("outf", [128, DM])

                with es2:
                    def ld_w2(e):
                        ins = []
                        for w, dst in (("wg", swg), ("wo", swo)):
                            for cc2 in range(8):
                                ins.append(e.dma_start(out=dst[:, cc2 * DM:(cc2 + 1) * DM],
                                                       in_=hw[w][cc2, :, :]))
                        return ins
                    p.sec("gdma", ld_w2, wr=["w2"], n_dma=16)

                    for cc2 in range(8):
                        def ld_wt(e, cc2=cc2):
                            return e.dma_start(out=wtf2[:, :], in_=hwt[cc2, :, :])
                        p.sec("dma", ld_wt, rd=["hwt_%d" % cc2], wr=["wtf2"], n_dma=1)

                        def q_wt(e, cc2=cc2):
                            return e.tensor_scalar(out=wt8[:, cc2 * DM:(cc2 + 1) * DM],
                                                   in0=wtf2[:, :], scalar1=sxcol[:, 2:3],
                                                   scalar2=None, op0=ALU.mult)
                        p.sec("dve", q_wt, rd=["wtf2", "sxcol"], wr=["wt8_%d" % cc2])
                    wt8_r = ["wt8_%d" % i for i in range(8)]

                    for T in range(n_tiles):
                        def ld_p2(e, T=T):
                            ins = [e.dma_start(out=xnf2[:, :], in_=hxnf[T, :, :])]
                            for fb in range(8):
                                ins.append(e.dma_start(
                                    out=ofm[:, fb * 512:(fb + 1) * 512],
                                    in_=houts[T * TILE:(T + 1) * TILE, fb * 128:(fb + 1) * 128],
                                    transpose=True))
                            return ins
                        deps_outs = ["houts_%d_%d" % (T, cc3) for cc3 in range(NCH)] if do_attn else []
                        p.sec("dma", ld_p2, rd=["hxnf_%d_0" % T, "hxnf_%d_1" % T,
                                                "hxnf_%d_2" % T, "hxnf_%d_3" % T] + deps_outs,
                              wr=["xnf2", "ofm"], n_dma=9)

                        def cast_x2(e):
                            return e.tensor_copy(out=xnfb2[:, :], in_=xnf2[:, :])
                        p.sec("dve", cast_x2, rd=["xnf2"], wr=["xnfb2"])

                        def q_x(e):
                            return e.activation(out=xq8[:, :], in_=xnf2[:, :], func=AF.Identity,
                                                scale=sxcol[:, 0:1])
                        p.sec("act", q_x, rd=["xnf2", "sxcol"], wr=["xq8"])

                        for fbp in range(4):
                            psg, pkg = (pa, "pa") if fbp % 2 == 0 else (pb, "pb")

                            def mm_g(e, fbp=fbp, psg=psg):
                                last = None
                                for half in range(2):
                                    fb = fbp * 2 + half
                                    for cc2 in range(8):
                                        last = e.matmul(
                                            psg[:, half * 512:(half + 1) * 512],
                                            swg[:, cc2 * DM + fb * 128: cc2 * DM + (fb + 1) * 128],
                                            xnfb2[:, cc2 * 512:(cc2 + 1) * 512],
                                            start=(cc2 == 0), stop=(cc2 == 7))
                                return last
                            p.sec("pe", mm_g, rd=["xnfb2", "w2"], wr=[pkg])

                            def ev_g(e, fbp=fbp, psg=psg):
                                last = None
                                for half in range(2):
                                    fb = fbp * 2 + half
                                    last = e.activation(out=gf[:, fb * 512:(fb + 1) * 512],
                                                        in_=psg[:, half * 512:(half + 1) * 512],
                                                        func=AF.Silu, bias=sbias["bg"][:, fb:fb + 1])
                                return last
                            p.sec("act", ev_g, rd=[pkg, "consts"], wr=["gf_%d" % fbp])

                        def mk_y(e):
                            last = None
                            for fb in range(8):
                                last = e.tensor_scalar(out=yf[:, fb * 512:(fb + 1) * 512],
                                                       in0=ofm[:, fb * 512:(fb + 1) * 512],
                                                       scalar1=gnmul[:, fb:fb + 1],
                                                       scalar2=gnadd[:, fb:fb + 1],
                                                       op0=ALU.mult, op1=ALU.add)
                            return last
                        p.sec("dve", mk_y, rd=["ofm", "gnfin"], wr=["yf"])

                        def mk_gy(e):
                            return e.tensor_mul(gyf[:, :], gf[:, :], yf[:, :])
                        p.sec("dve", mk_gy, rd=["yf"] + ["gf_%d" % i for i in range(4)],
                              wr=["gyf"])

                        for s_ in range(NSUB):
                            def ld_x2(e, T=T, s_=s_):
                                return e.dma_start(
                                    out=xt2[:, :],
                                    in_=hx[T * TILE + s_ * 128: T * TILE + (s_ + 1) * 128, :])
                            p.sec("dma", ld_x2, wr=["xt2"], n_dma=1)

                            atth = ((pa[:, 0:512], pa[:, 512:1024]) if s_ % 2 == 0
                                    else (pc[:, 0:512], pstat[:, 0:512]))
                            attk = "pa" if s_ % 2 == 0 else "pcs"

                            def mm_att(e, s_=s_, atth=atth):
                                last = None
                                for half in range(2):
                                    for cc2 in range(8):
                                        e.matmul(
                                            atth[half],
                                            gyf[:, cc2 * 512 + s_ * 128: cc2 * 512 + (s_ + 1) * 128],
                                            swo[:, cc2 * DM + half * 512: cc2 * DM + (half + 1) * 512],
                                            start=(cc2 == 0), stop=False)
                                    last = e.matmul(atth[half], sc["ONES1"][:, :],
                                                    sbrow16[0:1, half * 512:(half + 1) * 512],
                                                    start=False, stop=True)
                                return last
                            p.sec("pe", mm_att, rd=["gyf", "w2", "consts", "brow"], wr=[attk])

                            def mm_ffn(e, s_=s_):
                                last = None
                                for half in range(2):
                                    for cc2 in range(8):
                                        e.matmul(
                                            pb[:, half * 512:(half + 1) * 512],
                                            xq8[:, cc2 * 512 + s_ * 128: cc2 * 512 + (s_ + 1) * 128],
                                            wt8[:, cc2 * DM + half * 512: cc2 * DM + (half + 1) * 512],
                                            start=(cc2 == 0), stop=False)
                                    last = e.matmul(pb[:, half * 512:(half + 1) * 512],
                                                    sc["ONES33"][32:33, :],
                                                    sbrow16[32:33, half * 512:(half + 1) * 512],
                                                    start=False, stop=True)
                                return last
                            p.sec("pe", mm_ffn, rd=["xq8", "brow", "consts"] + wt8_r, wr=["pb"])

                            def mk_out1(e, s_=s_):
                                return e.scalar_tensor_tensor(
                                    out=outf[:, :], in0=pb[:, :], scalar=sxcol[:, 1:2],
                                    in1=xt2[:, :], op0=ALU.mult, op1=ALU.add)
                            p.sec("dve", mk_out1, rd=["pb", "xt2", "sxcol"], wr=["outf_a"])

                            def mk_out2(e, s_=s_, atth=atth):
                                e.tensor_tensor(out=outf[:, 0:512], in0=outf[:, 0:512],
                                                in1=atth[0], op=ALU.add)
                                return e.tensor_tensor(out=outf[:, 512:1024],
                                                       in0=outf[:, 512:1024],
                                                       in1=atth[1], op=ALU.add)
                            p.sec("dve", mk_out2, rd=[attk, "outf_a"], wr=["outf"])

                            def st_y(e, T=T, s_=s_):
                                return e.dma_start(
                                    out=hy[T * TILE + s_ * 128: T * TILE + (s_ + 1) * 128, :],
                                    in_=outf[:, :])
                            p.sec("dma", st_y, rd=["outf"], wr=["y_%d_%d" % (T, s_)], n_dma=1)

            def fin(e):
                return None
            p.sec("dma", fin, no_inc=True,
                  extra=[p.t("gdma"), p.t("pe"), p.t("dve"), p.t("act"),
                         p.t("gps"), p.t("dma")])
            p.flush()

    return nc


def kernel(**inputs):
    cm = prep_common(inputs)
    x = np.asarray(inputs["L0_attn_x"], np.float32)
    nc = build_program()
    in_maps = []
    for i in range(8):
        m = dict(cm)
        m["x"] = np.ascontiguousarray(x[i])
        in_maps.append(m)
    res = run_bass_kernel_spmd(nc, in_maps, core_ids=list(range(8)))
    return np.stack([r["y"] for r in res.results], axis=0)
